# version 23
# baseline (speedup 1.0000x reference)
import numpy as np
from contextlib import ExitStack

import concourse.bass as bass
import concourse.mybir as mybir
from concourse.bass_utils import run_bass_kernel_spmd

F32 = mybir.dt.float32
BF16 = mybir.dt.bfloat16
I16 = mybir.dt.int16
ALU = mybir.AluOpType
AF = mybir.ActivationFunctionType
AX = mybir.AxisListType

B, S, D = 8, 2048, 1024
H, HD = 16, 64
I3 = 3072
NT = S // 128
NC = D // 128
NI = I3 // 128
MAGIC = 12582912.0
QMAX = 32767.0
EPS = 1e-5

import os
STAGE = int(os.environ.get("MK_STAGE", "4"))
SUBST = float(os.environ.get("MK_SUBST", "99"))


def host_constants():
    c = {}
    inv = 1.0 / (10000.0 ** (np.arange(0, HD, 2, dtype=np.float64) / HD))
    ang = np.outer(np.arange(S, dtype=np.float64), inv)
    cosv, sinv = np.cos(ang), np.sin(ang)
    cc = np.zeros((128, S), dtype=np.float64)
    ss = np.zeros((128, S), dtype=np.float64)
    for p in range(128):
        i = (p % 64) // 2
        cc[p] = cosv[:, i]
        ss[p] = sinv[:, i] * (1.0 if (p % 2) else -1.0)
    c["cc_tab"] = cc.astype(np.float32)
    c["ss_tab"] = ss.astype(np.float32)
    c["ident"] = np.eye(128, dtype=np.float32)
    psw = np.zeros((128, 128), dtype=np.float32)
    for m in range(128):
        psw[m ^ 1, m] = 1.0
    c["pswap"] = psw
    mn = np.zeros((128, 4, 512), dtype=np.float32)
    for d in range(4):
        for p in range(128):
            mn[p, d, : min(512, p + 128 * d)] = -1.0e9
    c["mneg"] = mn.reshape(128, 4 * 512)
    oneh = np.zeros((128, NC, 16), dtype=np.float32)
    for ch in range(NC):
        oneh[0:64, ch, 2 * ch] = 1.0
        oneh[64:128, ch, 2 * ch + 1] = 1.0
    c["oneh"] = oneh.reshape(128, NC * 16)
    mr = np.zeros((64, 544), dtype=np.float32)
    for kk in range(32):
        mr[kk, : 32 * (kk // 2) + (kk % 2) + 30] = -1.0e9
    mr[32, :] = -1.0e9
    c["mskr"] = np.ascontiguousarray(np.vstack([mr] * 2))
    ml = np.zeros((64, 4 * 128), dtype=np.float32)
    for jq in range(4):
        for j in range(128):
            mm = j % 64 - 16 * jq
            if 0 <= mm < 16:
                ml[2 * mm + (j // 64), 128 * jq + j] = 1.0
            elif mm >= 16:
                ml[32, 128 * jq + j] = 1.0
    c["mskl"] = np.ascontiguousarray(np.vstack([ml] * 2))
    o2 = np.zeros((128, 2), dtype=np.float32)
    o2[0:64, 0] = 1.0
    o2[64:128, 1] = 1.0
    c["ones2"] = o2
    return c


ENGINES = ("sync", "tensor", "vector", "scalar", "gpsimd")


class Plan:
    def __init__(self):
        self.items = []

    def op(self, eng, fn):
        self.items.append(("op", eng, fn))

    def t(self, fn): self.op("tensor", fn)
    def v(self, fn): self.op("vector", fn)
    def s(self, fn): self.op("scalar", fn)
    def g(self, fn): self.op("gpsimd", fn)

    def dma(self, q, fn):
        self.op(q, ("DMA", fn))

    def coll(self, fn):
        self.op("gpsimd", ("COLL", fn))

    def bar(self):
        import sys as _s
        f = _s._getframe(1)
        self.items.append(("bar", f"{f.f_lineno}"))

    def nbars(self):
        return sum(1 for it in self.items if it[0] == "bar")

    def ndma(self, q):
        return sum(1 for it in self.items
                   if it[0] == "op" and it[1] == q
                   and isinstance(it[2], tuple) and it[2][0] == "DMA")


def build_nc():
    nc = bass.Bass("TRN2", target_bir_lowering=False)
    st = ExitStack()

    d_x = nc.dram_tensor("x", [S, D], F32, kind="ExternalInput").ap()
    d_y = nc.dram_tensor("y", [S, D], F32, kind="ExternalOutput").ap()
    d_yt = d_y.rearrange("(t p) d -> p t d", p=128)

    din = {}
    for nm, shp in [
        ("l0_wq", [D, D]), ("l0_wk", [D, D]), ("l0_wv", [D, D]),
        ("l0_wo", [D, D]), ("l0_wob", [D]),
        ("l0_wg", [I3, D]), ("l0_wu", [I3, D]), ("l0_wd", [D, I3]),
        ("l1_qkv", [D, 3 * D]),
        ("l1_bw_t", [D, D]), ("l1_bb", [D]),
        ("l1_pack", [D, 16]), ("l1_rb16", [16]), ("l1_B", [64, D]),
        ("cc_tab", [128, S]), ("ss_tab", [128, S]),
        ("ident", [128, 128]), ("pswap", [128, 128]),
        ("mneg", [128, 4 * 512]), ("oneh", [128, NC * 16]),
        ("mskr", [128, 544]), ("mskl", [128, 4 * 128]),
        ("ones2", [128, 2]),
    ]:
        din[nm] = nc.dram_tensor(nm, shp, F32, kind="ExternalInput").ap()

    d_up = nc.dram_tensor("up_sp", [NI, 128, S], BF16).ap()
    d_gate = nc.dram_tensor("gate_sp", [NI, 128, S], BF16).ap()
    d_t = nc.dram_tensor("t_sp", [NI, 128, S], F32).ap()
    d_t2 = nc.dram_tensor("t2_sp", [NI, 128, S], F32).ap()
    d_wdT = nc.dram_tensor("wdT_sp", [NI, 128, D], BF16).ap()
    d_agi = [nc.dram_tensor(f"agi{k}", [1, 2], F32).ap() for k in range(4)]
    d_ago = [nc.dram_tensor(f"ago{k}", [8, 2], F32).ap() for k in range(4)]

    SB_BASE = 0x4400

    def at(name, off, shape, dtype):
        return nc.alloc_sbuf_tensor_at(name, shape, dtype, offset=SB_BASE + off)

    A0, B0, C0, D0, E0 = 0x0000, 0x8000, 0x10000, 0x18000, 0x20000
    F0, G0, H0, H1 = 0x28200, 0x2C200, 0x2E200, 0x30200
    K0 = 0x32200

    XNT = at("XNT", A0, [128, NT, 1024], BF16)
    OB_A = at("OB_A", A0, [128, 8, S], BF16)
    CCT = at("CCT", A0, [128, S], BF16)
    SST = at("SST", A0 + 0x1000, [128, S], BF16)
    SQS = at("SQS", A0 + 0x2000, [128, NC, 512], BF16)
    ROPET = at("ROPET", A0 + 0x4000, [128, 4, 512], BF16)
    ROPET2 = at("ROPET2", A0 + 0x5000, [128, 4, 512], BF16)
    SQS2 = at("SQS2", A0 + 0x6000, [128, NC, 512], BF16)
    HT = at("HT", A0, [128, NT, 1024], BF16)
    KTTA = at("KTTA", A0, [128, 8, 2048], BF16)
    XNF = at("XNF", B0, [128, NC, S], BF16)
    HOUT = at("HOUT", B0, [128, NT, 1024], BF16)
    LNX = at("LNX", C0, [128, NT, 1024], F32)
    QF = at("QF", C0, [128, NC, S], BF16)
    EV0 = at("EV0", C0 + 0x0000, [128, S], F32)
    EV1 = at("EV1", C0 + 0x2000, [128, S], F32)
    EV2 = at("EV2", C0 + 0x4000, [128, S], F32)
    EB0 = at("EB0", C0 + 0x6000, [128, S], BF16)
    EB1 = at("EB1", C0 + 0x7000, [128, S], BF16)
    EV5 = at("EV5", A0 + 0x0000, [128, S], F32)
    EV6 = at("EV6", A0 + 0x2000, [128, S], F32)
    EV7 = at("EV7", A0 + 0x4000, [128, S], F32)
    KF = at("KF", D0, [128, NC, S], BF16)
    KTTD = at("KTTD", D0, [128, 8, 2048], BF16)
    XQF = at("XQF", D0, [128, NC, S], BF16)
    EV3 = at("EV3", D0 + 0x0000, [128, S], F32)
    EV4 = at("EV4", D0 + 0x2000, [128, S], F32)
    GQ = [at(f"GQ{i}", D0 + 0x4000 + i * 0x1000, [128, S], I16) for i in range(3)]
    OB_D = at("OB_D", D0, [128, 8, S], BF16)
    VA = at("VA", E0, [128, NT, H, 65], BF16)
    OB_E = at("OB_E", E0, [128, 8, S], BF16)
    ZF = at("ZF", E0 + 0x4000, [64, NT, 128], BF16)
    WBL = at("WBL", E0 + 0x5000, [64, 1024], BF16)
    WUF0 = at("WUF0", F0, [128, 1024], F32)
    WUF1 = at("WUF1", F0 + 0x1000, [128, 1024], F32)
    WQT = at("WQT", F0 + 0x2000, [128, 1024], F32)
    WUB0 = at("WUB0", F0 + 0x3000, [128, 1024], BF16)
    WUB1 = at("WUB1", F0 + 0x3800, [128, 1024], BF16)
    WOBR = at("WOBR", F0, [1, 1024], F32)
    BBR = at("BBR", F0 + 0x1000, [1, 1024], F32)
    MNEG = at("MNEG", F0 + 0x2000, [128, 4, 512], BF16)
    PTS = at("PTS", F0 + 0x3000, [128, 4, 512], BF16)
    LG = at("LG", F0 + 0x3800, [128, NT, 16], F32)
    GT = at("GT", F0 + 0x3C00, [128, NT, 8], BF16)
    M2T = at("M2T", F0 + 0x3D00, [128, NT, 8], F32)
    SQKQ = at("SQKQ", F0, [128, NC, 512], BF16)
    SQKQ2 = at("SQKQ2", F0 + 0x2000, [128, NC, 512], BF16)
    WLT0 = at("WLT0", G0, [128, 8, 128], BF16)
    WLT1 = at("WLT1", G0 + 0x800, [128, 8, 128], BF16)
    WDR0 = at("WDR0", G0, [128, 1024], BF16)
    WDR1 = at("WDR1", G0 + 0x800, [128, 1024], BF16)
    EVG = at("EVG", G0 + 0x1000, [128, 1024], F32)
    EVG2 = at("EVG2", G0 + 0x0000, [128, 1024], F32)
    WB = at("WB", H0, [128, NC, 512], BF16)
    WB2 = at("WB2", H1, [128, NC, 512], BF16)
    WDR2 = at("WDR2", H0, [128, 1024], BF16)
    WDR3 = at("WDR3", H0 + 0x800, [128, 1024], BF16)
    IDB = at("IDB", K0, [128, 128], BF16)
    IDF = at("IDF", K0 + 0x100, [128, 128], F32)
    PSW = at("PSW", K0 + 0x300, [128, 128], BF16)
    ONESF = at("ONESF", K0 + 0x400, [1, 128], F32)
    ONEHS = at("ONEHS", K0 + 0x600, [128, NC, 16], BF16)
    WPK = at("WPK", K0 + 0x700, [128, NC, 16], BF16)
    RB16 = at("RB16", K0 + 0x800, [1, 16], F32)
    SC = at("SC", K0 + 0x880, [128, 32], F32)
    SR = at("SR", K0 + 0x900, [1, 64], F32)
    SX = at("SX", K0 + 0xA00, [128, 16], F32)
    SX2 = at("SX2", K0 + 0xA40, [128, 16], F32)
    SM1 = at("SM1", K0 + 0xA80, [128, 16], F32)
    SM2 = at("SM2", K0 + 0xAC0, [128, 16], F32)
    SM3 = at("SM3", K0 + 0xB00, [128, 16], F32)
    SM4 = at("SM4", K0 + 0xB40, [128, 16], F32)
    UXM = at("UXM", K0 + 0xB80, [128, 24], F32)
    GXM = at("GXM", K0 + 0xBE0, [128, 24], F32)
    TXM = at("TXM", K0 + 0xC40, [128, 24], F32)
    T2XM = at("T2XM", K0 + 0xCA0, [128, 24], F32)
    AGR = at("AGR", K0 + 0xD00, [1, 16], F32)
    RCP = at("RCP", K0 + 0xD40, [128, 8], F32)
    MSKR = at("MSKR", K0 + 0xD60, [128, 544], BF16)
    MSKL = at("MSKL", K0 + 0x1460, [128, 4 * 128], BF16)
    ONES2 = at("ONES2", K0 + 0x1400, [128, 2], BF16)
    K2M = at("K2M", K0 + 0x1420, [128, 16], F32)
    assert 0x4400 + K0 + 0x18A0 <= 0x38000

    PB = [st.enter_context(nc.psum_tensor(f"PB{i}", [128, 512], F32))
          for i in range(8)]
    PBB = [pb.bitcast(BF16) for pb in PB]

    phase_sems = {e: st.enter_context(nc.semaphore(f"ph_{e}")) for e in ENGINES}
    dma_sems = {q: st.enter_context(nc.semaphore(f"dq_{q}"))
                for q in ("sync", "gpsimd", "scalar")}
    cc_sem = st.enter_context(nc.semaphore("ccs"))
    sem_sc = st.enter_context(nc.semaphore("sc_sc"))
    sem_ex = st.enter_context(nc.semaphore("sc_ex"))
    sem_av = st.enter_context(nc.semaphore("sc_av"))
    sem_ev = st.enter_context(nc.semaphore("sc_ev"))
    sem_dd = st.enter_context(nc.semaphore("sc_dd"))

    plan = Plan()

    def DMA1(q, out, in_):
        def f(e, sem, out=out, in_=in_):
            e.dma_start(out=out, in_=in_).then_inc(sem, 16)
            return 1
        plan.dma(q, f)

    def DMAT(q, out, in_):
        def f(e, sem, out=out, in_=in_):
            e.dma_start_transpose(out, in_).then_inc(sem, 16)
            return 1
        plan.dma(q, f)

    def DMAACC(out, in_, sem2=None, q="gpsimd"):
        def f(e, sem, out=out, in_=in_, sem2=sem2):
            if sem2 is not None:
                e.dma_start(out=out, in_=in_, accum_op=ALU.add).then_inc(sem2, 16)
                return 0
            e.dma_start(out=out, in_=in_, accum_op=ALU.add).then_inc(sem, 16)
            return 1
        plan.dma(q, f)

    def scalar_from_col(col_ap, dst_idx):
        plan.t(lambda e, c=col_ap: e.transpose(PB[6][0:1, 0:128], c, IDF[:, :]))
        plan.bar()

        def vf(e, i=dst_idx):
            e.reduce_max(SR[0:1, i:i + 1], PB[6][0:1, 0:128], axis=AX.X)
        plan.v(vf)
        plan.bar()

    def bcast_scalar(src_idx, dst_col):
        plan.t(lambda e, i=src_idx: e.matmul(
            PB[6][:, 0:1], ONESF[0:1, :], SR[0:1, i:i + 1], start=True, stop=True))
        plan.bar()

        def vf(e, d=dst_col):
            e.tensor_copy(SC[:, d:d + 1], PB[6][:, 0:1])
        plan.v(vf)
        plan.bar()

    def emit_allgather(k):
        DMA1("gpsimd", d_agi[k], SR[0:1, 0:2])
        plan.bar()
        plan.coll(lambda e, k=k: e.collective_compute(
            "AllGather", ALU.bypass, replica_groups=[list(range(8))],
            ins=[d_agi[k]], outs=[d_ago[k]]))
        plan.bar()
        DMA1("gpsimd", AGR[0:1, 0:16], d_ago[k])
        plan.bar()

    AG2 = AGR.ap()[0:1, 0:16].rearrange("a (r v) -> a r v", v=2)

    DMA1("gpsimd", IDB[:, :], din["ident"])
    DMA1("sync", IDF[:, :], din["ident"])
    DMA1("gpsimd", PSW[:, :], din["pswap"])
    DMA1("gpsimd", ONEHS.ap().rearrange("p c m -> p (c m)"), din["oneh"])

    def initc(e):
        e.memset(ONESF[:, :], 1.0)
        e.memset(SR[:, :], 0.0)
    plan.v(initc)
    plan.bar()

    def emit_transpose_tok_to_fm(src_tok, dst_fm):
        for w in range(NT + 1):
            if w < NT:
                pbb = PBB[w % 2]

                def tw(e, w=w, pbb=pbb):
                    for c in range(NC):
                        e.transpose(pbb[:, 128 * c:128 * c + 128],
                                    src_tok[:, w, 128 * c:128 * c + 128],
                                    IDB[:, :])
                plan.t(tw)
            wv = w - 1
            if wv >= 0:
                pbb = PBB[wv % 2]

                def ew(e, wv=wv, pbb=pbb):
                    for c in range(NC):
                        e.copy(dst_fm[:, c, 128 * wv:128 * wv + 128],
                               pbb[:, 128 * c:128 * c + 128])
                plan.s(ew)
            plan.bar()

    def emit_layernorm(src_tiled, copy_y=False):
        for t in range(NT):
            DMA1("sync" if t % 2 == 0 else "gpsimd", LNX[:, t, :], src_tiled[:, t, :])
        plan.bar()
        if copy_y:
            for t in range(NT):
                DMA1("gpsimd" if t % 2 == 0 else "scalar",
                     d_yt[:, t, :], LNX[:, t, :])

        def stats(e):
            for t in range(NT):
                e.reduce_sum(SX[:, t:t + 1], LNX[:, t, :], axis=AX.X)
                e.tensor_mul(EVG[:, :], LNX[:, t, :], LNX[:, t, :])
                e.reduce_sum(SX2[:, t:t + 1], EVG[:, :], axis=AX.X)
        plan.v(stats)
        plan.bar()

        def smalls(e):
            e.tensor_scalar(out=SM1[:, :], in0=SX[:, :], scalar1=1.0 / D,
                            scalar2=None, op0=ALU.mult)
            e.tensor_scalar(out=SM2[:, :], in0=SX2[:, :], scalar1=1.0 / D,
                            scalar2=None, op0=ALU.mult)
            e.tensor_mul(SM3[:, :], SM1[:, :], SM1[:, :])
            e.tensor_sub(SM2[:, :], SM2[:, :], SM3[:, :])
            e.tensor_scalar(out=SM2[:, :], in0=SM2[:, :], scalar1=EPS,
                            scalar2=None, op0=ALU.add)
        plan.v(smalls)
        plan.bar()
        plan.s(lambda e: e.activation(SM3[:, :], SM2[:, :], AF.Sqrt))
        plan.bar()

        def newton(e):
            e.reciprocal(SM4[:, :], SM3[:, :])
            e.tensor_mul(SM3[:, :], SM4[:, :], SM4[:, :])
            e.tensor_mul(SM3[:, :], SM3[:, :], SM2[:, :])
            e.tensor_scalar(out=SM3[:, :], in0=SM3[:, :], scalar1=-0.5,
                            scalar2=1.5, op0=ALU.mult, op1=ALU.add)
            e.tensor_mul(SM4[:, :], SM4[:, :], SM3[:, :])
        plan.v(newton)
        plan.bar()

        def apply(e):
            for t in range(NT):
                e.tensor_scalar(out=XNT[:, t, :], in0=LNX[:, t, :],
                                scalar1=SM1[:, t:t + 1], scalar2=SM4[:, t:t + 1],
                                op0=ALU.subtract, op1=ALU.mult)
        plan.v(apply)
        plan.bar()
        emit_transpose_tok_to_fm(XNT, XNF)

    def dequant_v(e, wsrc, wdst):
        w3 = wsrc.ap().rearrange("p (b k) -> p b k", k=32)
        e.reduce_max(EVG[:, 0:32], w3, axis=AX.X, apply_absolute_value=True)
        e.tensor_scalar(out=EVG[:, 32:64], in0=EVG[:, 0:32], scalar1=1.0 / 7.0,
                        scalar2=1e-12, op0=ALU.mult, op1=ALU.max)
        e.reciprocal(EVG[:, 64:96], EVG[:, 32:64])
        e.tensor_mul(WQT.ap().rearrange("p (b k) -> p b k", k=32),
                     w3, EVG.ap()[:, 64:96, None].broadcast_to([128, 32, 32]))
        e.tensor_scalar(out=WQT[:, :], in0=WQT[:, :], scalar1=MAGIC,
                        scalar2=MAGIC, op0=ALU.add, op1=ALU.subtract)
        e.tensor_mul(wdst.ap().rearrange("p (b k) -> p b k", k=32),
                     WQT.ap().rearrange("p (b k) -> p b k", k=32),
                     EVG.ap()[:, 32:64, None].broadcast_to([128, 32, 32]))

    def emit_upgate(w_dram, spill, maxcol):
        for m in range(NI + 5):
            if m < NI:
                DMA1("sync", (WUF0 if m % 2 == 0 else WUF1)[:, :],
                     w_dram[128 * m:128 * m + 128, :])
            md = m - 1
            if 0 <= md < NI:
                plan.v(lambda e, md=md: dequant_v(
                    e, WUF0 if md % 2 == 0 else WUF1,
                    WUB0 if md % 2 == 0 else WUB1))
            mt = m - 2
            if 0 <= mt < NI:
                wsrc = WUB0 if mt % 2 == 0 else WUB1
                wdst = WLT0 if mt % 2 == 0 else WLT1
                for c in range(NC):
                    DMAT("scalar", wdst[:, c, :], wsrc[:, 128 * c:128 * c + 128])
            mm_ = m - 3
            if 0 <= mm_ < NI:
                wsrc = WLT0 if mm_ % 2 == 0 else WLT1
                pbase = (mm_ % 2) * 4

                def mf(e, wsrc=wsrc, pbase=pbase):
                    for nt in range(4):
                        for kc in range(NC):
                            e.matmul(PB[pbase + nt][:, 0:512], wsrc[:, kc, :],
                                     XQF[:, kc, 512 * nt:512 * nt + 512],
                                     start=(kc == 0), stop=(kc == NC - 1))
                if not os.environ.get("MK_NOMM"):
                    plan.t(mf)
            me = m - 4
            if 0 <= me < NI:
                pbase = (me % 2) * 4
                ebuf = EB0 if me % 2 == 0 else EB1

                def evf(e, pbase=pbase, ebuf=ebuf):
                    for nt in range(4):
                        e.copy(ebuf[:, 512 * nt:512 * nt + 512],
                               PB[pbase + nt][:, 0:512])
                def evfv(e, pbase=pbase, ebuf=ebuf):
                    for nt in range(4):
                        e.tensor_copy(ebuf[:, 512 * nt:512 * nt + 512],
                                      PB[pbase + nt][:, 0:512])
                if os.environ.get("MK_EVACS"):
                    plan.s(evf)
                elif not os.environ.get("MK_NOEVAC"):
                    plan.v(evfv)
            ms = m - 5
            if 0 <= ms < NI:
                ebuf = EB0 if ms % 2 == 0 else EB1

                def rf(e, ebuf=ebuf, ms=ms, maxcol=maxcol):
                    e.reduce_max(maxcol[:, ms:ms + 1], ebuf[:, :], axis=AX.X,
                                 apply_absolute_value=True)
                if not os.environ.get("MK_NORF"):
                    plan.v(rf)
                if not os.environ.get("MK_NOSPILL"):
                    DMA1("gpsimd", spill[ms, :, :], ebuf[:, :])
            plan.bar()

    def emit_wd_dequant():
        for m in range(NI + 5):
            if m < NI:
                for r in range(8):
                    DMA1("sync",
                         (WUF0 if m % 2 == 0 else WUF1)[:, 128 * r:128 * r + 128],
                         din["l0_wd"][128 * r:128 * r + 128, 128 * m:128 * m + 128])
            md = m - 1
            if 0 <= md < NI:
                plan.v(lambda e, md=md: dequant_v(
                    e, WUF0 if md % 2 == 0 else WUF1,
                    WUB0 if md % 2 == 0 else WUB1))
            mt = m - 2
            if 0 <= mt < NI:
                wsrc = WUB0 if mt % 2 == 0 else WUB1

                def twd(e, wsrc=wsrc, mt=mt):
                    pbb = PBB[mt % 2]
                    for r in range(8):
                        e.transpose(pbb[:, 128 * r:128 * r + 128],
                                    wsrc[:, 128 * r:128 * r + 128], IDB[:, :])
                plan.t(twd)
            mw = m - 3
            if 0 <= mw < NI:
                wdst = WLT0 if mw % 2 == 0 else WLT1

                def ewd(e, wdst=wdst, mw=mw):
                    pbb = PBB[mw % 2]
                    for r in range(8):
                        e.copy(wdst[:, r, :], pbb[:, 128 * r:128 * r + 128])
                plan.s(ewd)
            msv = m - 4
            if 0 <= msv < NI:
                wsrc = WLT0 if msv % 2 == 0 else WLT1
                DMA1("gpsimd", d_wdT[msv, :, :],
                     wsrc.ap().rearrange("p r k -> p (r k)"))
            plan.bar()

    attn_cnt = {"sc": 0, "ex": 0}

    def emit_attention(l1):
        for gi in range(10):
            def vup(e, gi=gi):
                if gi < 8:
                    qsrc = QF if gi < 4 else KF
                    nt = gi % 4
                    sq = SQS if gi % 2 == 0 else SQS2
                    for c in range(NC):
                        e.tensor_mul(sq[:, c, :],
                                     qsrc[:, c, 512 * nt:512 * nt + 512],
                                     qsrc[:, c, 512 * nt:512 * nt + 512])
                gr = gi - 2
                if gr >= 0:
                    e.reduce_max(SX[0:16, gr:gr + 1],
                                 PB[4 if gr % 2 == 0 else 5][0:16, 0:512],
                                 axis=AX.X)
            plan.v(vup)
            gm = gi - 1
            if 0 <= gm < 8:
                def msq(e, gm=gm):
                    sq = SQS if gm % 2 == 0 else SQS2
                    pb = PB[4 if gm % 2 == 0 else 5]
                    for c in range(NC):
                        e.matmul(pb[0:16, 0:512], ONEHS[:, c, :], sq[:, c, :],
                                 start=(c == 0), stop=(c == NC - 1))
                plan.t(msq)
            plan.bar()

        def vredu(e):
            e.reduce_max(SC[0:16, 20:21], SX[0:16, 0:4], axis=AX.X)
            e.reduce_max(SC[0:16, 21:22], SX[0:16, 4:8], axis=AX.X)
            e.tensor_mul(SC[0:16, 22:23], SC[0:16, 20:21], SC[0:16, 21:22])
        plan.v(vredu)
        plan.bar()
        plan.t(lambda e: e.transpose(PB[6][0:1, 0:16], SC[0:16, 22:23],
                                     IDF[0:16, 0:16]))
        plan.bar()

        def vU(e):
            e.reduce_max(SR[0:1, 20:21], PB[6][0:1, 0:16], axis=AX.X)
        plan.v(vU)
        plan.bar()
        plan.s(lambda e: e.activation(SR[0:1, 21:22], SR[0:1, 20:21], AF.Sqrt))
        plan.bar()

        def vnU(e):
            e.tensor_scalar(out=SR[0:1, 22:23], in0=SR[0:1, 21:22],
                            scalar1=-0.125, scalar2=None, op0=ALU.mult)
        plan.v(vnU)
        plan.bar()
        bcast_scalar(22, 23)

        blocks = [(h, jq) for h in range(H) for jq in range(4)]
        for bi in range(len(blocks) + 1):
            if bi < len(blocks):
                h, jq = blocks[bi]
                po = 64 * (h % 2)
                ch = h // 2
                n_kt = 4 * jq + 4
                pav = PB[2 + (bi % 2)]
                sc_b = attn_cnt["sc"]
                ex_b = attn_cnt["ex"]
                attn_cnt["sc"] += n_kt
                attn_cnt["ex"] += n_kt

                def tf(e, h=h, jq=jq, n_kt=n_kt, po=po, ch=ch, pav=pav,
                       sc_b=sc_b, ex_b=ex_b):
                    for i in range(n_kt):
                        ps = PB[i % 2]
                        if i >= 2:
                            e.wait_ge(sem_ex, ex_b + i - 1)
                        d = i - 4 * jq
                        if d >= 0:
                            e.matmul(ps[:, 0:512],
                                     KF[po:po + 64, ch, 128 * i:128 * i + 128],
                                     QF[po:po + 64, ch, 512 * jq:512 * jq + 512],
                                     start=True, stop=False)
                            e.matmul(ps[:, 0:512], IDB[:, :], MNEG[:, d, :],
                                     start=False, stop=True).then_inc(sem_sc, 1)
                        else:
                            e.matmul(ps[:, 0:512],
                                     KF[po:po + 64, ch, 128 * i:128 * i + 128],
                                     QF[po:po + 64, ch, 512 * jq:512 * jq + 512],
                                     start=True, stop=True).then_inc(sem_sc, 1)
                        if i >= 1:
                            e.wait_ge(sem_ex, ex_b + i)
                            for cx in range(4):
                                e.matmul(pav[:, 65 * cx:65 * cx + 65],
                                         PTS[:, (i - 1) % 2,
                                             128 * cx:128 * cx + 128],
                                         VA[:, i - 1, h, :],
                                         start=(i == 1 and cx == 0), stop=False)
                    e.wait_ge(sem_ex, ex_b + n_kt)
                    for cx in range(4):
                        e.matmul(pav[:, 65 * cx:65 * cx + 65],
                                 PTS[:, (n_kt - 1) % 2, 128 * cx:128 * cx + 128],
                                 VA[:, n_kt - 1, h, :],
                                 start=False, stop=(cx == 3))
                plan.t(tf)

                def sf(e, n_kt=n_kt, sc_b=sc_b):
                    for i in range(n_kt):
                        e.wait_ge(sem_sc, sc_b + i + 1)
                        e.activation(PTS[:, i % 2, :], PB[i % 2][:, 0:512], AF.Exp,
                                     bias=SC[:, 23:24], scale=1.0
                                     ).then_inc(sem_ex, 1)
                plan.s(sf)
            bj = bi - 1
            if bj >= 0:
                hj, jqj = blocks[bj]
                pavj = PB[2 + (bj % 2)]
                pav3 = pavj.ap()[:, 0:260].rearrange("p (c j) -> p c j", c=4)
                rcpj = RCP[:, 0:4] if bj % 2 == 0 else RCP[:, 4:8]

                def ev(e, hj=hj, jqj=jqj, pav3=pav3, rcpj=rcpj):
                    e.reciprocal(rcpj, pav3[:, :, 64:65])
                    e.tensor_mul(
                        HT[:, 4 * jqj:4 * jqj + 4, 64 * hj:64 * hj + 64],
                        pav3[:, :, 0:64],
                        rcpj[:, :, None].broadcast_to([128, 4, 64]))
                plan.v(ev)
            plan.bar()

    def emit_proj_fm(dst_fm, w_ap, col0, rope, qscale=None):
        for half in range(2):
            DMA1("gpsimd", (WB if half == 0 else WB2).ap(),
                 w_ap[:, col0 + 512 * half: col0 + 512 * (half + 1)]
                 .rearrange("(c p) n -> p c n", p=128))
        plan.bar()
        for gmc in range(9):
            if gmc < 8:
                wb = WB if gmc < 4 else WB2
                mc = gmc % 4
                pbase = (gmc % 2) * 4

                def mf(e, wb=wb, mc=mc, pbase=pbase):
                    for nt in range(4):
                        for kc in range(NC):
                            e.matmul(PB[pbase + nt][:, 0:512],
                                     wb[:, kc, 128 * mc:128 * mc + 128],
                                     XNF[:, kc, 512 * nt:512 * nt + 512],
                                     start=(kc == 0), stop=(kc == NC - 1))
                plan.t(mf)
            ge = gmc - 1
            if ge >= 0:
                pbase = (ge % 2) * 4

                def evf(e, pbase=pbase, ge=ge, dst=dst_fm, qs=qscale):
                    for nt in range(4):
                        if qs is None:
                            e.copy(dst[:, ge, 512 * nt:512 * nt + 512],
                                   PB[pbase + nt][:, 0:512])
                        else:
                            e.activation(dst[:, ge, 512 * nt:512 * nt + 512],
                                         PB[pbase + nt][:, 0:512], AF.Copy,
                                         scale=qs)
                plan.s(evf)
            plan.bar()
        if rope:
            for c in range(NC + 1):
                if c < NC:
                    def mro(e, c=c, dst=dst_fm):
                        pbase = (c % 2) * 4
                        for nt in range(4):
                            e.matmul(PB[pbase + nt][:, 0:512], PSW[:, :],
                                     dst[:, c, 512 * nt:512 * nt + 512],
                                     start=True, stop=True)
                    plan.t(mro)

                def vro(e, c=c, dst=dst_fm):
                    if c < NC:
                        t1 = ROPET if c % 2 == 0 else ROPET2
                        for nt in range(4):
                            e.tensor_mul(t1[:, nt, :],
                                         dst[:, c, 512 * nt:512 * nt + 512],
                                         CCT[:, 512 * nt:512 * nt + 512])
                    cm = c - 1
                    if cm >= 0:
                        t1m = ROPET if cm % 2 == 0 else ROPET2
                        pbase = (cm % 2) * 4
                        for nt in range(4):
                            e.tensor_mul(EVG[:, 0:512], PB[pbase + nt][:, 0:512],
                                         SST[:, 512 * nt:512 * nt + 512])
                            e.tensor_add(dst[:, cm, 512 * nt:512 * nt + 512],
                                         t1m[:, nt, :], EVG[:, 0:512])
                plan.v(vro)
                plan.bar()

    def _pipe_tok(mm_fn, ev_fn):
        for tt in range(NT + 1):
            if tt < NT:
                plan.t(lambda e, tt=tt: mm_fn(e, tt))
            if tt - 1 >= 0:
                plan.s(lambda e, tt=tt: ev_fn(e, tt - 1))
            plan.bar()

    def _kt_scatter(e, tt, swapped):
        pb0, pb1 = PB[(tt % 2) * 2], PB[(tt % 2) * 2 + 1]
        for b5, pb in ((0, pb0), (1, pb1)):
            srcv = pb.ap()[:, 0:512].rearrange(
                "p (g par ih il d) -> p par ih il g d",
                g=4, par=2, ih=2, il=8, d=4)
            for ph2 in range(2):
                r = 2 * tt + ((1 - ph2) if swapped else ph2)
                for ih, buf in ((0, KTTD), (1, KTTA)):
                    s4 = srcv[:, ph2, ih]
                    dst = buf.ap().rearrange(
                        "p s (G p2 d r) -> p s G p2 d r",
                        G=8, p2=2, d=4, r=32)
                    e.copy(dst[:, :, 4 * b5:4 * b5 + 4, ph2, :, r], s4)

    def emit_proj_tok_KT():
        def mm_fn(e, tt):
            pb0, pb1 = PB[(tt % 2) * 2], PB[(tt % 2) * 2 + 1]
            for kc in range(NC):
                e.matmul(pb0[:, 0:512], XNF[:, kc, 128 * tt:128 * tt + 128],
                         WB[:, kc, 0:512], start=(kc == 0), stop=(kc == NC - 1))
            for kc in range(NC):
                e.matmul(pb1[:, 0:512], XNF[:, kc, 128 * tt:128 * tt + 128],
                         WB2[:, kc, 0:512], start=(kc == 0), stop=(kc == NC - 1))

        def ev_fn(e, tt):
            _kt_scatter(e, tt, swapped=False)
        _pipe_tok(mm_fn, ev_fn)

    def emit_proj_tok_KTS():
        def mm_fn(e, tt):
            pb0, pb1 = PB[(tt % 2) * 2], PB[(tt % 2) * 2 + 1]
            for half in range(2):
                src_sl = slice(128 * tt + 64 * (1 - half),
                               128 * tt + 64 * (1 - half) + 64)
                for kc in range(NC):
                    e.matmul(pb0[64 * half:64 * half + 64, 0:512],
                             XNF[:, kc, src_sl], WB[:, kc, 0:512],
                             start=(kc == 0), stop=(kc == NC - 1))
                for kc in range(NC):
                    e.matmul(pb1[64 * half:64 * half + 64, 0:512],
                             XNF[:, kc, src_sl], WB2[:, kc, 0:512],
                             start=(kc == 0), stop=(kc == NC - 1))

        def ev_fn(e, tt):
            _kt_scatter(e, tt, swapped=True)
        _pipe_tok(mm_fn, ev_fn)

    def emit_proj_tok_VS():
        def mm_fn(e, blk):
            pb0, pb1 = PB[(blk % 2) * 2], PB[(blk % 2) * 2 + 1]
            for slq in range(2):
                lh = XNF.ap()[:, :, 2 * blk + slq: 2 * blk + slq + 32 * 63 + 1:32]
                for kc in range(NC):
                    e.matmul(pb0[64 * slq:64 * slq + 64, 0:512], lh[:, kc, :],
                             WB[:, kc, 0:512], start=(kc == 0), stop=(kc == NC - 1))
                for kc in range(NC):
                    e.matmul(pb1[64 * slq:64 * slq + 64, 0:512], lh[:, kc, :],
                             WB2[:, kc, 0:512], start=(kc == 0), stop=(kc == NC - 1))

        def ev_fn(e, blk):
            pb0, pb1 = PB[(blk % 2) * 2], PB[(blk % 2) * 2 + 1]
            e.copy(VA[:, blk, 0:8, 0:64], pb0[:, 0:512])
            e.copy(VA[:, blk, 8:16, 0:64], pb1[:, 0:512])
        _pipe_tok(mm_fn, ev_fn)

    def emit_attn_l1s_ubound():
        for gi in range(6):
            def vup(e, gi=gi):
                if gi < 4:
                    sq = SQKQ if gi % 2 == 0 else SQKQ2
                    for c in range(NC):
                        e.tensor_mul(sq[:, c, :],
                                     QF[:, c, 512 * gi:512 * gi + 512],
                                     QF[:, c, 512 * gi:512 * gi + 512])
                gr = gi - 2
                if gr >= 0:
                    e.reduce_max(SX[0:16, gr:gr + 1],
                                 PB[4 if gr % 2 == 0 else 5][0:16, 0:512],
                                 axis=AX.X)
            plan.v(vup)
            gm = gi - 1
            if 0 <= gm < 4:
                def msq(e, gm=gm):
                    sq = SQKQ if gm % 2 == 0 else SQKQ2
                    pb = PB[4 if gm % 2 == 0 else 5]
                    for c in range(NC):
                        e.matmul(pb[0:16, 0:512], ONEHS[:, c, :], sq[:, c, :],
                                 start=(c == 0), stop=(c == NC - 1))
                plan.t(msq)
            plan.bar()
        for tt in range(NT + 2):
            def vkp(e, tt=tt):
                if tt < NT:
                    sq = SQKQ if tt % 2 == 0 else SQKQ2
                    kd = (KTTD if tt < 8 else KTTA).ap()[:, tt % 8, :] \
                        .rearrange("p (a b) -> p a b", b=256)
                    for c4, cl in ((0, 0), (1, 192)):
                        e.tensor_mul(sq[:, c4, :].rearrange(
                                         "p (a b) -> p a b", b=64),
                                     kd[:, :, cl:cl + 64], kd[:, :, cl:cl + 64])
                tr = tt - 2
                if tr >= 0:
                    e.reduce_max(K2M[0:2, tr:tr + 1],
                                 PB[5 if tr % 2 == 0 else 6][0:2, 0:512],
                                 axis=AX.X)
            plan.v(vkp)
            tm = tt - 1
            if 0 <= tm < NT:
                def mks(e, tm=tm):
                    sq = SQKQ if tm % 2 == 0 else SQKQ2
                    pb = PB[5 if tm % 2 == 0 else 6]
                    e.matmul(pb[0:2, 0:512], ONES2[:, :], sq[:, 0, :],
                             start=True, stop=False)
                    e.matmul(pb[0:2, 0:512], ONES2[:, :], sq[:, 1, :],
                             start=False, stop=True)
                plan.t(mks)
            plan.bar()

        def vkred(e):
            e.reduce_max(SC[0:16, 20:21], SX[0:16, 0:4], axis=AX.X)
            e.reduce_max(SC[0:2, 21:22], K2M[0:2, 0:16], axis=AX.X)
        plan.v(vkred)
        plan.bar()
        plan.t(lambda e: e.transpose(PB[6][0:1, 0:16], SC[0:16, 20:21],
                                     IDF[0:16, 0:16]))
        plan.bar()

        def vq2(e):
            e.reduce_max(SR[0:1, 20:21], PB[6][0:1, 0:16], axis=AX.X)
        plan.v(vq2)
        plan.bar()
        plan.t(lambda e: e.transpose(PB[6][0:1, 0:2], SC[0:2, 21:22],
                                     IDF[0:2, 0:2]))
        plan.bar()

        def vk2(e):
            e.reduce_max(SR[0:1, 21:22], PB[6][0:1, 0:2], axis=AX.X)
            e.tensor_mul(SR[0:1, 22:23], SR[0:1, 20:21], SR[0:1, 21:22])
        plan.v(vk2)
        plan.bar()
        plan.s(lambda e: e.activation(SR[0:1, 23:24], SR[0:1, 22:23], AF.Sqrt))
        plan.bar()

        def vnU(e):
            e.tensor_scalar(out=SR[0:1, 24:25], in0=SR[0:1, 23:24],
                            scalar1=-1.0, scalar2=None, op0=ALU.mult)
        plan.v(vnU)
        plan.bar()
        bcast_scalar(24, 23)

    def emit_attention_l1s():
        blocks = [(h, jq) for h in range(H) for jq in range(4)]
        nb = len(blocks)
        for bi in range(nb + 2):
            if bi < nb:
                h, jq = blocks[bi]
                n_kt = 4 * jq + 4
                pav = PB[4 + bi % 3]
                sc_b = attn_cnt["sc"]
                ex_b = attn_cnt["ex"]
                attn_cnt["sc"] += n_kt
                attn_cnt["ex"] += n_kt

                def tf(e, h=h, jq=jq, bi=bi, n_kt=n_kt, pav=pav,
                       sc_b=sc_b, ex_b=ex_b):
                    hb = 64 * (h % 2)

                    def av(j, start, stop):
                        for cx in range(4):
                            mm = e.matmul(pav[:, 65 * cx:65 * cx + 65],
                                          PTS[:, j % 4, 128 * cx:128 * cx + 128],
                                          VA[:, j, h, :],
                                          start=(start and cx == 0),
                                          stop=(stop and cx == 3))
                        return mm

                    for i in range(n_kt):
                        ps = PB[i % 4]
                        if ex_b + i - 3 > 0:
                            e.wait_ge(sem_ex, ex_b + i - 3)
                        kt = KTTD if i < 8 else KTTA
                        d = i - 4 * jq
                        if d >= 0:
                            e.matmul(ps[0:128, 0:512],
                                     kt[hb:hb + 64, i % 8, 128 * h:128 * h + 128],
                                     QF[hb:hb + 64, h // 2,
                                        512 * jq:512 * jq + 512],
                                     start=True, stop=False)
                            e.matmul(ps[0:128, 0:512], IDB[:, :], MNEG[:, d, :],
                                     start=False, stop=True).then_inc(sem_sc, 1)
                        else:
                            e.matmul(ps[0:128, 0:512],
                                     kt[hb:hb + 64, i % 8, 128 * h:128 * h + 128],
                                     QF[hb:hb + 64, h // 2,
                                        512 * jq:512 * jq + 512],
                                     start=True, stop=True).then_inc(sem_sc, 1)
                        if i >= 3:
                            if i == 3 and bi >= 3:
                                e.wait_ge(sem_ev, bi - 2)
                            e.wait_ge(sem_ex, ex_b + i - 2)
                            av(i - 3, start=(i == 3), stop=False)
                    for j in (n_kt - 3, n_kt - 2, n_kt - 1):
                        e.wait_ge(sem_ex, ex_b + j + 1)
                        mm = av(j, start=False, stop=(j == n_kt - 1))
                    mm.then_inc(sem_av, 1)
                plan.t(tf)

                def sf(e, n_kt=n_kt, sc_b=sc_b):
                    for i in range(n_kt):
                        e.wait_ge(sem_sc, sc_b + i + 1)
                        e.activation(PTS[:, i % 4, :], PB[i % 4][:, 0:512], AF.Exp,
                                     bias=SC[:, 23:24], scale=1.0
                                     ).then_inc(sem_ex, 1)
                plan.s(sf)
            bj = bi - 1
            if 0 <= bj < nb:
                hj, jqj = blocks[bj]
                pavj = PB[4 + bj % 3]
                pav3 = pavj.ap()[:, 0:260].rearrange("p (c j) -> p c j", c=4)
                rcpj = RCP[:, 0:4] if bj % 2 == 0 else RCP[:, 4:8]
                hslot = HOUT.ap()[:, 4 * jqj:4 * jqj + 4, 64 * hj:64 * hj + 64]

                def ev(e, bj=bj, pav3=pav3, rcpj=rcpj, hslot=hslot):
                    e.wait_ge(sem_av, bj + 1)
                    e.reciprocal(rcpj, pav3[:, :, 64:65])
                    e.tensor_mul(hslot, pav3[:, :, 0:64],
                                 rcpj[:, :, None].broadcast_to([128, 4, 64]))
                    e.drain().then_inc(sem_ev, 1)
                plan.v(ev)
        plan.bar()
        for tt in range(NT):
            DMAACC(d_yt[:, tt, :], HOUT[:, tt, :])
        plan.bar()

    def emit_proj_tok_V():
        def mm_fn(e, tt):
            pb0, pb1 = PB[(tt % 2) * 2], PB[(tt % 2) * 2 + 1]
            for kc in range(NC):
                e.matmul(pb0[:, 0:512], XNF[:, kc, 128 * tt:128 * tt + 128],
                         WB[:, kc, 0:512], start=(kc == 0), stop=(kc == NC - 1))
            for kc in range(NC):
                e.matmul(pb1[:, 0:512], XNF[:, kc, 128 * tt:128 * tt + 128],
                         WB2[:, kc, 0:512], start=(kc == 0), stop=(kc == NC - 1))

        def ev_fn(e, tt):
            pb0, pb1 = PB[(tt % 2) * 2], PB[(tt % 2) * 2 + 1]
            e.copy(VA[:, tt, 0:8, 0:64], pb0[:, 0:512])
            e.copy(VA[:, tt, 8:16, 0:64], pb1[:, 0:512])
        _pipe_tok(mm_fn, ev_fn)

    def load_wb_halves(w_ap, col0):
        DMA1("gpsimd", WB.ap(),
             w_ap[:, col0:col0 + 512].rearrange("(c p) n -> p c n", p=128))
        DMA1("gpsimd", WB2.ap(),
             w_ap[:, col0 + 512:col0 + 1024].rearrange("(c p) n -> p c n", p=128))

    DMA1("gpsimd", d_y, d_x)
    plan.bar()
    if SUBST >= 0.2:
        emit_layernorm(d_x.rearrange("(t p) d -> p t d", p=128))

    def _noop(e):
        pass

    def xnmax(e):
        for t in range(NT):
            e.reduce_max(SX[:, t:t + 1], XNT[:, t, :], axis=AX.X,
                         apply_absolute_value=True)
        e.reduce_max(SC[:, 0:1], SX[:, :], axis=AX.X)
    if SUBST >= 0.3:
        plan.v(xnmax)
        plan.bar()
        scalar_from_col(SC[:, 0:1], 0)
        plan.v(lambda e: (e.tensor_copy(SR[0:1, 1:2], SR[0:1, 0:1]), e.drain()))
        plan.bar()
        emit_allgather(0)

    def mkscale1(e):
        e.reduce_max(SR[0:1, 0:1], AG2[:, :, 0:1], axis=AX.XY)
        e.tensor_scalar(out=SR[0:1, 1:2], in0=SR[0:1, 0:1], scalar1=1.0 / QMAX,
                        scalar2=1e-12, op0=ALU.mult, op1=ALU.max)
        e.reciprocal(SR[0:1, 2:3], SR[0:1, 1:2])
    if SUBST >= 0.3:
        plan.v(mkscale1)
        plan.bar()
        bcast_scalar(1, 1)
        bcast_scalar(2, 2)

    def qxn(e):
        for c in range(NC):
            e.tensor_scalar(out=EV0[:, :], in0=XNF[:, c, :], scalar1=SC[:, 2:3],
                            scalar2=MAGIC, op0=ALU.mult, op1=ALU.add)
            e.tensor_scalar(out=XQF[:, c, :], in0=EV0[:, :], scalar1=MAGIC,
                            scalar2=SC[:, 1:2], op0=ALU.subtract, op1=ALU.mult)
    if SUBST >= 0.4:
        plan.v(qxn)
        plan.bar()
    if SUBST >= 0.5:
        emit_upgate(din["l0_wu"], d_up, UXM)
    if SUBST >= 0.6:
        emit_upgate(din["l0_wg"], d_gate, GXM)
        emit_wd_dequant()

    def upgmax(e):
        e.reduce_max(SC[:, 0:1], UXM[:, :], axis=AX.X)
        e.reduce_max(SC[:, 3:4], GXM[:, :], axis=AX.X)
    if SUBST < 0.7:
        plan.bar()

        def _finish0(e):
            pass
        plan.v(_finish0)
    _guard = SUBST >= 0.7
    if _guard:
        plan.v(upgmax)
        plan.bar()
        scalar_from_col(SC[:, 0:1], 0)
        scalar_from_col(SC[:, 3:4], 1)
        emit_allgather(1)

    def mkscale23(e):
        e.reduce_max(SR[0:1, 0:1], AG2[:, :, 0:1], axis=AX.XY)
        e.reduce_max(SR[0:1, 3:4], AG2[:, :, 1:2], axis=AX.XY)
        e.tensor_scalar(out=SR[0:1, 1:2], in0=SR[0:1, 0:1], scalar1=1.0 / QMAX,
                        scalar2=1e-12, op0=ALU.mult, op1=ALU.max)
        e.tensor_scalar(out=SR[0:1, 4:5], in0=SR[0:1, 3:4], scalar1=1.0 / QMAX,
                        scalar2=1e-12, op0=ALU.mult, op1=ALU.max)
        e.reciprocal(SR[0:1, 2:3], SR[0:1, 1:2])
        e.reciprocal(SR[0:1, 5:6], SR[0:1, 4:5])
        e.tensor_copy(SR[0:1, 17:18], SR[0:1, 1:2])
    plan.v(mkscale23)
    plan.bar()
    bcast_scalar(1, 3)
    bcast_scalar(2, 4)
    bcast_scalar(4, 5)
    bcast_scalar(5, 6)

    for m in range(NI + 4):
        if m < NI:
            DMA1("sync", (EB0 if m % 2 == 0 else EB1)[:, :], d_gate[m, :, :])
        ma = m - 1
        if 0 <= ma < NI:
            ebuf = EB0 if ma % 2 == 0 else EB1
            gq = GQ[ma % 3]

            def cfa(e, ebuf=ebuf, gq=gq):
                e.tensor_scalar(out=EV5[:, :], in0=ebuf[:, :], scalar1=SC[:, 6:7],
                                scalar2=MAGIC, op0=ALU.mult, op1=ALU.add)
                e.tensor_scalar(out=gq[:, :], in0=EV5[:, :], scalar1=MAGIC,
                                scalar2=None, op0=ALU.subtract)
            plan.g(cfa)
        mb = m - 2
        if 0 <= mb < NI:
            gq = GQ[mb % 3]
            sg = EV3 if mb % 2 == 0 else EV4
            plan.s(lambda e, gq=gq, sg=sg: e.activation(
                sg[:, :], gq[:, :], AF.Sigmoid, scale=SC[:, 5:6]))
        mc_ = m - 3
        if 0 <= mc_ < NI:
            gq = GQ[mc_ % 3]
            sg = EV3 if mc_ % 2 == 0 else EV4
            tout = EV0 if mc_ % 2 == 0 else EV1

            def cfc(e, gq=gq, sg=sg, tout=tout, mc_=mc_):
                e.tensor_scalar(out=EV2[:, :], in0=sg[:, :], scalar1=65536.0,
                                scalar2=MAGIC, op0=ALU.mult, op1=ALU.add)
                e.tensor_scalar(out=EV2[:, :], in0=EV2[:, :], scalar1=MAGIC,
                                scalar2=65535.0, op0=ALU.subtract, op1=ALU.min)
                e.tensor_mul(tout[:, :], gq[:, :], EV2[:, :])
                e.reduce_max(TXM[:, mc_:mc_ + 1], tout[:, :], axis=AX.X,
                             apply_absolute_value=True)
            plan.v(cfc)
        msp = m - 4
        if 0 <= msp < NI:
            tsrc = EV0 if msp % 2 == 0 else EV1
            DMA1("scalar", d_t[msp, :, :], tsrc[:, :])
        plan.bar()

    def tmaxv(e):
        e.reduce_max(SC[:, 0:1], TXM[:, :], axis=AX.X)
    plan.v(tmaxv)
    plan.bar()
    scalar_from_col(SC[:, 0:1], 0)
    plan.v(lambda e: (e.tensor_copy(SR[0:1, 1:2], SR[0:1, 0:1]), e.drain()))
    plan.bar()
    emit_allgather(2)

    def mkscale4(e):
        e.reduce_max(SR[0:1, 0:1], AG2[:, :, 0:1], axis=AX.XY)
        e.tensor_scalar(out=SR[0:1, 6:7], in0=SR[0:1, 4:5], scalar1=1.0 / 65536.0,
                        scalar2=None, op0=ALU.mult)
        e.tensor_mul(SR[0:1, 7:8], SR[0:1, 0:1], SR[0:1, 6:7])
        e.tensor_scalar(out=SR[0:1, 11:12], in0=SR[0:1, 7:8], scalar1=1.0 / QMAX,
                        scalar2=1e-12, op0=ALU.mult, op1=ALU.max)
        e.reciprocal(SR[0:1, 18:19], SR[0:1, 11:12])
        e.tensor_mul(SR[0:1, 12:13], SR[0:1, 6:7], SR[0:1, 18:19])
    plan.v(mkscale4)
    plan.bar()
    bcast_scalar(11, 7)
    bcast_scalar(12, 8)

    EVT3 = (EV3, EV4, EV2)
    EV56 = (EV5, EV6)
    for m in range(NI + 4):
        if m < NI:
            DMA1("sync", (EV0 if m % 2 == 0 else EV1)[:, :], d_t[m, :, :])
            DMA1("scalar", (EB0 if m % 2 == 0 else EB1)[:, :], d_up[m, :, :])
        ma = m - 1
        if 0 <= ma < NI:
            tsrc = EV0 if ma % 2 == 0 else EV1
            usrc = EB0 if ma % 2 == 0 else EB1

            def cfb1(e, tsrc=tsrc, tb=EVT3[ma % 3]):
                e.tensor_scalar(out=tb[:, :], in0=tsrc[:, :], scalar1=SC[:, 8:9],
                                scalar2=MAGIC, op0=ALU.mult, op1=ALU.add)
            plan.v(cfb1)

            def cfb2(e, usrc=usrc, ub=EV56[ma % 2]):
                e.tensor_scalar(out=ub[:, :], in0=usrc[:, :], scalar1=SC[:, 4:5],
                                scalar2=MAGIC, op0=ALU.mult, op1=ALU.add)
            plan.g(cfb2)
        mb = m - 2
        if 0 <= mb < NI:

            def cfb3(e, tb=EVT3[mb % 3], ub=EV56[mb % 2], mb=mb):
                e.tensor_scalar(out=EV7[:, :], in0=ub[:, :], scalar1=MAGIC,
                                scalar2=None, op0=ALU.subtract)
                e.scalar_tensor_tensor(out=tb[:, :], in0=tb[:, :], scalar=MAGIC,
                                       in1=EV7[:, :], op0=ALU.subtract,
                                       op1=ALU.mult)
                e.reduce_max(T2XM[:, mb:mb + 1], tb[:, :], axis=AX.X,
                             apply_absolute_value=True)
            plan.v(cfb3)
        msp = m - 3
        if 0 <= msp < NI:
            DMA1("scalar", d_t2[msp, :, :], EVT3[msp % 3][:, :])
        plan.bar()

    def t2maxv(e):
        e.reduce_max(SC[:, 0:1], T2XM[:, :], axis=AX.X)
    plan.v(t2maxv)
    plan.bar()
    scalar_from_col(SC[:, 0:1], 0)
    plan.v(lambda e: (e.tensor_copy(SR[0:1, 1:2], SR[0:1, 0:1]), e.drain()))
    plan.bar()
    emit_allgather(3)

    def mkscale5(e):
        e.reduce_max(SR[0:1, 0:1], AG2[:, :, 0:1], axis=AX.XY)
        e.tensor_mul(SR[0:1, 13:14], SR[0:1, 17:18], SR[0:1, 11:12])
        e.tensor_mul(SR[0:1, 14:15], SR[0:1, 0:1], SR[0:1, 13:14])
        e.tensor_scalar(out=SR[0:1, 15:16], in0=SR[0:1, 14:15], scalar1=1.0 / QMAX,
                        scalar2=1e-12, op0=ALU.mult, op1=ALU.max)
        e.reciprocal(SR[0:1, 18:19], SR[0:1, 15:16])
        e.tensor_mul(SR[0:1, 16:17], SR[0:1, 13:14], SR[0:1, 18:19])
    plan.v(mkscale5)
    plan.bar()
    bcast_scalar(15, 9)
    bcast_scalar(16, 10)

    def obf(i):
        if i < 8:
            return OB_A[:, i, :]
        if i < 16:
            return OB_D[:, i - 8, :]
        return OB_E[:, i - 16, :]

    for m in range(NI + 1):
        if m < NI:
            DMA1("sync", (EV0 if m % 2 == 0 else EV1)[:, :], d_t2[m, :, :])
        ma = m - 1
        if 0 <= ma < NI:
            tsrc = EV0 if ma % 2 == 0 else EV1

            def cfo(e, tsrc=tsrc, ma=ma):
                e.tensor_scalar(out=EV2[:, :], in0=tsrc[:, :], scalar1=SC[:, 10:11],
                                scalar2=MAGIC, op0=ALU.mult, op1=ALU.add)
                e.tensor_scalar(out=obf(ma), in0=EV2[:, :], scalar1=MAGIC,
                                scalar2=SC[:, 9:10], op0=ALU.subtract, op1=ALU.mult)
            plan.v(cfo)
        plan.bar()

    for g in range(4):
        for mi in range(NI // 2 + 1):
            if mi < NI // 2:
                wl = (WDR0, WDR1) if mi % 2 == 0 else (WDR2, WDR3)
                DMA1("sync", wl[0][:, :], d_wdT[2 * mi, :, :])
                DMA1("sync", wl[1][:, :], d_wdT[2 * mi + 1, :, :])
            mmi = mi - 1
            if mmi >= 0:
                wl = (WDR0, WDR1) if mmi % 2 == 0 else (WDR2, WDR3)

                def mfd(e, wl=wl, g=g, mmi=mmi):
                    for k in range(2):
                        mm_ = 2 * mmi + k
                        for tt in range(4):
                            gt = 4 * g + tt
                            for hh in range(2):
                                e.matmul(PB[2 * tt + hh][:, 0:512],
                                         obf(mm_)[:, 128 * gt:128 * gt + 128],
                                         wl[k][:, 512 * hh:512 * hh + 512],
                                         start=(mm_ == 0), stop=(mm_ == NI - 1))
                plan.t(mfd)
            plan.bar()
        for tt in range(4):
            gt = 4 * g + tt

            def evd(e, tt=tt):
                e.copy(EVG[:, 0:512], PB[2 * tt][:, 0:512])
                e.copy(EVG[:, 512:1024], PB[2 * tt + 1][:, 0:512])
            plan.s(evd)
            plan.bar()
            DMAACC(d_yt[:, gt, :], EVG[:, :])
            plan.bar()

    if STAGE >= 2:
        DMA1("gpsimd", CCT[:, :], din["cc_tab"])
        DMA1("gpsimd", SST[:, :], din["ss_tab"])
        DMA1("gpsimd", MNEG.ap().rearrange("p a b -> p (a b)"), din["mneg"])
        plan.bar()
        emit_proj_fm(QF, din["l0_wq"], 0, rope=True, qscale=0.125)
        emit_proj_fm(KF, din["l0_wk"], 0, rope=True)
        load_wb_halves(din["l0_wv"], 0)

        def vone(e):
            e.memset(VA[:, :, :, 64:65], 1.0)
        plan.v(vone)
        plan.bar()
        emit_proj_tok_V()
        print("[kernel] bars before L0 attention:", plan.nbars())
        emit_attention(l1=False)
        print("[kernel] bars after L0 attention:", plan.nbars())

        DMA1("gpsimd", WOBR[0:1, :], din["l0_wob"])
        load_wb_halves(din["l0_wo"], 0)
        emit_transpose_tok_to_fm(HT, QF)
        for tt in range(NT + 2):
            if tt < NT:
                pb0, pb1 = PB[(tt % 2) * 2], PB[(tt % 2) * 2 + 1]

                def mfo(e, pb0=pb0, pb1=pb1, tt=tt):
                    for kc in range(NC):
                        e.matmul(pb0[:, 0:512], QF[:, kc, 128 * tt:128 * tt + 128],
                                 WB[:, kc, 0:512], start=(kc == 0), stop=False)
                    e.matmul(pb0[:, 0:512], ONESF[0:1, :], WOBR[0:1, 0:512],
                             start=False, stop=True)
                    for kc in range(NC):
                        e.matmul(pb1[:, 0:512], QF[:, kc, 128 * tt:128 * tt + 128],
                                 WB2[:, kc, 0:512], start=(kc == 0), stop=False)
                    e.matmul(pb1[:, 0:512], ONESF[0:1, :], WOBR[0:1, 512:1024],
                             start=False, stop=True)
                plan.t(mfo)
            te = tt - 1
            if 0 <= te < NT:
                pb0, pb1 = PB[(te % 2) * 2], PB[(te % 2) * 2 + 1]
                evg = EVG if te % 2 == 0 else EVG2

                def evo(e, pb0=pb0, pb1=pb1, evg=evg):
                    e.copy(evg[:, 0:512], pb0[:, 0:512])
                    e.copy(evg[:, 512:1024], pb1[:, 0:512])
                plan.s(evo)
            ta = tt - 2
            if ta >= 0:
                evg = EVG if ta % 2 == 0 else EVG2
                DMAACC(d_yt[:, ta, :], evg[:, :])
            plan.bar()

    if STAGE >= 3:
        emit_layernorm(d_yt)

        load_wb_halves(din["l1_bw_t"], 0)
        DMA1("gpsimd", WBL[0:64, :], din["l1_B"])
        DMA1("gpsimd", WPK.ap(),
             din["l1_pack"].rearrange("(c p) n -> p c n", p=128))
        DMA1("sync", RB16[0:1, 0:16], din["l1_rb16"])
        DMA1("sync", BBR[0:1, :], din["l1_bb"])
        plan.bar()

        for tt in range(NT + 1):
            if tt < NT:
                pb = PB[4 + tt % 2]

                def mfl(e, pb=pb, tt=tt):
                    for kc in range(NC):
                        e.matmul(pb[:, 0:16], XNF[:, kc, 128 * tt:128 * tt + 128],
                                 WPK[:, kc, :], start=(kc == 0), stop=False)
                    e.matmul(pb[:, 0:16], ONESF[0:1, :], RB16[0:1, 0:16],
                             start=False, stop=True)
                plan.t(mfl)
            te = tt - 1
            if te >= 0:
                pbe = PB[4 + te % 2]
                plan.s(lambda e, pbe=pbe, te=te: e.copy(LG[:, te, :], pbe[:, 0:16]))
            plan.bar()

        def vmax_all(e):
            for tt in range(NT):
                e.reduce_max(SM1[:, tt:tt + 1], LG[:, tt, 0:8], axis=AX.X,
                             negate=True)
                e.tensor_scalar(out=M2T[:, tt, :], in0=LG[:, tt, 8:16],
                                scalar1=2.0, scalar2=None, op0=ALU.mult)
        plan.v(vmax_all)
        plan.bar()

        def sexp_all(e):
            for tt in range(NT):
                e.activation(LG[:, tt, 0:8], LG[:, tt, 0:8], AF.Exp,
                             bias=SM1[:, tt:tt + 1], scale=1.0,
                             accum_out=SM2[:, tt:tt + 1])
        plan.s(sexp_all)
        plan.bar()

        def vz_all(e):
            e.reciprocal(SM3[:, :], SM2[:, :])
            for tt in range(NT):
                e.tensor_scalar(out=GT[:, tt, :], in0=LG[:, tt, 0:8],
                                scalar1=SM3[:, tt:tt + 1], scalar2=None,
                                op0=ALU.mult)
            for tt in range(NT):
                e.tensor_mul(
                    EVG.ap()[:, 64 * tt:64 * tt + 64]
                    .rearrange("p (a b) -> p a b", a=8),
                    GT[:, tt, :, None].broadcast_to([128, 8, 8]),
                    M2T[:, tt, None, :].broadcast_to([128, 8, 8]))
        plan.v(vz_all)
        plan.bar()

        def tz_all(e):
            for tt in range(NT):
                e.transpose(PB[4 + tt // 4][0:64, 128 * (tt % 4):128 * (tt % 4) + 128],
                            EVG[:, 64 * tt:64 * tt + 64], IDF[:, :])
        plan.t(tz_all)
        plan.bar()

        def sz_all(e):
            for tt in range(NT):
                e.copy(ZF[0:64, tt, :],
                       PB[4 + tt // 4][0:64, 128 * (tt % 4):128 * (tt % 4) + 128])
        plan.s(sz_all)
        plan.bar()

        for tt in range(NT + 2):
            if tt < NT:
                pb0, pb1 = PB[(tt % 2) * 2], PB[(tt % 2) * 2 + 1]

                def mfb(e, pb0=pb0, pb1=pb1, tt=tt):
                    for kc in range(NC):
                        e.matmul(pb0[:, 0:512], XNF[:, kc, 128 * tt:128 * tt + 128],
                                 WB[:, kc, 0:512], start=(kc == 0), stop=False)
                    e.matmul(pb0[:, 0:512], ZF[0:64, tt, :], WBL[0:64, 0:512],
                             start=False, stop=False)
                    e.matmul(pb0[:, 0:512], ONESF[0:1, :], BBR[0:1, 0:512],
                             start=False, stop=True)
                    for kc in range(NC):
                        e.matmul(pb1[:, 0:512], XNF[:, kc, 128 * tt:128 * tt + 128],
                                 WB2[:, kc, 0:512], start=(kc == 0), stop=False)
                    e.matmul(pb1[:, 0:512], ZF[0:64, tt, :], WBL[0:64, 512:1024],
                             start=False, stop=False)
                    e.matmul(pb1[:, 0:512], ONESF[0:1, :], BBR[0:1, 512:1024],
                             start=False, stop=True)
                plan.t(mfb)
            te = tt - 1
            if 0 <= te < NT:
                pb0, pb1 = PB[(te % 2) * 2], PB[(te % 2) * 2 + 1]
                evg = EVG if te % 2 == 0 else EVG2

                def evb(e, pb0=pb0, pb1=pb1, evg=evg):
                    e.copy(evg[:, 0:512], pb0[:, 0:512])
                    e.copy(evg[:, 512:1024], pb1[:, 0:512])
                plan.s(evb)
            ta = tt - 2
            if ta >= 0:
                evg = EVG if ta % 2 == 0 else EVG2
                DMAACC(d_yt[:, ta, :], evg[:, :])
            plan.bar()

    if STAGE >= 4:
        DMA1("gpsimd", ONES2[:, :], din["ones2"])
        plan.bar()
        emit_proj_fm(QF, din["l1_qkv"], 0, rope=False, qscale=0.125)
        load_wb_halves(din["l1_qkv"], 1024)
        plan.bar()
        emit_proj_tok_KT()
        print("[kernel] bars after KT:", plan.nbars())
        emit_proj_tok_KTS()
        print("[kernel] bars after KTS:", plan.nbars())
        emit_attn_l1s_ubound()
        print("[kernel] bars after ubound:", plan.nbars())
        DMA1("gpsimd", MNEG.ap().rearrange("p a b -> p (a b)"), din["mneg"])
        load_wb_halves(din["l1_qkv"], 2048)

        def vone2(e):
            e.memset(VA[:, :, :, 64:65], 1.0)
        plan.v(vone2)
        plan.bar()
        emit_proj_tok_V()
        print("[kernel] bars after V:", plan.nbars())
        emit_attention_l1s()

    plan.bar()

    bar_limit = int(os.environ.get("MK_TRUNC", "999999"))

    def _ap_range(ap):
        try:
            t = ap.tensor
            off = ap.offset
            dims = list(ap.ap)
            if len(dims) >= 2:
                pitch, pcount = dims[0]
                if pitch <= 0:
                    pitch = 1
                p0 = off // pitch
                f0 = off % pitch
                span = 0
                for step, count in dims[1:]:
                    if count > 1:
                        span += abs(step) * (count - 1)
                return (t.name, p0, p0 + pcount, f0, f0 + span + 1)
            span = 0
            for step, count in dims:
                if count > 1:
                    span += abs(step) * (count - 1)
            return (t.name, 0, 1 << 30, off, off + span + 1)
        except Exception:
            return None

    _DRAINSTAT = [0, 0, 0]

    def _is_ap(x):
        return hasattr(x, "tensor") and hasattr(x, "ap")

    class AutoDrain:
        _PASS = ("wait_ge", "nop", "sem_inc", "dma_start",
                 "dma_start_transpose", "collective_compute", "load_library")

        def __init__(self, e):
            object.__setattr__(self, "_e", e)
            object.__setattr__(self, "_hr", [])
            object.__setattr__(self, "_hw", [])

        def drain(self):
            self._hr.clear()
            self._hw.clear()
            return self._e.drain()

        def _emit(self, name, a, args, kw):
            e = self._e
            aps = []
            for i, x in enumerate(args):
                if _is_ap(x):
                    aps.append(("out" if i == 0 else "in", x))
            for k, x in kw.items():
                if _is_ap(x):
                    kind = "out" if (k.startswith("out") or k == "accum_out"
                                     or k == "dest") else "in"
                    aps.append((kind, x))
            writes = [r for r in (_ap_range(x) for k, x in aps if k == "out") if r]
            reads = [r for r in (_ap_range(x) for k, x in aps if k == "in") if r]
            if name == "memset":
                writes, reads = writes + reads, []

            def overlap(r1, r2):
                return (r1[0] == r2[0] and r1[1] < r2[2] and r2[1] < r1[2]
                        and r1[3] < r2[4] and r2[3] < r1[4])

            hr, hw = self._hr, self._hw
            conflict = (any(overlap(w, p) for w in writes for p in hr)
                        or any(overlap(w, p) for w in writes for p in hw)
                        or any(overlap(r, p) for r in reads for p in hw))
            if conflict:
                e.drain()
                _DRAINSTAT[0] += 1
                hr.clear()
                hw.clear()
            _DRAINSTAT[1] += 1
            _DRAINSTAT[2] += len(writes) + len(reads)
            res = a(*args, **kw)
            hr.extend(reads)
            hw.extend(writes)
            return res

        def __getattr__(self, n):
            a = getattr(self._e, n)
            if callable(a) and n not in self._PASS and not n.startswith("_"):
                def w(*args, _n=n, _a=a, **kw):
                    return self._emit(_n, _a, args, kw)
                return w
            return a

    def make_body(eng_name):
        def body(e):
            if eng_name in ("vector", "scalar", "gpsimd"):
                e = AutoDrain(e)
            dma_count = 0
            cc_count = 0
            bar_i = 0
            my_dma_sem = dma_sems.get(eng_name)
            for it in plan.items:
                if bar_i >= bar_limit:
                    break
                if it[0] == "op":
                    _, oeng, fn = it
                    if oeng != eng_name:
                        continue
                    if isinstance(fn, tuple):
                        kind, f = fn
                        if kind == "DMA":
                            dma_count += f(e, my_dma_sem)
                        else:
                            f(e).then_inc(cc_sem, 1)
                            cc_count += 1
                    else:
                        fn(e)
                else:
                    bar_i += 1
                    if my_dma_sem is not None and dma_count > 0:
                        e.wait_ge(my_dma_sem, dma_count * 16)
                    if eng_name == "gpsimd" and cc_count > 0:
                        e.wait_ge(cc_sem, cc_count)
                    e.drain().then_inc(phase_sems[eng_name], 1)
                    for other in ENGINES:
                        if other != eng_name:
                            e.wait_ge(phase_sems[other], bar_i)
        return body

    nbars = sum(1 for it in plan.items if it[0] == "bar")
    print(f"[kernel] plan: {len(plan.items)} items, {nbars} barriers")
    with nc.Block() as block:
        block.sync(make_body("sync"))
        block.tensor(make_body("tensor"))
        block.vector(make_body("vector"))
        block.scalar(make_body("scalar"))
        block.gpsimd(make_body("gpsimd"))
    st.close()
    print("[kernel] proxy: drains", _DRAINSTAT[0], "ops", _DRAINSTAT[1],
          "aps", _DRAINSTAT[2])
    return nc


_cache = {}


def kernel(**inputs):
    cst = host_constants()
    wd = {
        "l0_wq": inputs["L0_attn_wq"],
        "l0_wk": inputs["L0_attn_wk"],
        "l0_wv": inputs["L0_attn_wv"],
        "l0_wo": inputs["L0_attn_wo_w"],
        "l0_wob": inputs["L0_attn_wo_b"],
        "l0_wg": inputs["L0_ffn_w_gate"],
        "l0_wu": inputs["L0_ffn_w_up"],
        "l0_wd": inputs["L0_ffn_w_down"],
        "l1_qkv": inputs["L1_attn_qkv_weights"],
        "l1_bw_t": np.asarray(inputs["L1_ffn_base_w"]).T,
        "l1_bb": inputs["L1_ffn_base_b"],
        "l1_pack": np.concatenate([np.asarray(inputs["L1_ffn_router_w"]).T,
                                   np.asarray(inputs["L1_ffn_lora_A"])], axis=1),
        "l1_rb16": np.concatenate([np.asarray(inputs["L1_ffn_router_b"]),
                                   np.zeros(8, np.float32)]),
        "l1_B": np.asarray(inputs["L1_ffn_lora_B"]).reshape(64, D),
    }
    wd = {k: np.ascontiguousarray(np.asarray(v, dtype=np.float32))
          for k, v in wd.items()}
    x_full = np.ascontiguousarray(np.asarray(inputs["L0_attn_x"], dtype=np.float32))

    if "nc" not in _cache:
        _cache["nc"] = build_nc()
    nc = _cache["nc"]

    in_maps = []
    for b in range(B):
        m = {"x": np.ascontiguousarray(x_full[b])}
        m.update(wd)
        m.update(cst)
        in_maps.append(m)
    res = run_bass_kernel_spmd(nc, in_maps, core_ids=list(range(8)))
    out = np.stack([res.results[b]["y"] for b in range(B)], axis=0)
    return out.astype(np.float32)


# version 24
# speedup vs baseline: 1.1284x; 1.1284x over previous
import numpy as np
from contextlib import ExitStack

import concourse.bass as bass
import concourse.mybir as mybir
from concourse.bass_utils import run_bass_kernel_spmd

F32 = mybir.dt.float32
BF16 = mybir.dt.bfloat16
I16 = mybir.dt.int16
ALU = mybir.AluOpType
AF = mybir.ActivationFunctionType
AX = mybir.AxisListType

B, S, D = 8, 2048, 1024
H, HD = 16, 64
I3 = 3072
NT = S // 128
NC = D // 128
NI = I3 // 128
MAGIC = 12582912.0
QMAX = 32767.0
EPS = 1e-5

import os
STAGE = int(os.environ.get("MK_STAGE", "4"))
SUBST = float(os.environ.get("MK_SUBST", "99"))


def host_constants():
    c = {}
    inv = 1.0 / (10000.0 ** (np.arange(0, HD, 2, dtype=np.float64) / HD))
    ang = np.outer(np.arange(S, dtype=np.float64), inv)
    cosv, sinv = np.cos(ang), np.sin(ang)
    cc = np.zeros((128, S), dtype=np.float64)
    ss = np.zeros((128, S), dtype=np.float64)
    for p in range(128):
        i = (p % 64) // 2
        cc[p] = cosv[:, i]
        ss[p] = sinv[:, i] * (1.0 if (p % 2) else -1.0)
    c["cc_tab"] = cc.astype(np.float32)
    c["ss_tab"] = ss.astype(np.float32)
    c["ident"] = np.eye(128, dtype=np.float32)
    psw = np.zeros((128, 128), dtype=np.float32)
    for m in range(128):
        psw[m ^ 1, m] = 1.0
    c["pswap"] = psw
    mn = np.zeros((128, 4, 512), dtype=np.float32)
    for d in range(4):
        for p in range(128):
            mn[p, d, : min(512, p + 128 * d)] = -1.0e9
    c["mneg"] = mn.reshape(128, 4 * 512)
    oneh = np.zeros((128, NC, 16), dtype=np.float32)
    for ch in range(NC):
        oneh[0:64, ch, 2 * ch] = 1.0
        oneh[64:128, ch, 2 * ch + 1] = 1.0
    c["oneh"] = oneh.reshape(128, NC * 16)
    mr = np.zeros((64, 544), dtype=np.float32)
    for kk in range(32):
        mr[kk, : 32 * (kk // 2) + (kk % 2) + 30] = -1.0e9
    mr[32, :] = -1.0e9
    c["mskr"] = np.ascontiguousarray(np.vstack([mr] * 2))
    ml = np.zeros((64, 4 * 128), dtype=np.float32)
    for jq in range(4):
        for j in range(128):
            mm = j % 64 - 16 * jq
            if 0 <= mm < 16:
                ml[2 * mm + (j // 64), 128 * jq + j] = 1.0
            elif mm >= 16:
                ml[32, 128 * jq + j] = 1.0
    c["mskl"] = np.ascontiguousarray(np.vstack([ml] * 2))
    o2 = np.zeros((128, 2), dtype=np.float32)
    o2[0:64, 0] = 1.0
    o2[64:128, 1] = 1.0
    c["ones2"] = o2
    return c


ENGINES = ("sync", "tensor", "vector", "scalar", "gpsimd")


class Plan:
    def __init__(self):
        self.items = []

    def op(self, eng, fn):
        self.items.append(("op", eng, fn))

    def t(self, fn): self.op("tensor", fn)
    def v(self, fn): self.op("vector", fn)
    def s(self, fn): self.op("scalar", fn)
    def g(self, fn): self.op("gpsimd", fn)

    def dma(self, q, fn):
        self.op(q, ("DMA", fn))

    def coll(self, fn):
        self.op("gpsimd", ("COLL", fn))

    def bar(self):
        import sys as _s
        f = _s._getframe(1)
        self.items.append(("bar", f"{f.f_lineno}"))

    def nbars(self):
        return sum(1 for it in self.items if it[0] == "bar")

    def ndma(self, q):
        return sum(1 for it in self.items
                   if it[0] == "op" and it[1] == q
                   and isinstance(it[2], tuple) and it[2][0] == "DMA")


def build_nc():
    nc = bass.Bass("TRN2", target_bir_lowering=False)
    st = ExitStack()

    d_x = nc.dram_tensor("x", [S, D], F32, kind="ExternalInput").ap()
    d_y = nc.dram_tensor("y", [S, D], F32, kind="ExternalOutput").ap()
    d_yt = d_y.rearrange("(t p) d -> p t d", p=128)

    din = {}
    for nm, shp in [
        ("l0_wq", [D, D]), ("l0_wk", [D, D]), ("l0_wv", [D, D]),
        ("l0_wo", [D, D]), ("l0_wob", [D]),
        ("l0_wg", [I3, D]), ("l0_wu", [I3, D]), ("l0_wd", [D, I3]),
        ("l1_qkv", [D, 3 * D]),
        ("l1_bw_t", [D, D]), ("l1_bb", [D]),
        ("l1_pack", [D, 16]), ("l1_rb16", [16]), ("l1_B", [64, D]),
        ("cc_tab", [128, S]), ("ss_tab", [128, S]),
        ("ident", [128, 128]), ("pswap", [128, 128]),
        ("mneg", [128, 4 * 512]), ("oneh", [128, NC * 16]),
        ("mskr", [128, 544]), ("mskl", [128, 4 * 128]),
        ("ones2", [128, 2]),
    ]:
        din[nm] = nc.dram_tensor(nm, shp, F32, kind="ExternalInput").ap()

    d_up = nc.dram_tensor("up_sp", [NI, 128, S], BF16).ap()
    d_gate = nc.dram_tensor("gate_sp", [NI, 128, S], BF16).ap()
    d_t = nc.dram_tensor("t_sp", [NI, 128, S], F32).ap()
    d_t2 = nc.dram_tensor("t2_sp", [NI, 128, S], F32).ap()
    d_wdT = nc.dram_tensor("wdT_sp", [NI, 128, D], BF16).ap()
    d_agi = [nc.dram_tensor(f"agi{k}", [1, 2], F32).ap() for k in range(4)]
    d_ago = [nc.dram_tensor(f"ago{k}", [8, 2], F32).ap() for k in range(4)]

    SB_BASE = 0x4400

    def at(name, off, shape, dtype):
        return nc.alloc_sbuf_tensor_at(name, shape, dtype, offset=SB_BASE + off)

    A0, B0, C0, D0, E0 = 0x0000, 0x8000, 0x10000, 0x18000, 0x20000
    F0, G0, H0, H1 = 0x28200, 0x2C200, 0x2E200, 0x30200
    K0 = 0x32200

    XNT = at("XNT", A0, [128, NT, 1024], BF16)
    OB_A = at("OB_A", A0, [128, 8, S], BF16)
    CCT = at("CCT", A0, [128, S], BF16)
    SST = at("SST", A0 + 0x1000, [128, S], BF16)
    SQS = at("SQS", A0 + 0x2000, [128, NC, 512], BF16)
    ROPET = at("ROPET", A0 + 0x4000, [128, 4, 512], BF16)
    ROPET2 = at("ROPET2", A0 + 0x5000, [128, 4, 512], BF16)
    SQS2 = at("SQS2", A0 + 0x6000, [128, NC, 512], BF16)
    HT = at("HT", A0, [128, NT, 1024], BF16)
    KTTA = at("KTTA", A0, [128, 8, 2048], BF16)
    XNF = at("XNF", B0, [128, NC, S], BF16)
    HOUT = at("HOUT", B0, [128, NT, 1024], BF16)
    LNX = at("LNX", C0, [128, NT, 1024], F32)
    QF = at("QF", C0, [128, NC, S], BF16)
    EV0 = at("EV0", C0 + 0x0000, [128, S], F32)
    EV1 = at("EV1", C0 + 0x2000, [128, S], F32)
    EV2 = at("EV2", C0 + 0x4000, [128, S], F32)
    EB0 = at("EB0", C0 + 0x6000, [128, S], BF16)
    EB1 = at("EB1", C0 + 0x7000, [128, S], BF16)
    EV5 = at("EV5", A0 + 0x0000, [128, S], F32)
    EV6 = at("EV6", A0 + 0x2000, [128, S], F32)
    EV7 = at("EV7", A0 + 0x4000, [128, S], F32)
    KF = at("KF", D0, [128, NC, S], BF16)
    KTTD = at("KTTD", D0, [128, 8, 2048], BF16)
    XQF = at("XQF", D0, [128, NC, S], BF16)
    EV3 = at("EV3", D0 + 0x0000, [128, S], F32)
    EV4 = at("EV4", D0 + 0x2000, [128, S], F32)
    GQ = [at(f"GQ{i}", D0 + 0x4000 + i * 0x1000, [128, S], I16) for i in range(3)]
    OB_D = at("OB_D", D0, [128, 8, S], BF16)
    VA = at("VA", E0, [128, NT, H, 65], BF16)
    OB_E = at("OB_E", E0, [128, 8, S], BF16)
    ZF = at("ZF", E0 + 0x4000, [64, NT, 128], BF16)
    WBL = at("WBL", E0 + 0x5000, [64, 1024], BF16)
    WUF0 = at("WUF0", F0, [128, 1024], F32)
    WUF1 = at("WUF1", F0 + 0x1000, [128, 1024], F32)
    WQT = at("WQT", F0 + 0x2000, [128, 1024], F32)
    WUB0 = at("WUB0", F0 + 0x3000, [128, 1024], BF16)
    WUB1 = at("WUB1", F0 + 0x3800, [128, 1024], BF16)
    WOBR = at("WOBR", F0, [1, 1024], F32)
    BBR = at("BBR", F0 + 0x1000, [1, 1024], F32)
    MNEG = at("MNEG", F0 + 0x2000, [128, 4, 512], BF16)
    PTS = at("PTS", F0 + 0x3000, [128, 4, 512], BF16)
    LG = at("LG", F0 + 0x3800, [128, NT, 16], F32)
    GT = at("GT", F0 + 0x3C00, [128, NT, 8], BF16)
    M2T = at("M2T", F0 + 0x3D00, [128, NT, 8], F32)
    SQKQ = at("SQKQ", F0, [128, NC, 512], BF16)
    SQKQ2 = at("SQKQ2", F0 + 0x2000, [128, NC, 512], BF16)
    WLT0 = at("WLT0", G0, [128, 8, 128], BF16)
    WLT1 = at("WLT1", G0 + 0x800, [128, 8, 128], BF16)
    WDR0 = at("WDR0", G0, [128, 1024], BF16)
    WDR1 = at("WDR1", G0 + 0x800, [128, 1024], BF16)
    EVG = at("EVG", G0 + 0x1000, [128, 1024], F32)
    EVG2 = at("EVG2", G0 + 0x0000, [128, 1024], F32)
    WB = at("WB", H0, [128, NC, 512], BF16)
    WB2 = at("WB2", H1, [128, NC, 512], BF16)
    WDR2 = at("WDR2", H0, [128, 1024], BF16)
    WDR3 = at("WDR3", H0 + 0x800, [128, 1024], BF16)
    IDB = at("IDB", K0, [128, 128], BF16)
    IDF = at("IDF", K0 + 0x100, [128, 128], F32)
    PSW = at("PSW", K0 + 0x300, [128, 128], BF16)
    ONESF = at("ONESF", K0 + 0x400, [1, 128], F32)
    ONEHS = at("ONEHS", K0 + 0x600, [128, NC, 16], BF16)
    WPK = at("WPK", K0 + 0x700, [128, NC, 16], BF16)
    RB16 = at("RB16", K0 + 0x800, [1, 16], F32)
    SC = at("SC", K0 + 0x880, [128, 32], F32)
    SR = at("SR", K0 + 0x900, [1, 64], F32)
    SX = at("SX", K0 + 0xA00, [128, 16], F32)
    SX2 = at("SX2", K0 + 0xA40, [128, 16], F32)
    SM1 = at("SM1", K0 + 0xA80, [128, 16], F32)
    SM2 = at("SM2", K0 + 0xAC0, [128, 16], F32)
    SM3 = at("SM3", K0 + 0xB00, [128, 16], F32)
    SM4 = at("SM4", K0 + 0xB40, [128, 16], F32)
    UXM = at("UXM", K0 + 0xB80, [128, 24], F32)
    GXM = at("GXM", K0 + 0xBE0, [128, 24], F32)
    TXM = at("TXM", K0 + 0xC40, [128, 24], F32)
    T2XM = at("T2XM", K0 + 0xCA0, [128, 24], F32)
    AGR = at("AGR", K0 + 0xD00, [1, 16], F32)
    RCP = at("RCP", K0 + 0xD40, [128, 8], F32)
    MSKR = at("MSKR", K0 + 0xD60, [128, 544], BF16)
    MSKL = at("MSKL", K0 + 0x1460, [128, 4 * 128], BF16)
    ONES2 = at("ONES2", K0 + 0x1400, [128, 2], BF16)
    K2M = at("K2M", K0 + 0x1420, [128, 16], F32)
    assert 0x4400 + K0 + 0x18A0 <= 0x38000

    PB = [st.enter_context(nc.psum_tensor(f"PB{i}", [128, 512], F32))
          for i in range(8)]
    PBB = [pb.bitcast(BF16) for pb in PB]

    phase_sems = {e: st.enter_context(nc.semaphore(f"ph_{e}")) for e in ENGINES}
    dma_sems = {q: st.enter_context(nc.semaphore(f"dq_{q}"))
                for q in ("sync", "gpsimd", "scalar")}
    cc_sem = st.enter_context(nc.semaphore("ccs"))
    sem_sc = st.enter_context(nc.semaphore("sc_sc"))
    sem_ex = st.enter_context(nc.semaphore("sc_ex"))
    sem_av = st.enter_context(nc.semaphore("sc_av"))
    sem_ev = st.enter_context(nc.semaphore("sc_ev"))
    sem_dd = st.enter_context(nc.semaphore("sc_dd"))

    plan = Plan()

    def DMA1(q, out, in_):
        def f(e, sem, out=out, in_=in_):
            e.dma_start(out=out, in_=in_).then_inc(sem, 16)
            return 1
        plan.dma(q, f)

    def DMAT(q, out, in_):
        def f(e, sem, out=out, in_=in_):
            e.dma_start_transpose(out, in_).then_inc(sem, 16)
            return 1
        plan.dma(q, f)

    def DMAACC(out, in_, sem2=None, q="gpsimd"):
        def f(e, sem, out=out, in_=in_, sem2=sem2):
            if sem2 is not None:
                e.dma_start(out=out, in_=in_, accum_op=ALU.add).then_inc(sem2, 16)
                return 0
            e.dma_start(out=out, in_=in_, accum_op=ALU.add).then_inc(sem, 16)
            return 1
        plan.dma(q, f)

    def scalar_from_col(col_ap, dst_idx):
        plan.t(lambda e, c=col_ap: e.transpose(PB[6][0:1, 0:128], c, IDF[:, :]))
        plan.bar()

        def vf(e, i=dst_idx):
            e.reduce_max(SR[0:1, i:i + 1], PB[6][0:1, 0:128], axis=AX.X)
        plan.v(vf)
        plan.bar()

    def bcast_scalar(src_idx, dst_col):
        plan.t(lambda e, i=src_idx: e.matmul(
            PB[6][:, 0:1], ONESF[0:1, :], SR[0:1, i:i + 1], start=True, stop=True))
        plan.bar()

        def vf(e, d=dst_col):
            e.tensor_copy(SC[:, d:d + 1], PB[6][:, 0:1])
        plan.v(vf)
        plan.bar()

    def emit_allgather(k):
        DMA1("gpsimd", d_agi[k], SR[0:1, 0:2])
        plan.bar()
        plan.coll(lambda e, k=k: e.collective_compute(
            "AllGather", ALU.bypass, replica_groups=[list(range(8))],
            ins=[d_agi[k]], outs=[d_ago[k]]))
        plan.bar()
        DMA1("gpsimd", AGR[0:1, 0:16], d_ago[k])
        plan.bar()

    AG2 = AGR.ap()[0:1, 0:16].rearrange("a (r v) -> a r v", v=2)

    DMA1("gpsimd", IDB[:, :], din["ident"])
    DMA1("sync", IDF[:, :], din["ident"])
    DMA1("gpsimd", PSW[:, :], din["pswap"])
    DMA1("gpsimd", ONEHS.ap().rearrange("p c m -> p (c m)"), din["oneh"])

    def initc(e):
        e.memset(ONESF[:, :], 1.0)
        e.memset(SR[:, :], 0.0)
    plan.v(initc)
    plan.bar()

    def emit_transpose_tok_to_fm(src_tok, dst_fm):
        for w in range(NT + 1):
            if w < NT:
                pbb = PBB[w % 2]

                def tw(e, w=w, pbb=pbb):
                    for c in range(NC):
                        e.transpose(pbb[:, 128 * c:128 * c + 128],
                                    src_tok[:, w, 128 * c:128 * c + 128],
                                    IDB[:, :])
                plan.t(tw)
            wv = w - 1
            if wv >= 0:
                pbb = PBB[wv % 2]

                def ew(e, wv=wv, pbb=pbb):
                    for c in range(NC):
                        e.copy(dst_fm[:, c, 128 * wv:128 * wv + 128],
                               pbb[:, 128 * c:128 * c + 128])
                plan.s(ew)
            plan.bar()

    def emit_layernorm(src_tiled, copy_y=False):
        for t in range(NT):
            DMA1("sync" if t % 2 == 0 else "gpsimd", LNX[:, t, :], src_tiled[:, t, :])
        plan.bar()
        if copy_y:
            for t in range(NT):
                DMA1("gpsimd" if t % 2 == 0 else "scalar",
                     d_yt[:, t, :], LNX[:, t, :])

        def stats(e):
            for t in range(NT):
                e.reduce_sum(SX[:, t:t + 1], LNX[:, t, :], axis=AX.X)
                e.tensor_mul(EVG[:, :], LNX[:, t, :], LNX[:, t, :])
                e.reduce_sum(SX2[:, t:t + 1], EVG[:, :], axis=AX.X)
        plan.v(stats)
        plan.bar()

        def smalls(e):
            e.tensor_scalar(out=SM1[:, :], in0=SX[:, :], scalar1=1.0 / D,
                            scalar2=None, op0=ALU.mult)
            e.tensor_scalar(out=SM2[:, :], in0=SX2[:, :], scalar1=1.0 / D,
                            scalar2=None, op0=ALU.mult)
            e.tensor_mul(SM3[:, :], SM1[:, :], SM1[:, :])
            e.tensor_sub(SM2[:, :], SM2[:, :], SM3[:, :])
            e.tensor_scalar(out=SM2[:, :], in0=SM2[:, :], scalar1=EPS,
                            scalar2=None, op0=ALU.add)
        plan.v(smalls)
        plan.bar()
        plan.s(lambda e: e.activation(SM3[:, :], SM2[:, :], AF.Sqrt))
        plan.bar()

        def newton(e):
            e.reciprocal(SM4[:, :], SM3[:, :])
            e.tensor_mul(SM3[:, :], SM4[:, :], SM4[:, :])
            e.tensor_mul(SM3[:, :], SM3[:, :], SM2[:, :])
            e.tensor_scalar(out=SM3[:, :], in0=SM3[:, :], scalar1=-0.5,
                            scalar2=1.5, op0=ALU.mult, op1=ALU.add)
            e.tensor_mul(SM4[:, :], SM4[:, :], SM3[:, :])
        plan.v(newton)
        plan.bar()

        def apply(e):
            for t in range(NT):
                e.tensor_scalar(out=XNT[:, t, :], in0=LNX[:, t, :],
                                scalar1=SM1[:, t:t + 1], scalar2=SM4[:, t:t + 1],
                                op0=ALU.subtract, op1=ALU.mult)
        plan.v(apply)
        plan.bar()
        emit_transpose_tok_to_fm(XNT, XNF)

    def dequant_v(e, wsrc, wdst):
        w3 = wsrc.ap().rearrange("p (b k) -> p b k", k=32)
        e.reduce_max(EVG[:, 0:32], w3, axis=AX.X, apply_absolute_value=True)
        e.tensor_scalar(out=EVG[:, 32:64], in0=EVG[:, 0:32], scalar1=1.0 / 7.0,
                        scalar2=1e-12, op0=ALU.mult, op1=ALU.max)
        e.reciprocal(EVG[:, 64:96], EVG[:, 32:64])
        e.tensor_mul(WQT.ap().rearrange("p (b k) -> p b k", k=32),
                     w3, EVG.ap()[:, 64:96, None].broadcast_to([128, 32, 32]))
        e.tensor_scalar(out=WQT[:, :], in0=WQT[:, :], scalar1=MAGIC,
                        scalar2=MAGIC, op0=ALU.add, op1=ALU.subtract)
        e.tensor_mul(wdst.ap().rearrange("p (b k) -> p b k", k=32),
                     WQT.ap().rearrange("p (b k) -> p b k", k=32),
                     EVG.ap()[:, 32:64, None].broadcast_to([128, 32, 32]))

    def emit_upgate(w_dram, spill, maxcol):
        for m in range(NI + 5):
            if m < NI:
                DMA1("sync", (WUF0 if m % 2 == 0 else WUF1)[:, :],
                     w_dram[128 * m:128 * m + 128, :])
            md = m - 1
            if 0 <= md < NI:
                plan.v(lambda e, md=md: dequant_v(
                    e, WUF0 if md % 2 == 0 else WUF1,
                    WUB0 if md % 2 == 0 else WUB1))
            mt = m - 2
            if 0 <= mt < NI:
                wsrc = WUB0 if mt % 2 == 0 else WUB1
                wdst = WLT0 if mt % 2 == 0 else WLT1
                for c in range(NC):
                    DMAT("scalar", wdst[:, c, :], wsrc[:, 128 * c:128 * c + 128])
            mm_ = m - 3
            if 0 <= mm_ < NI:
                wsrc = WLT0 if mm_ % 2 == 0 else WLT1
                pbase = (mm_ % 2) * 4

                def mf(e, wsrc=wsrc, pbase=pbase):
                    for nt in range(4):
                        for kc in range(NC):
                            e.matmul(PB[pbase + nt][:, 0:512], wsrc[:, kc, :],
                                     XQF[:, kc, 512 * nt:512 * nt + 512],
                                     start=(kc == 0), stop=(kc == NC - 1))
                if not os.environ.get("MK_NOMM"):
                    plan.t(mf)
            me = m - 4
            if 0 <= me < NI:
                pbase = (me % 2) * 4
                ebuf = EB0 if me % 2 == 0 else EB1

                def evf(e, pbase=pbase, ebuf=ebuf):
                    for nt in range(4):
                        e.copy(ebuf[:, 512 * nt:512 * nt + 512],
                               PB[pbase + nt][:, 0:512])
                def evfv(e, pbase=pbase, ebuf=ebuf):
                    for nt in range(4):
                        e.tensor_copy(ebuf[:, 512 * nt:512 * nt + 512],
                                      PB[pbase + nt][:, 0:512])
                if os.environ.get("MK_EVACS"):
                    plan.s(evf)
                elif not os.environ.get("MK_NOEVAC"):
                    plan.v(evfv)
            ms = m - 5
            if 0 <= ms < NI:
                ebuf = EB0 if ms % 2 == 0 else EB1

                def rf(e, ebuf=ebuf, ms=ms, maxcol=maxcol):
                    e.reduce_max(maxcol[:, ms:ms + 1], ebuf[:, :], axis=AX.X,
                                 apply_absolute_value=True)
                if not os.environ.get("MK_NORF"):
                    plan.v(rf)
                if not os.environ.get("MK_NOSPILL"):
                    DMA1("gpsimd", spill[ms, :, :], ebuf[:, :])
            plan.bar()

    def emit_wd_dequant():
        for m in range(NI + 5):
            if m < NI:
                for r in range(8):
                    DMA1("sync",
                         (WUF0 if m % 2 == 0 else WUF1)[:, 128 * r:128 * r + 128],
                         din["l0_wd"][128 * r:128 * r + 128, 128 * m:128 * m + 128])
            md = m - 1
            if 0 <= md < NI:
                plan.v(lambda e, md=md: dequant_v(
                    e, WUF0 if md % 2 == 0 else WUF1,
                    WUB0 if md % 2 == 0 else WUB1))
            mt = m - 2
            if 0 <= mt < NI:
                wsrc = WUB0 if mt % 2 == 0 else WUB1

                def twd(e, wsrc=wsrc, mt=mt):
                    pbb = PBB[mt % 2]
                    for r in range(8):
                        e.transpose(pbb[:, 128 * r:128 * r + 128],
                                    wsrc[:, 128 * r:128 * r + 128], IDB[:, :])
                plan.t(twd)
            mw = m - 3
            if 0 <= mw < NI:
                wdst = WLT0 if mw % 2 == 0 else WLT1

                def ewd(e, wdst=wdst, mw=mw):
                    pbb = PBB[mw % 2]
                    for r in range(8):
                        e.copy(wdst[:, r, :], pbb[:, 128 * r:128 * r + 128])
                plan.s(ewd)
            msv = m - 4
            if 0 <= msv < NI:
                wsrc = WLT0 if msv % 2 == 0 else WLT1
                DMA1("gpsimd", d_wdT[msv, :, :],
                     wsrc.ap().rearrange("p r k -> p (r k)"))
            plan.bar()

    attn_cnt = {"sc": 0, "ex": 0}

    def emit_attention(l1):
        for gi in range(10):
            def vup(e, gi=gi):
                if gi < 8:
                    qsrc = QF if gi < 4 else KF
                    nt = gi % 4
                    sq = SQS if gi % 2 == 0 else SQS2
                    for c in range(NC):
                        e.tensor_mul(sq[:, c, :],
                                     qsrc[:, c, 512 * nt:512 * nt + 512],
                                     qsrc[:, c, 512 * nt:512 * nt + 512])
                gr = gi - 2
                if gr >= 0:
                    e.reduce_max(SX[0:16, gr:gr + 1],
                                 PB[4 if gr % 2 == 0 else 5][0:16, 0:512],
                                 axis=AX.X)
            plan.v(vup)
            gm = gi - 1
            if 0 <= gm < 8:
                def msq(e, gm=gm):
                    sq = SQS if gm % 2 == 0 else SQS2
                    pb = PB[4 if gm % 2 == 0 else 5]
                    for c in range(NC):
                        e.matmul(pb[0:16, 0:512], ONEHS[:, c, :], sq[:, c, :],
                                 start=(c == 0), stop=(c == NC - 1))
                plan.t(msq)
            plan.bar()

        def vredu(e):
            e.reduce_max(SC[0:16, 20:21], SX[0:16, 0:4], axis=AX.X)
            e.reduce_max(SC[0:16, 21:22], SX[0:16, 4:8], axis=AX.X)
            e.tensor_mul(SC[0:16, 22:23], SC[0:16, 20:21], SC[0:16, 21:22])
        plan.v(vredu)
        plan.bar()
        plan.t(lambda e: e.transpose(PB[6][0:1, 0:16], SC[0:16, 22:23],
                                     IDF[0:16, 0:16]))
        plan.bar()

        def vU(e):
            e.reduce_max(SR[0:1, 20:21], PB[6][0:1, 0:16], axis=AX.X)
        plan.v(vU)
        plan.bar()
        plan.s(lambda e: e.activation(SR[0:1, 21:22], SR[0:1, 20:21], AF.Sqrt))
        plan.bar()

        def vnU(e):
            e.tensor_scalar(out=SR[0:1, 22:23], in0=SR[0:1, 21:22],
                            scalar1=-0.125, scalar2=None, op0=ALU.mult)
        plan.v(vnU)
        plan.bar()
        bcast_scalar(22, 23)

        blocks = [(h, jq) for h in range(H) for jq in range(4)]
        for bi in range(len(blocks) + 1):
            if bi < len(blocks):
                h, jq = blocks[bi]
                po = 64 * (h % 2)
                ch = h // 2
                n_kt = 4 * jq + 4
                pav = PB[2 + (bi % 2)]
                sc_b = attn_cnt["sc"]
                ex_b = attn_cnt["ex"]
                attn_cnt["sc"] += n_kt
                attn_cnt["ex"] += n_kt

                def tf(e, h=h, jq=jq, n_kt=n_kt, po=po, ch=ch, pav=pav,
                       sc_b=sc_b, ex_b=ex_b):
                    for i in range(n_kt):
                        ps = PB[i % 2]
                        if i >= 2:
                            e.wait_ge(sem_ex, ex_b + i - 1)
                        d = i - 4 * jq
                        if d >= 0:
                            e.matmul(ps[:, 0:512],
                                     KF[po:po + 64, ch, 128 * i:128 * i + 128],
                                     QF[po:po + 64, ch, 512 * jq:512 * jq + 512],
                                     start=True, stop=False)
                            e.matmul(ps[:, 0:512], IDB[:, :], MNEG[:, d, :],
                                     start=False, stop=True).then_inc(sem_sc, 1)
                        else:
                            e.matmul(ps[:, 0:512],
                                     KF[po:po + 64, ch, 128 * i:128 * i + 128],
                                     QF[po:po + 64, ch, 512 * jq:512 * jq + 512],
                                     start=True, stop=True).then_inc(sem_sc, 1)
                        if i >= 1:
                            e.wait_ge(sem_ex, ex_b + i)
                            for cx in range(4):
                                e.matmul(pav[:, 65 * cx:65 * cx + 65],
                                         PTS[:, (i - 1) % 2,
                                             128 * cx:128 * cx + 128],
                                         VA[:, i - 1, h, :],
                                         start=(i == 1 and cx == 0), stop=False)
                    e.wait_ge(sem_ex, ex_b + n_kt)
                    for cx in range(4):
                        e.matmul(pav[:, 65 * cx:65 * cx + 65],
                                 PTS[:, (n_kt - 1) % 2, 128 * cx:128 * cx + 128],
                                 VA[:, n_kt - 1, h, :],
                                 start=False, stop=(cx == 3))
                plan.t(tf)

                def sf(e, n_kt=n_kt, sc_b=sc_b):
                    for i in range(n_kt):
                        e.wait_ge(sem_sc, sc_b + i + 1)
                        e.activation(PTS[:, i % 2, :], PB[i % 2][:, 0:512], AF.Exp,
                                     bias=SC[:, 23:24], scale=1.0
                                     ).then_inc(sem_ex, 1)
                plan.s(sf)
            bj = bi - 1
            if bj >= 0:
                hj, jqj = blocks[bj]
                pavj = PB[2 + (bj % 2)]
                pav3 = pavj.ap()[:, 0:260].rearrange("p (c j) -> p c j", c=4)
                rcpj = RCP[:, 0:4] if bj % 2 == 0 else RCP[:, 4:8]

                def ev(e, hj=hj, jqj=jqj, pav3=pav3, rcpj=rcpj):
                    e.reciprocal(rcpj, pav3[:, :, 64:65])
                    e.tensor_mul(
                        HT[:, 4 * jqj:4 * jqj + 4, 64 * hj:64 * hj + 64],
                        pav3[:, :, 0:64],
                        rcpj[:, :, None].broadcast_to([128, 4, 64]))
                plan.v(ev)
            plan.bar()

    def emit_proj_fm(dst_fm, w_ap, col0, rope, qscale=None):
        for half in range(2):
            DMA1("gpsimd", (WB if half == 0 else WB2).ap(),
                 w_ap[:, col0 + 512 * half: col0 + 512 * (half + 1)]
                 .rearrange("(c p) n -> p c n", p=128))
        plan.bar()
        for gmc in range(9):
            if gmc < 8:
                wb = WB if gmc < 4 else WB2
                mc = gmc % 4
                pbase = (gmc % 2) * 4

                def mf(e, wb=wb, mc=mc, pbase=pbase):
                    for nt in range(4):
                        for kc in range(NC):
                            e.matmul(PB[pbase + nt][:, 0:512],
                                     wb[:, kc, 128 * mc:128 * mc + 128],
                                     XNF[:, kc, 512 * nt:512 * nt + 512],
                                     start=(kc == 0), stop=(kc == NC - 1))
                plan.t(mf)
            ge = gmc - 1
            if ge >= 0:
                pbase = (ge % 2) * 4

                def evf(e, pbase=pbase, ge=ge, dst=dst_fm, qs=qscale):
                    for nt in range(4):
                        if qs is None:
                            e.copy(dst[:, ge, 512 * nt:512 * nt + 512],
                                   PB[pbase + nt][:, 0:512])
                        else:
                            e.activation(dst[:, ge, 512 * nt:512 * nt + 512],
                                         PB[pbase + nt][:, 0:512], AF.Copy,
                                         scale=qs)
                plan.s(evf)
            plan.bar()
        if rope:
            for c in range(NC + 1):
                if c < NC:
                    def mro(e, c=c, dst=dst_fm):
                        pbase = (c % 2) * 4
                        for nt in range(4):
                            e.matmul(PB[pbase + nt][:, 0:512], PSW[:, :],
                                     dst[:, c, 512 * nt:512 * nt + 512],
                                     start=True, stop=True)
                    plan.t(mro)

                def vro(e, c=c, dst=dst_fm):
                    if c < NC:
                        t1 = ROPET if c % 2 == 0 else ROPET2
                        for nt in range(4):
                            e.tensor_mul(t1[:, nt, :],
                                         dst[:, c, 512 * nt:512 * nt + 512],
                                         CCT[:, 512 * nt:512 * nt + 512])
                    cm = c - 1
                    if cm >= 0:
                        t1m = ROPET if cm % 2 == 0 else ROPET2
                        pbase = (cm % 2) * 4
                        for nt in range(4):
                            e.tensor_mul(EVG[:, 0:512], PB[pbase + nt][:, 0:512],
                                         SST[:, 512 * nt:512 * nt + 512])
                            e.tensor_add(dst[:, cm, 512 * nt:512 * nt + 512],
                                         t1m[:, nt, :], EVG[:, 0:512])
                plan.v(vro)
                plan.bar()

    def _pipe_tok(mm_fn, ev_fn):
        for tt in range(NT + 1):
            if tt < NT:
                plan.t(lambda e, tt=tt: mm_fn(e, tt))
            if tt - 1 >= 0:
                plan.s(lambda e, tt=tt: ev_fn(e, tt - 1))
            plan.bar()

    def _kt_scatter(e, tt, swapped):
        pb0, pb1 = PB[(tt % 2) * 2], PB[(tt % 2) * 2 + 1]
        for b5, pb in ((0, pb0), (1, pb1)):
            srcv = pb.ap()[:, 0:512].rearrange(
                "p (g par ih il d) -> p par ih il g d",
                g=4, par=2, ih=2, il=8, d=4)
            for ph2 in range(2):
                r = 2 * tt + ((1 - ph2) if swapped else ph2)
                for ih, buf in ((0, KTTD), (1, KTTA)):
                    s4 = srcv[:, ph2, ih]
                    dst = buf.ap().rearrange(
                        "p s (G p2 d r) -> p s G p2 d r",
                        G=8, p2=2, d=4, r=32)
                    e.copy(dst[:, :, 4 * b5:4 * b5 + 4, ph2, :, r], s4)

    def emit_proj_tok_KT():
        def mm_fn(e, tt):
            pb0, pb1 = PB[(tt % 2) * 2], PB[(tt % 2) * 2 + 1]
            for kc in range(NC):
                e.matmul(pb0[:, 0:512], XNF[:, kc, 128 * tt:128 * tt + 128],
                         WB[:, kc, 0:512], start=(kc == 0), stop=(kc == NC - 1))
            for kc in range(NC):
                e.matmul(pb1[:, 0:512], XNF[:, kc, 128 * tt:128 * tt + 128],
                         WB2[:, kc, 0:512], start=(kc == 0), stop=(kc == NC - 1))

        def ev_fn(e, tt):
            _kt_scatter(e, tt, swapped=False)
        _pipe_tok(mm_fn, ev_fn)

    def emit_proj_tok_KTS():
        def mm_fn(e, tt):
            pb0, pb1 = PB[(tt % 2) * 2], PB[(tt % 2) * 2 + 1]
            for half in range(2):
                src_sl = slice(128 * tt + 64 * (1 - half),
                               128 * tt + 64 * (1 - half) + 64)
                for kc in range(NC):
                    e.matmul(pb0[64 * half:64 * half + 64, 0:512],
                             XNF[:, kc, src_sl], WB[:, kc, 0:512],
                             start=(kc == 0), stop=(kc == NC - 1))
                for kc in range(NC):
                    e.matmul(pb1[64 * half:64 * half + 64, 0:512],
                             XNF[:, kc, src_sl], WB2[:, kc, 0:512],
                             start=(kc == 0), stop=(kc == NC - 1))

        def ev_fn(e, tt):
            _kt_scatter(e, tt, swapped=True)
        _pipe_tok(mm_fn, ev_fn)

    def emit_proj_tok_VS():
        def mm_fn(e, blk):
            pb0, pb1 = PB[(blk % 2) * 2], PB[(blk % 2) * 2 + 1]
            for slq in range(2):
                lh = XNF.ap()[:, :, 2 * blk + slq: 2 * blk + slq + 32 * 63 + 1:32]
                for kc in range(NC):
                    e.matmul(pb0[64 * slq:64 * slq + 64, 0:512], lh[:, kc, :],
                             WB[:, kc, 0:512], start=(kc == 0), stop=(kc == NC - 1))
                for kc in range(NC):
                    e.matmul(pb1[64 * slq:64 * slq + 64, 0:512], lh[:, kc, :],
                             WB2[:, kc, 0:512], start=(kc == 0), stop=(kc == NC - 1))

        def ev_fn(e, blk):
            pb0, pb1 = PB[(blk % 2) * 2], PB[(blk % 2) * 2 + 1]
            e.copy(VA[:, blk, 0:8, 0:64], pb0[:, 0:512])
            e.copy(VA[:, blk, 8:16, 0:64], pb1[:, 0:512])
        _pipe_tok(mm_fn, ev_fn)

    def emit_attn_l1s_ubound():
        for gi in range(6):
            def vup(e, gi=gi):
                if gi < 4:
                    sq = SQKQ if gi % 2 == 0 else SQKQ2
                    for c in range(NC):
                        e.tensor_mul(sq[:, c, :],
                                     QF[:, c, 512 * gi:512 * gi + 512],
                                     QF[:, c, 512 * gi:512 * gi + 512])
                gr = gi - 2
                if gr >= 0:
                    e.reduce_max(SX[0:16, gr:gr + 1],
                                 PB[4 if gr % 2 == 0 else 5][0:16, 0:512],
                                 axis=AX.X)
            plan.v(vup)
            gm = gi - 1
            if 0 <= gm < 4:
                def msq(e, gm=gm):
                    sq = SQKQ if gm % 2 == 0 else SQKQ2
                    pb = PB[4 if gm % 2 == 0 else 5]
                    for c in range(NC):
                        e.matmul(pb[0:16, 0:512], ONEHS[:, c, :], sq[:, c, :],
                                 start=(c == 0), stop=(c == NC - 1))
                plan.t(msq)
            plan.bar()
        for tt in range(NT + 2):
            def vkp(e, tt=tt):
                if tt < NT:
                    sq = SQKQ if tt % 2 == 0 else SQKQ2
                    kd = (KTTD if tt < 8 else KTTA).ap()[:, tt % 8, :] \
                        .rearrange("p (a b) -> p a b", b=256)
                    for c4, cl in ((0, 0), (1, 192)):
                        e.tensor_mul(sq[:, c4, :].rearrange(
                                         "p (a b) -> p a b", b=64),
                                     kd[:, :, cl:cl + 64], kd[:, :, cl:cl + 64])
                tr = tt - 2
                if tr >= 0:
                    e.reduce_max(K2M[0:2, tr:tr + 1],
                                 PB[5 if tr % 2 == 0 else 6][0:2, 0:512],
                                 axis=AX.X)
            plan.v(vkp)
            tm = tt - 1
            if 0 <= tm < NT:
                def mks(e, tm=tm):
                    sq = SQKQ if tm % 2 == 0 else SQKQ2
                    pb = PB[5 if tm % 2 == 0 else 6]
                    e.matmul(pb[0:2, 0:512], ONES2[:, :], sq[:, 0, :],
                             start=True, stop=False)
                    e.matmul(pb[0:2, 0:512], ONES2[:, :], sq[:, 1, :],
                             start=False, stop=True)
                plan.t(mks)
            plan.bar()

        def vkred(e):
            e.reduce_max(SC[0:16, 20:21], SX[0:16, 0:4], axis=AX.X)
            e.reduce_max(SC[0:2, 21:22], K2M[0:2, 0:16], axis=AX.X)
        plan.v(vkred)
        plan.bar()
        plan.t(lambda e: e.transpose(PB[6][0:1, 0:16], SC[0:16, 20:21],
                                     IDF[0:16, 0:16]))
        plan.bar()

        def vq2(e):
            e.reduce_max(SR[0:1, 20:21], PB[6][0:1, 0:16], axis=AX.X)
        plan.v(vq2)
        plan.bar()
        plan.t(lambda e: e.transpose(PB[6][0:1, 0:2], SC[0:2, 21:22],
                                     IDF[0:2, 0:2]))
        plan.bar()

        def vk2(e):
            e.reduce_max(SR[0:1, 21:22], PB[6][0:1, 0:2], axis=AX.X)
            e.tensor_mul(SR[0:1, 22:23], SR[0:1, 20:21], SR[0:1, 21:22])
        plan.v(vk2)
        plan.bar()
        plan.s(lambda e: e.activation(SR[0:1, 23:24], SR[0:1, 22:23], AF.Sqrt))
        plan.bar()

        def vnU(e):
            e.tensor_scalar(out=SR[0:1, 24:25], in0=SR[0:1, 23:24],
                            scalar1=-1.0, scalar2=None, op0=ALU.mult)
        plan.v(vnU)
        plan.bar()
        bcast_scalar(24, 23)

    def emit_attention_l1s():
        blocks = [(h, jq) for h in range(H) for jq in range(4)]
        nb = len(blocks)
        for bi in range(nb + 2):
            if bi < nb:
                h, jq = blocks[bi]
                n_kt = 4 * jq + 4
                pav = PB[4 + bi % 3]
                sc_b = attn_cnt["sc"]
                ex_b = attn_cnt["ex"]
                attn_cnt["sc"] += n_kt
                attn_cnt["ex"] += n_kt

                def tf(e, h=h, jq=jq, bi=bi, n_kt=n_kt, pav=pav,
                       sc_b=sc_b, ex_b=ex_b):
                    hb = 64 * (h % 2)

                    def av(j, start, stop):
                        for cx in range(4):
                            mm = e.matmul(pav[:, 65 * cx:65 * cx + 65],
                                          PTS[:, j % 4, 128 * cx:128 * cx + 128],
                                          VA[:, j, h, :],
                                          start=(start and cx == 0),
                                          stop=(stop and cx == 3))
                        return mm

                    for i in range(n_kt):
                        ps = PB[i % 4]
                        if ex_b + i - 3 > 0:
                            e.wait_ge(sem_ex, ex_b + i - 3)
                        kt = KTTD if i < 8 else KTTA
                        d = i - 4 * jq
                        if d >= 0:
                            e.matmul(ps[0:128, 0:512],
                                     kt[hb:hb + 64, i % 8, 128 * h:128 * h + 128],
                                     QF[hb:hb + 64, h // 2,
                                        512 * jq:512 * jq + 512],
                                     start=True, stop=False)
                            e.matmul(ps[0:128, 0:512], IDB[:, :], MNEG[:, d, :],
                                     start=False, stop=True).then_inc(sem_sc, 1)
                        else:
                            e.matmul(ps[0:128, 0:512],
                                     kt[hb:hb + 64, i % 8, 128 * h:128 * h + 128],
                                     QF[hb:hb + 64, h // 2,
                                        512 * jq:512 * jq + 512],
                                     start=True, stop=True).then_inc(sem_sc, 1)
                        if i >= 3:
                            if i == 3 and bi >= 3:
                                e.wait_ge(sem_ev, bi - 2)
                            e.wait_ge(sem_ex, ex_b + i - 2)
                            av(i - 3, start=(i == 3), stop=False)
                    for j in (n_kt - 3, n_kt - 2, n_kt - 1):
                        e.wait_ge(sem_ex, ex_b + j + 1)
                        mm = av(j, start=False, stop=(j == n_kt - 1))
                    mm.then_inc(sem_av, 1)
                plan.t(tf)

                def sf(e, n_kt=n_kt, sc_b=sc_b):
                    for i in range(n_kt):
                        e.wait_ge(sem_sc, sc_b + i + 1)
                        e.activation(PTS[:, i % 4, :], PB[i % 4][:, 0:512], AF.Exp,
                                     bias=SC[:, 23:24], scale=1.0
                                     ).then_inc(sem_ex, 1)
                plan.s(sf)
            bj = bi - 1
            if 0 <= bj < nb:
                hj, jqj = blocks[bj]
                pavj = PB[4 + bj % 3]
                pav3 = pavj.ap()[:, 0:260].rearrange("p (c j) -> p c j", c=4)
                rcpj = RCP[:, 0:4] if bj % 2 == 0 else RCP[:, 4:8]
                hslot = HOUT.ap()[:, 4 * jqj:4 * jqj + 4, 64 * hj:64 * hj + 64]

                def ev(e, bj=bj, pav3=pav3, rcpj=rcpj, hslot=hslot):
                    e.wait_ge(sem_av, bj + 1)
                    e.reciprocal(rcpj, pav3[:, :, 64:65])
                    e.tensor_mul(hslot, pav3[:, :, 0:64],
                                 rcpj[:, :, None].broadcast_to([128, 4, 64]))
                    e.drain().then_inc(sem_ev, 1)
                plan.v(ev)
        plan.bar()
        for tt in range(NT):
            DMAACC(d_yt[:, tt, :], HOUT[:, tt, :])
        plan.bar()

    def emit_proj_tok_V():
        def mm_fn(e, tt):
            pb0, pb1 = PB[(tt % 2) * 2], PB[(tt % 2) * 2 + 1]
            for kc in range(NC):
                e.matmul(pb0[:, 0:512], XNF[:, kc, 128 * tt:128 * tt + 128],
                         WB[:, kc, 0:512], start=(kc == 0), stop=(kc == NC - 1))
            for kc in range(NC):
                e.matmul(pb1[:, 0:512], XNF[:, kc, 128 * tt:128 * tt + 128],
                         WB2[:, kc, 0:512], start=(kc == 0), stop=(kc == NC - 1))

        def ev_fn(e, tt):
            pb0, pb1 = PB[(tt % 2) * 2], PB[(tt % 2) * 2 + 1]
            e.copy(VA[:, tt, 0:8, 0:64], pb0[:, 0:512])
            e.copy(VA[:, tt, 8:16, 0:64], pb1[:, 0:512])
        _pipe_tok(mm_fn, ev_fn)

    def load_wb_halves(w_ap, col0):
        DMA1("gpsimd", WB.ap(),
             w_ap[:, col0:col0 + 512].rearrange("(c p) n -> p c n", p=128))
        DMA1("gpsimd", WB2.ap(),
             w_ap[:, col0 + 512:col0 + 1024].rearrange("(c p) n -> p c n", p=128))

    DMA1("gpsimd", d_y, d_x)
    plan.bar()
    if SUBST >= 0.2:
        emit_layernorm(d_x.rearrange("(t p) d -> p t d", p=128))

    def _noop(e):
        pass

    def xnmax(e):
        for t in range(NT):
            e.reduce_max(SX[:, t:t + 1], XNT[:, t, :], axis=AX.X,
                         apply_absolute_value=True)
        e.reduce_max(SC[:, 0:1], SX[:, :], axis=AX.X)
    if SUBST >= 0.3:
        plan.v(xnmax)
        plan.bar()
        scalar_from_col(SC[:, 0:1], 0)
        plan.v(lambda e: (e.tensor_copy(SR[0:1, 1:2], SR[0:1, 0:1]), e.drain()))
        plan.bar()
        emit_allgather(0)

    def mkscale1(e):
        e.reduce_max(SR[0:1, 0:1], AG2[:, :, 0:1], axis=AX.XY)
        e.tensor_scalar(out=SR[0:1, 1:2], in0=SR[0:1, 0:1], scalar1=1.0 / QMAX,
                        scalar2=1e-12, op0=ALU.mult, op1=ALU.max)
        e.reciprocal(SR[0:1, 2:3], SR[0:1, 1:2])
    if SUBST >= 0.3:
        plan.v(mkscale1)
        plan.bar()
        bcast_scalar(1, 1)
        bcast_scalar(2, 2)

    def qxn(e):
        for c in range(NC):
            e.tensor_scalar(out=EV0[:, :], in0=XNF[:, c, :], scalar1=SC[:, 2:3],
                            scalar2=MAGIC, op0=ALU.mult, op1=ALU.add)
            e.tensor_scalar(out=XQF[:, c, :], in0=EV0[:, :], scalar1=MAGIC,
                            scalar2=SC[:, 1:2], op0=ALU.subtract, op1=ALU.mult)
    if SUBST >= 0.4:
        plan.v(qxn)
        plan.bar()
    if SUBST >= 0.5:
        emit_upgate(din["l0_wu"], d_up, UXM)
    if SUBST >= 0.6:
        emit_upgate(din["l0_wg"], d_gate, GXM)
        emit_wd_dequant()

    def upgmax(e):
        e.reduce_max(SC[:, 0:1], UXM[:, :], axis=AX.X)
        e.reduce_max(SC[:, 3:4], GXM[:, :], axis=AX.X)
    if SUBST < 0.7:
        plan.bar()

        def _finish0(e):
            pass
        plan.v(_finish0)
    _guard = SUBST >= 0.7
    if _guard:
        plan.v(upgmax)
        plan.bar()
        scalar_from_col(SC[:, 0:1], 0)
        scalar_from_col(SC[:, 3:4], 1)
        emit_allgather(1)

    def mkscale23(e):
        e.reduce_max(SR[0:1, 0:1], AG2[:, :, 0:1], axis=AX.XY)
        e.reduce_max(SR[0:1, 3:4], AG2[:, :, 1:2], axis=AX.XY)
        e.tensor_scalar(out=SR[0:1, 1:2], in0=SR[0:1, 0:1], scalar1=1.0 / QMAX,
                        scalar2=1e-12, op0=ALU.mult, op1=ALU.max)
        e.tensor_scalar(out=SR[0:1, 4:5], in0=SR[0:1, 3:4], scalar1=1.0 / QMAX,
                        scalar2=1e-12, op0=ALU.mult, op1=ALU.max)
        e.reciprocal(SR[0:1, 2:3], SR[0:1, 1:2])
        e.reciprocal(SR[0:1, 5:6], SR[0:1, 4:5])
        e.tensor_copy(SR[0:1, 17:18], SR[0:1, 1:2])
    plan.v(mkscale23)
    plan.bar()
    bcast_scalar(1, 3)
    bcast_scalar(2, 4)
    bcast_scalar(4, 5)
    bcast_scalar(5, 6)

    for m in range(NI + 4):
        if m < NI:
            DMA1("sync", (EB0 if m % 2 == 0 else EB1)[:, :], d_gate[m, :, :])
        ma = m - 1
        if 0 <= ma < NI:
            ebuf = EB0 if ma % 2 == 0 else EB1
            gq = GQ[ma % 3]

            def cfa(e, ebuf=ebuf, gq=gq):
                e.tensor_scalar(out=EV5[:, :], in0=ebuf[:, :], scalar1=SC[:, 6:7],
                                scalar2=MAGIC, op0=ALU.mult, op1=ALU.add)
                e.tensor_scalar(out=gq[:, :], in0=EV5[:, :], scalar1=MAGIC,
                                scalar2=None, op0=ALU.subtract)
            plan.v(cfa)
        mb = m - 2
        if 0 <= mb < NI:
            gq = GQ[mb % 3]
            sg = EV3 if mb % 2 == 0 else EV4
            plan.s(lambda e, gq=gq, sg=sg: e.activation(
                sg[:, :], gq[:, :], AF.Sigmoid, scale=SC[:, 5:6]))
        mc_ = m - 3
        if 0 <= mc_ < NI:
            gq = GQ[mc_ % 3]
            sg = EV3 if mc_ % 2 == 0 else EV4
            tout = EV0 if mc_ % 2 == 0 else EV1

            def cfc(e, gq=gq, sg=sg, tout=tout, mc_=mc_):
                e.tensor_scalar(out=EV2[:, :], in0=sg[:, :], scalar1=65536.0,
                                scalar2=MAGIC, op0=ALU.mult, op1=ALU.add)
                e.tensor_scalar(out=EV2[:, :], in0=EV2[:, :], scalar1=MAGIC,
                                scalar2=65535.0, op0=ALU.subtract, op1=ALU.min)
                e.tensor_mul(tout[:, :], gq[:, :], EV2[:, :])
                e.reduce_max(TXM[:, mc_:mc_ + 1], tout[:, :], axis=AX.X,
                             apply_absolute_value=True)
            plan.v(cfc)
        msp = m - 4
        if 0 <= msp < NI:
            tsrc = EV0 if msp % 2 == 0 else EV1
            DMA1("scalar", d_t[msp, :, :], tsrc[:, :])
        plan.bar()

    def tmaxv(e):
        e.reduce_max(SC[:, 0:1], TXM[:, :], axis=AX.X)
    plan.v(tmaxv)
    plan.bar()
    scalar_from_col(SC[:, 0:1], 0)
    plan.v(lambda e: (e.tensor_copy(SR[0:1, 1:2], SR[0:1, 0:1]), e.drain()))
    plan.bar()
    emit_allgather(2)

    def mkscale4(e):
        e.reduce_max(SR[0:1, 0:1], AG2[:, :, 0:1], axis=AX.XY)
        e.tensor_scalar(out=SR[0:1, 6:7], in0=SR[0:1, 4:5], scalar1=1.0 / 65536.0,
                        scalar2=None, op0=ALU.mult)
        e.tensor_mul(SR[0:1, 7:8], SR[0:1, 0:1], SR[0:1, 6:7])
        e.tensor_scalar(out=SR[0:1, 11:12], in0=SR[0:1, 7:8], scalar1=1.0 / QMAX,
                        scalar2=1e-12, op0=ALU.mult, op1=ALU.max)
        e.reciprocal(SR[0:1, 18:19], SR[0:1, 11:12])
        e.tensor_mul(SR[0:1, 12:13], SR[0:1, 6:7], SR[0:1, 18:19])
    plan.v(mkscale4)
    plan.bar()
    bcast_scalar(11, 7)
    bcast_scalar(12, 8)

    EVT3 = (EV3, EV4, EV2)
    EV56 = (EV5, EV6)
    for m in range(NI + 4):
        if m < NI:
            DMA1("sync", (EV0 if m % 2 == 0 else EV1)[:, :], d_t[m, :, :])
            DMA1("scalar", (EB0 if m % 2 == 0 else EB1)[:, :], d_up[m, :, :])
        ma = m - 1
        if 0 <= ma < NI:
            tsrc = EV0 if ma % 2 == 0 else EV1
            usrc = EB0 if ma % 2 == 0 else EB1

            def cfb1(e, tsrc=tsrc, tb=EVT3[ma % 3]):
                e.tensor_scalar(out=tb[:, :], in0=tsrc[:, :], scalar1=SC[:, 8:9],
                                scalar2=MAGIC, op0=ALU.mult, op1=ALU.add)
            plan.v(cfb1)

            def cfb2(e, usrc=usrc, ub=EV56[ma % 2]):
                e.tensor_scalar(out=ub[:, :], in0=usrc[:, :], scalar1=SC[:, 4:5],
                                scalar2=MAGIC, op0=ALU.mult, op1=ALU.add)
            plan.g(cfb2)
        mb = m - 2
        if 0 <= mb < NI:

            def cfb3(e, tb=EVT3[mb % 3], ub=EV56[mb % 2], mb=mb):
                e.tensor_scalar(out=EV7[:, :], in0=ub[:, :], scalar1=MAGIC,
                                scalar2=None, op0=ALU.subtract)
                e.scalar_tensor_tensor(out=tb[:, :], in0=tb[:, :], scalar=MAGIC,
                                       in1=EV7[:, :], op0=ALU.subtract,
                                       op1=ALU.mult)
                e.reduce_max(T2XM[:, mb:mb + 1], tb[:, :], axis=AX.X,
                             apply_absolute_value=True)
            plan.v(cfb3)
        msp = m - 3
        if 0 <= msp < NI:
            DMA1("scalar", d_t2[msp, :, :], EVT3[msp % 3][:, :])
        plan.bar()

    def t2maxv(e):
        e.reduce_max(SC[:, 0:1], T2XM[:, :], axis=AX.X)
    plan.v(t2maxv)
    plan.bar()
    scalar_from_col(SC[:, 0:1], 0)
    plan.v(lambda e: (e.tensor_copy(SR[0:1, 1:2], SR[0:1, 0:1]), e.drain()))
    plan.bar()
    emit_allgather(3)

    def mkscale5(e):
        e.reduce_max(SR[0:1, 0:1], AG2[:, :, 0:1], axis=AX.XY)
        e.tensor_mul(SR[0:1, 13:14], SR[0:1, 17:18], SR[0:1, 11:12])
        e.tensor_mul(SR[0:1, 14:15], SR[0:1, 0:1], SR[0:1, 13:14])
        e.tensor_scalar(out=SR[0:1, 15:16], in0=SR[0:1, 14:15], scalar1=1.0 / QMAX,
                        scalar2=1e-12, op0=ALU.mult, op1=ALU.max)
        e.reciprocal(SR[0:1, 18:19], SR[0:1, 15:16])
        e.tensor_mul(SR[0:1, 16:17], SR[0:1, 13:14], SR[0:1, 18:19])
    plan.v(mkscale5)
    plan.bar()
    bcast_scalar(15, 9)
    bcast_scalar(16, 10)

    def obf(i):
        if i < 8:
            return OB_A[:, i, :]
        if i < 16:
            return OB_D[:, i - 8, :]
        return OB_E[:, i - 16, :]

    for m in range(NI + 1):
        if m < NI:
            DMA1("sync", (EV0 if m % 2 == 0 else EV1)[:, :], d_t2[m, :, :])
        ma = m - 1
        if 0 <= ma < NI:
            tsrc = EV0 if ma % 2 == 0 else EV1

            def cfo(e, tsrc=tsrc, ma=ma):
                e.tensor_scalar(out=EV2[:, :], in0=tsrc[:, :], scalar1=SC[:, 10:11],
                                scalar2=MAGIC, op0=ALU.mult, op1=ALU.add)
                e.tensor_scalar(out=obf(ma), in0=EV2[:, :], scalar1=MAGIC,
                                scalar2=SC[:, 9:10], op0=ALU.subtract, op1=ALU.mult)
            plan.v(cfo)
        plan.bar()

    for g in range(4):
        for mi in range(NI // 2 + 1):
            if mi < NI // 2:
                wl = (WDR0, WDR1) if mi % 2 == 0 else (WDR2, WDR3)
                DMA1("sync", wl[0][:, :], d_wdT[2 * mi, :, :])
                DMA1("sync", wl[1][:, :], d_wdT[2 * mi + 1, :, :])
            mmi = mi - 1
            if mmi >= 0:
                wl = (WDR0, WDR1) if mmi % 2 == 0 else (WDR2, WDR3)

                def mfd(e, wl=wl, g=g, mmi=mmi):
                    for k in range(2):
                        mm_ = 2 * mmi + k
                        for tt in range(4):
                            gt = 4 * g + tt
                            for hh in range(2):
                                e.matmul(PB[2 * tt + hh][:, 0:512],
                                         obf(mm_)[:, 128 * gt:128 * gt + 128],
                                         wl[k][:, 512 * hh:512 * hh + 512],
                                         start=(mm_ == 0), stop=(mm_ == NI - 1))
                plan.t(mfd)
            plan.bar()
        for tt in range(4):
            gt = 4 * g + tt

            def evd(e, tt=tt):
                e.copy(EVG[:, 0:512], PB[2 * tt][:, 0:512])
                e.copy(EVG[:, 512:1024], PB[2 * tt + 1][:, 0:512])
            plan.s(evd)
            plan.bar()
            DMAACC(d_yt[:, gt, :], EVG[:, :])
            plan.bar()

    if STAGE >= 2:
        DMA1("gpsimd", CCT[:, :], din["cc_tab"])
        DMA1("gpsimd", SST[:, :], din["ss_tab"])
        DMA1("gpsimd", MNEG.ap().rearrange("p a b -> p (a b)"), din["mneg"])
        plan.bar()
        emit_proj_fm(QF, din["l0_wq"], 0, rope=True, qscale=0.125)
        emit_proj_fm(KF, din["l0_wk"], 0, rope=True)
        load_wb_halves(din["l0_wv"], 0)

        def vone(e):
            e.memset(VA[:, :, :, 64:65], 1.0)
        plan.v(vone)
        plan.bar()
        emit_proj_tok_V()
        print("[kernel] bars before L0 attention:", plan.nbars())
        emit_attention(l1=False)
        print("[kernel] bars after L0 attention:", plan.nbars())

        DMA1("gpsimd", WOBR[0:1, :], din["l0_wob"])
        load_wb_halves(din["l0_wo"], 0)
        emit_transpose_tok_to_fm(HT, QF)
        for tt in range(NT + 2):
            if tt < NT:
                pb0, pb1 = PB[(tt % 2) * 2], PB[(tt % 2) * 2 + 1]

                def mfo(e, pb0=pb0, pb1=pb1, tt=tt):
                    for kc in range(NC):
                        e.matmul(pb0[:, 0:512], QF[:, kc, 128 * tt:128 * tt + 128],
                                 WB[:, kc, 0:512], start=(kc == 0), stop=False)
                    e.matmul(pb0[:, 0:512], ONESF[0:1, :], WOBR[0:1, 0:512],
                             start=False, stop=True)
                    for kc in range(NC):
                        e.matmul(pb1[:, 0:512], QF[:, kc, 128 * tt:128 * tt + 128],
                                 WB2[:, kc, 0:512], start=(kc == 0), stop=False)
                    e.matmul(pb1[:, 0:512], ONESF[0:1, :], WOBR[0:1, 512:1024],
                             start=False, stop=True)
                plan.t(mfo)
            te = tt - 1
            if 0 <= te < NT:
                pb0, pb1 = PB[(te % 2) * 2], PB[(te % 2) * 2 + 1]
                evg = EVG if te % 2 == 0 else EVG2

                def evo(e, pb0=pb0, pb1=pb1, evg=evg):
                    e.copy(evg[:, 0:512], pb0[:, 0:512])
                    e.copy(evg[:, 512:1024], pb1[:, 0:512])
                plan.s(evo)
            ta = tt - 2
            if ta >= 0:
                evg = EVG if ta % 2 == 0 else EVG2
                DMAACC(d_yt[:, ta, :], evg[:, :])
            plan.bar()

    if STAGE >= 3:
        emit_layernorm(d_yt)

        load_wb_halves(din["l1_bw_t"], 0)
        DMA1("gpsimd", WBL[0:64, :], din["l1_B"])
        DMA1("gpsimd", WPK.ap(),
             din["l1_pack"].rearrange("(c p) n -> p c n", p=128))
        DMA1("sync", RB16[0:1, 0:16], din["l1_rb16"])
        DMA1("sync", BBR[0:1, :], din["l1_bb"])
        plan.bar()

        for tt in range(NT + 1):
            if tt < NT:
                pb = PB[4 + tt % 2]

                def mfl(e, pb=pb, tt=tt):
                    for kc in range(NC):
                        e.matmul(pb[:, 0:16], XNF[:, kc, 128 * tt:128 * tt + 128],
                                 WPK[:, kc, :], start=(kc == 0), stop=False)
                    e.matmul(pb[:, 0:16], ONESF[0:1, :], RB16[0:1, 0:16],
                             start=False, stop=True)
                plan.t(mfl)
            te = tt - 1
            if te >= 0:
                pbe = PB[4 + te % 2]
                plan.s(lambda e, pbe=pbe, te=te: e.copy(LG[:, te, :], pbe[:, 0:16]))
            plan.bar()

        def vmax_all(e):
            for tt in range(NT):
                e.reduce_max(SM1[:, tt:tt + 1], LG[:, tt, 0:8], axis=AX.X,
                             negate=True)
                e.tensor_scalar(out=M2T[:, tt, :], in0=LG[:, tt, 8:16],
                                scalar1=2.0, scalar2=None, op0=ALU.mult)
        plan.v(vmax_all)
        plan.bar()

        def sexp_all(e):
            for tt in range(NT):
                e.activation(LG[:, tt, 0:8], LG[:, tt, 0:8], AF.Exp,
                             bias=SM1[:, tt:tt + 1], scale=1.0,
                             accum_out=SM2[:, tt:tt + 1])
        plan.s(sexp_all)
        plan.bar()

        def vz_all(e):
            e.reciprocal(SM3[:, :], SM2[:, :])
            for tt in range(NT):
                e.tensor_scalar(out=GT[:, tt, :], in0=LG[:, tt, 0:8],
                                scalar1=SM3[:, tt:tt + 1], scalar2=None,
                                op0=ALU.mult)
            for tt in range(NT):
                e.tensor_mul(
                    EVG.ap()[:, 64 * tt:64 * tt + 64]
                    .rearrange("p (a b) -> p a b", a=8),
                    GT[:, tt, :, None].broadcast_to([128, 8, 8]),
                    M2T[:, tt, None, :].broadcast_to([128, 8, 8]))
        plan.v(vz_all)
        plan.bar()

        def tz_all(e):
            for tt in range(NT):
                e.transpose(PB[4 + tt // 4][0:64, 128 * (tt % 4):128 * (tt % 4) + 128],
                            EVG[:, 64 * tt:64 * tt + 64], IDF[:, :])
        plan.t(tz_all)
        plan.bar()

        def sz_all(e):
            for tt in range(NT):
                e.copy(ZF[0:64, tt, :],
                       PB[4 + tt // 4][0:64, 128 * (tt % 4):128 * (tt % 4) + 128])
        plan.s(sz_all)
        plan.bar()

        for tt in range(NT + 2):
            if tt < NT:
                pb0, pb1 = PB[(tt % 2) * 2], PB[(tt % 2) * 2 + 1]

                def mfb(e, pb0=pb0, pb1=pb1, tt=tt):
                    for kc in range(NC):
                        e.matmul(pb0[:, 0:512], XNF[:, kc, 128 * tt:128 * tt + 128],
                                 WB[:, kc, 0:512], start=(kc == 0), stop=False)
                    e.matmul(pb0[:, 0:512], ZF[0:64, tt, :], WBL[0:64, 0:512],
                             start=False, stop=False)
                    e.matmul(pb0[:, 0:512], ONESF[0:1, :], BBR[0:1, 0:512],
                             start=False, stop=True)
                    for kc in range(NC):
                        e.matmul(pb1[:, 0:512], XNF[:, kc, 128 * tt:128 * tt + 128],
                                 WB2[:, kc, 0:512], start=(kc == 0), stop=False)
                    e.matmul(pb1[:, 0:512], ZF[0:64, tt, :], WBL[0:64, 512:1024],
                             start=False, stop=False)
                    e.matmul(pb1[:, 0:512], ONESF[0:1, :], BBR[0:1, 512:1024],
                             start=False, stop=True)
                plan.t(mfb)
            te = tt - 1
            if 0 <= te < NT:
                pb0, pb1 = PB[(te % 2) * 2], PB[(te % 2) * 2 + 1]
                evg = EVG if te % 2 == 0 else EVG2

                def evb(e, pb0=pb0, pb1=pb1, evg=evg):
                    e.copy(evg[:, 0:512], pb0[:, 0:512])
                    e.copy(evg[:, 512:1024], pb1[:, 0:512])
                plan.s(evb)
            ta = tt - 2
            if ta >= 0:
                evg = EVG if ta % 2 == 0 else EVG2
                DMAACC(d_yt[:, ta, :], evg[:, :])
            plan.bar()

    if STAGE >= 4:
        DMA1("gpsimd", ONES2[:, :], din["ones2"])
        plan.bar()
        emit_proj_fm(QF, din["l1_qkv"], 0, rope=False, qscale=0.125)
        load_wb_halves(din["l1_qkv"], 1024)
        plan.bar()
        emit_proj_tok_KT()
        print("[kernel] bars after KT:", plan.nbars())
        emit_proj_tok_KTS()
        print("[kernel] bars after KTS:", plan.nbars())
        emit_attn_l1s_ubound()
        print("[kernel] bars after ubound:", plan.nbars())
        DMA1("gpsimd", MNEG.ap().rearrange("p a b -> p (a b)"), din["mneg"])
        load_wb_halves(din["l1_qkv"], 2048)

        def vone2(e):
            e.memset(VA[:, :, :, 64:65], 1.0)
        plan.v(vone2)
        plan.bar()
        emit_proj_tok_V()
        print("[kernel] bars after V:", plan.nbars())
        emit_attention_l1s()

    plan.bar()

    bar_limit = int(os.environ.get("MK_TRUNC", "999999"))

    def _ap_range(ap):
        try:
            t = ap.tensor
            off = ap.offset
            dims = list(ap.ap)
            if len(dims) >= 2:
                pitch, pcount = dims[0]
                if pitch <= 0:
                    pitch = 1
                p0 = off // pitch
                f0 = off % pitch
                span = 0
                for step, count in dims[1:]:
                    if count > 1:
                        span += abs(step) * (count - 1)
                return (t.name, p0, p0 + pcount, f0, f0 + span + 1)
            span = 0
            for step, count in dims:
                if count > 1:
                    span += abs(step) * (count - 1)
            return (t.name, 0, 1 << 30, off, off + span + 1)
        except Exception:
            return None

    _DRAINSTAT = [0, 0, 0]

    def _is_ap(x):
        return hasattr(x, "tensor") and hasattr(x, "ap")

    class AutoDrain:
        _PASS = ("wait_ge", "nop", "sem_inc", "dma_start",
                 "dma_start_transpose", "collective_compute", "load_library")

        def __init__(self, e):
            object.__setattr__(self, "_e", e)
            object.__setattr__(self, "_hr", [])
            object.__setattr__(self, "_hw", [])

        def drain(self):
            self._hr.clear()
            self._hw.clear()
            return self._e.drain()

        def _emit(self, name, a, args, kw):
            e = self._e
            aps = []
            for i, x in enumerate(args):
                if _is_ap(x):
                    aps.append(("out" if i == 0 else "in", x))
            for k, x in kw.items():
                if _is_ap(x):
                    kind = "out" if (k.startswith("out") or k == "accum_out"
                                     or k == "dest") else "in"
                    aps.append((kind, x))
            writes = [r for r in (_ap_range(x) for k, x in aps if k == "out") if r]
            reads = [r for r in (_ap_range(x) for k, x in aps if k == "in") if r]
            if name == "memset":
                writes, reads = writes + reads, []

            def overlap(r1, r2):
                return (r1[0] == r2[0] and r1[1] < r2[2] and r2[1] < r1[2]
                        and r1[3] < r2[4] and r2[3] < r1[4])

            hr, hw = self._hr, self._hw
            conflict = (any(overlap(w, p) for w in writes for p in hr)
                        or any(overlap(w, p) for w in writes for p in hw)
                        or any(overlap(r, p) for r in reads for p in hw))
            if conflict:
                e.drain()
                _DRAINSTAT[0] += 1
                hr.clear()
                hw.clear()
            _DRAINSTAT[1] += 1
            _DRAINSTAT[2] += len(writes) + len(reads)
            res = a(*args, **kw)
            hr.extend(reads)
            hw.extend(writes)
            return res

        def __getattr__(self, n):
            a = getattr(self._e, n)
            if callable(a) and n not in self._PASS and not n.startswith("_"):
                def w(*args, _n=n, _a=a, **kw):
                    return self._emit(_n, _a, args, kw)
                return w
            return a

    def make_body(eng_name):
        def body(e):
            if eng_name in ("vector", "scalar", "gpsimd"):
                e = AutoDrain(e)
            dma_count = 0
            cc_count = 0
            bar_i = 0
            my_dma_sem = dma_sems.get(eng_name)
            for it in plan.items:
                if bar_i >= bar_limit:
                    break
                if it[0] == "op":
                    _, oeng, fn = it
                    if oeng != eng_name:
                        continue
                    if isinstance(fn, tuple):
                        kind, f = fn
                        if kind == "DMA":
                            dma_count += f(e, my_dma_sem)
                        else:
                            f(e).then_inc(cc_sem, 1)
                            cc_count += 1
                    else:
                        fn(e)
                else:
                    bar_i += 1
                    if my_dma_sem is not None and dma_count > 0:
                        e.wait_ge(my_dma_sem, dma_count * 16)
                    if eng_name == "gpsimd" and cc_count > 0:
                        e.wait_ge(cc_sem, cc_count)
                    e.drain().then_inc(phase_sems[eng_name], 1)
                    for other in ENGINES:
                        if other != eng_name:
                            e.wait_ge(phase_sems[other], bar_i)
        return body

    nbars = sum(1 for it in plan.items if it[0] == "bar")
    print(f"[kernel] plan: {len(plan.items)} items, {nbars} barriers")
    with nc.Block() as block:
        block.sync(make_body("sync"))
        block.tensor(make_body("tensor"))
        block.vector(make_body("vector"))
        block.scalar(make_body("scalar"))
        block.gpsimd(make_body("gpsimd"))
    st.close()
    print("[kernel] proxy: drains", _DRAINSTAT[0], "ops", _DRAINSTAT[1],
          "aps", _DRAINSTAT[2])
    return nc


_cache = {}


def kernel(**inputs):
    cst = host_constants()
    wd = {
        "l0_wq": inputs["L0_attn_wq"],
        "l0_wk": inputs["L0_attn_wk"],
        "l0_wv": inputs["L0_attn_wv"],
        "l0_wo": inputs["L0_attn_wo_w"],
        "l0_wob": inputs["L0_attn_wo_b"],
        "l0_wg": inputs["L0_ffn_w_gate"],
        "l0_wu": inputs["L0_ffn_w_up"],
        "l0_wd": inputs["L0_ffn_w_down"],
        "l1_qkv": inputs["L1_attn_qkv_weights"],
        "l1_bw_t": np.asarray(inputs["L1_ffn_base_w"]).T,
        "l1_bb": inputs["L1_ffn_base_b"],
        "l1_pack": np.concatenate([np.asarray(inputs["L1_ffn_router_w"]).T,
                                   np.asarray(inputs["L1_ffn_lora_A"])], axis=1),
        "l1_rb16": np.concatenate([np.asarray(inputs["L1_ffn_router_b"]),
                                   np.zeros(8, np.float32)]),
        "l1_B": np.asarray(inputs["L1_ffn_lora_B"]).reshape(64, D),
    }
    wd = {k: np.ascontiguousarray(np.asarray(v, dtype=np.float32))
          for k, v in wd.items()}
    x_full = np.ascontiguousarray(np.asarray(inputs["L0_attn_x"], dtype=np.float32))

    if "nc" not in _cache:
        _cache["nc"] = build_nc()
    nc = _cache["nc"]

    in_maps = []
    for b in range(B):
        m = {"x": np.ascontiguousarray(x_full[b])}
        m.update(wd)
        m.update(cst)
        in_maps.append(m)
    res = run_bass_kernel_spmd(nc, in_maps, core_ids=list(range(8)))
    out = np.stack([res.results[b]["y"] for b in range(B)], axis=0)
    return out.astype(np.float32)


# version 26 (final)
# speedup vs baseline: 1.1599x; 1.0279x over previous
import numpy as np
from contextlib import ExitStack

import concourse.bass as bass
import concourse.mybir as mybir
from concourse.bass_utils import run_bass_kernel_spmd

F32 = mybir.dt.float32
BF16 = mybir.dt.bfloat16
I16 = mybir.dt.int16
ALU = mybir.AluOpType
AF = mybir.ActivationFunctionType
AX = mybir.AxisListType

B, S, D = 8, 2048, 1024
H, HD = 16, 64
I3 = 3072
NT = S // 128
NC = D // 128
NI = I3 // 128
MAGIC = 12582912.0
QMAX = 32767.0
EPS = 1e-5

import os
STAGE = int(os.environ.get("MK_STAGE", "4"))
SUBST = float(os.environ.get("MK_SUBST", "99"))


def host_constants():
    c = {}
    inv = 1.0 / (10000.0 ** (np.arange(0, HD, 2, dtype=np.float64) / HD))
    ang = np.outer(np.arange(S, dtype=np.float64), inv)
    cosv, sinv = np.cos(ang), np.sin(ang)
    cc = np.zeros((128, S), dtype=np.float64)
    ss = np.zeros((128, S), dtype=np.float64)
    for p in range(128):
        i = (p % 64) // 2
        cc[p] = cosv[:, i]
        ss[p] = sinv[:, i] * (1.0 if (p % 2) else -1.0)
    c["cc_tab"] = cc.astype(np.float32)
    c["ss_tab"] = ss.astype(np.float32)
    c["ident"] = np.eye(128, dtype=np.float32)
    psw = np.zeros((128, 128), dtype=np.float32)
    for m in range(128):
        psw[m ^ 1, m] = 1.0
    c["pswap"] = psw
    mn = np.zeros((128, 4, 512), dtype=np.float32)
    for d in range(4):
        for p in range(128):
            mn[p, d, : min(512, p + 128 * d)] = -1.0e9
    c["mneg"] = mn.reshape(128, 4 * 512)
    oneh = np.zeros((128, NC, 16), dtype=np.float32)
    for ch in range(NC):
        oneh[0:64, ch, 2 * ch] = 1.0
        oneh[64:128, ch, 2 * ch + 1] = 1.0
    c["oneh"] = oneh.reshape(128, NC * 16)
    mr = np.zeros((64, 544), dtype=np.float32)
    for kk in range(32):
        mr[kk, : 32 * (kk // 2) + (kk % 2) + 30] = -1.0e9
    mr[32, :] = -1.0e9
    c["mskr"] = np.ascontiguousarray(np.vstack([mr] * 2))
    ml = np.zeros((64, 4 * 128), dtype=np.float32)
    for jq in range(4):
        for j in range(128):
            mm = j % 64 - 16 * jq
            if 0 <= mm < 16:
                ml[2 * mm + (j // 64), 128 * jq + j] = 1.0
            elif mm >= 16:
                ml[32, 128 * jq + j] = 1.0
    c["mskl"] = np.ascontiguousarray(np.vstack([ml] * 2))
    o2 = np.zeros((128, 2), dtype=np.float32)
    o2[0:64, 0] = 1.0
    o2[64:128, 1] = 1.0
    c["ones2"] = o2
    return c


ENGINES = ("sync", "tensor", "vector", "scalar", "gpsimd")


class Plan:
    def __init__(self):
        self.items = []

    def op(self, eng, fn):
        self.items.append(("op", eng, fn))

    def t(self, fn): self.op("tensor", fn)
    def v(self, fn): self.op("vector", fn)
    def s(self, fn): self.op("scalar", fn)
    def g(self, fn): self.op("gpsimd", fn)

    def dma(self, q, fn):
        self.op(q, ("DMA", fn))

    def coll(self, fn):
        self.op("gpsimd", ("COLL", fn))

    def bar(self):
        import sys as _s
        f = _s._getframe(1)
        self.items.append(("bar", f"{f.f_lineno}"))

    def nbars(self):
        return sum(1 for it in self.items if it[0] == "bar")

    def ndma(self, q):
        return sum(1 for it in self.items
                   if it[0] == "op" and it[1] == q
                   and isinstance(it[2], tuple) and it[2][0] == "DMA")


def build_nc():
    nc = bass.Bass("TRN2", target_bir_lowering=False)
    st = ExitStack()

    d_x = nc.dram_tensor("x", [S, D], F32, kind="ExternalInput").ap()
    d_y = nc.dram_tensor("y", [S, D], F32, kind="ExternalOutput").ap()
    d_yt = d_y.rearrange("(t p) d -> p t d", p=128)

    din = {}
    for nm, shp in [
        ("l0_wq", [D, D]), ("l0_wk", [D, D]), ("l0_wv", [D, D]),
        ("l0_wo", [D, D]), ("l0_wob", [D]),
        ("l0_wg", [I3, D]), ("l0_wu", [I3, D]), ("l0_wd", [D, I3]),
        ("l1_qkv", [D, 3 * D]),
        ("l1_bw_t", [D, D]), ("l1_bb", [D]),
        ("l1_pack", [D, 16]), ("l1_rb16", [16]), ("l1_B", [64, D]),
        ("cc_tab", [128, S]), ("ss_tab", [128, S]),
        ("ident", [128, 128]), ("pswap", [128, 128]),
        ("mneg", [128, 4 * 512]), ("oneh", [128, NC * 16]),
        ("mskr", [128, 544]), ("mskl", [128, 4 * 128]),
        ("ones2", [128, 2]),
    ]:
        din[nm] = nc.dram_tensor(nm, shp, F32, kind="ExternalInput").ap()

    d_up = nc.dram_tensor("up_sp", [NI, 128, S], BF16).ap()
    d_gate = nc.dram_tensor("gate_sp", [NI, 128, S], BF16).ap()
    d_t = nc.dram_tensor("t_sp", [NI, 128, S], F32).ap()
    d_t2 = nc.dram_tensor("t2_sp", [NI, 128, S], F32).ap()
    d_wdT = nc.dram_tensor("wdT_sp", [NI, 128, D], BF16).ap()
    d_agi = [nc.dram_tensor(f"agi{k}", [1, 2], F32).ap() for k in range(4)]
    d_ago = [nc.dram_tensor(f"ago{k}", [8, 2], F32).ap() for k in range(4)]

    SB_BASE = 0x4400

    def at(name, off, shape, dtype):
        return nc.alloc_sbuf_tensor_at(name, shape, dtype, offset=SB_BASE + off)

    A0, B0, C0, D0, E0 = 0x0000, 0x8000, 0x10000, 0x18000, 0x20000
    F0, G0, H0, H1 = 0x28200, 0x2C200, 0x2E200, 0x30200
    K0 = 0x32200

    XNT = at("XNT", A0, [128, NT, 1024], BF16)
    OB_A = at("OB_A", A0, [128, 8, S], BF16)
    CCT = at("CCT", A0, [128, S], BF16)
    SST = at("SST", A0 + 0x1000, [128, S], BF16)
    SQS = at("SQS", A0 + 0x2000, [128, NC, 512], BF16)
    ROPET = at("ROPET", A0 + 0x4000, [128, 4, 512], BF16)
    ROPET2 = at("ROPET2", A0 + 0x5000, [128, 4, 512], BF16)
    SQS2 = at("SQS2", A0 + 0x6000, [128, NC, 512], BF16)
    HT = at("HT", A0, [128, NT, 1024], BF16)
    KTTA = at("KTTA", A0, [128, 8, 2048], BF16)
    XNF = at("XNF", B0, [128, NC, S], BF16)
    HOUT = at("HOUT", B0, [128, NT, 1024], BF16)
    LNX = at("LNX", C0, [128, NT, 1024], F32)
    QF = at("QF", C0, [128, NC, S], BF16)
    EV0 = at("EV0", C0 + 0x0000, [128, S], F32)
    EV1 = at("EV1", C0 + 0x2000, [128, S], F32)
    EV2 = at("EV2", C0 + 0x4000, [128, S], F32)
    EB0 = at("EB0", C0 + 0x6000, [128, S], BF16)
    EB1 = at("EB1", C0 + 0x7000, [128, S], BF16)
    EV5 = at("EV5", A0 + 0x0000, [128, S], F32)
    EV6 = at("EV6", A0 + 0x2000, [128, S], F32)
    EV7 = at("EV7", A0 + 0x4000, [128, S], F32)
    KF = at("KF", D0, [128, NC, S], BF16)
    KTTD = at("KTTD", D0, [128, 8, 2048], BF16)
    XQF = at("XQF", D0, [128, NC, S], BF16)
    EV3 = at("EV3", D0 + 0x0000, [128, S], F32)
    EV4 = at("EV4", D0 + 0x2000, [128, S], F32)
    GQ = [at(f"GQ{i}", D0 + 0x4000 + i * 0x1000, [128, S], I16) for i in range(3)]
    OB_D = at("OB_D", D0, [128, 8, S], BF16)
    VA = at("VA", E0, [128, NT, H, 65], BF16)
    OB_E = at("OB_E", E0, [128, 8, S], BF16)
    ZF = at("ZF", E0 + 0x4000, [64, NT, 128], BF16)
    WBL = at("WBL", E0 + 0x5000, [64, 1024], BF16)
    WUF0 = at("WUF0", F0, [128, 1024], F32)
    WUF1 = at("WUF1", F0 + 0x1000, [128, 1024], F32)
    WQT = at("WQT", F0 + 0x2000, [128, 1024], F32)
    WUB0 = at("WUB0", F0 + 0x3000, [128, 1024], BF16)
    WUB1 = at("WUB1", F0 + 0x3800, [128, 1024], BF16)
    WOBR = at("WOBR", F0, [1, 1024], F32)
    BBR = at("BBR", F0 + 0x1000, [1, 1024], F32)
    MNEG = at("MNEG", F0 + 0x2000, [128, 4, 512], BF16)
    PTS = at("PTS", F0 + 0x3000, [128, 4, 512], BF16)
    LG = at("LG", F0 + 0x3800, [128, NT, 16], F32)
    GT = at("GT", F0 + 0x3C00, [128, NT, 8], BF16)
    M2T = at("M2T", F0 + 0x3D00, [128, NT, 8], F32)
    SQKQ = at("SQKQ", F0, [128, NC, 512], BF16)
    SQKQ2 = at("SQKQ2", F0 + 0x2000, [128, NC, 512], BF16)
    WLT0 = at("WLT0", G0, [128, 8, 128], BF16)
    WLT1 = at("WLT1", G0 + 0x800, [128, 8, 128], BF16)
    WDR0 = at("WDR0", G0, [128, 1024], BF16)
    WDR1 = at("WDR1", G0 + 0x800, [128, 1024], BF16)
    EVG = at("EVG", G0 + 0x1000, [128, 1024], F32)
    EVG2 = at("EVG2", G0 + 0x0000, [128, 1024], F32)
    WB = at("WB", H0, [128, NC, 512], BF16)
    WB2 = at("WB2", H1, [128, NC, 512], BF16)
    WDR2 = at("WDR2", H0, [128, 1024], BF16)
    WDR3 = at("WDR3", H0 + 0x800, [128, 1024], BF16)
    IDB = at("IDB", K0, [128, 128], BF16)
    IDF = at("IDF", K0 + 0x100, [128, 128], F32)
    PSW = at("PSW", K0 + 0x300, [128, 128], BF16)
    ONESF = at("ONESF", K0 + 0x400, [1, 128], F32)
    ONEHS = at("ONEHS", K0 + 0x600, [128, NC, 16], BF16)
    WPK = at("WPK", K0 + 0x700, [128, NC, 16], BF16)
    RB16 = at("RB16", K0 + 0x800, [1, 16], F32)
    SC = at("SC", K0 + 0x880, [128, 32], F32)
    SR = at("SR", K0 + 0x900, [1, 64], F32)
    SX = at("SX", K0 + 0xA00, [128, 16], F32)
    SX2 = at("SX2", K0 + 0xA40, [128, 16], F32)
    SM1 = at("SM1", K0 + 0xA80, [128, 16], F32)
    SM2 = at("SM2", K0 + 0xAC0, [128, 16], F32)
    SM3 = at("SM3", K0 + 0xB00, [128, 16], F32)
    SM4 = at("SM4", K0 + 0xB40, [128, 16], F32)
    UXM = at("UXM", K0 + 0xB80, [128, 24], F32)
    GXM = at("GXM", K0 + 0xBE0, [128, 24], F32)
    TXM = at("TXM", K0 + 0xC40, [128, 24], F32)
    T2XM = at("T2XM", K0 + 0xCA0, [128, 24], F32)
    AGR = at("AGR", K0 + 0xD00, [1, 16], F32)
    RCP = at("RCP", K0 + 0xD40, [128, 8], F32)
    MSKR = at("MSKR", K0 + 0xD60, [128, 544], BF16)
    MSKL = at("MSKL", K0 + 0x1460, [128, 4 * 128], BF16)
    ONES2 = at("ONES2", K0 + 0x1400, [128, 2], BF16)
    K2M = at("K2M", K0 + 0x1420, [128, 16], F32)
    assert 0x4400 + K0 + 0x18A0 <= 0x38000

    PB = [st.enter_context(nc.psum_tensor(f"PB{i}", [128, 512], F32))
          for i in range(8)]
    PBB = [pb.bitcast(BF16) for pb in PB]

    phase_sems = {e: st.enter_context(nc.semaphore(f"ph_{e}")) for e in ENGINES}
    dma_sems = {q: st.enter_context(nc.semaphore(f"dq_{q}"))
                for q in ("sync", "gpsimd", "scalar")}
    cc_sem = st.enter_context(nc.semaphore("ccs"))
    sem_sc = st.enter_context(nc.semaphore("sc_sc"))
    sem_ex = st.enter_context(nc.semaphore("sc_ex"))
    sem_av = st.enter_context(nc.semaphore("sc_av"))
    sem_ev = st.enter_context(nc.semaphore("sc_ev"))
    sem_dd = st.enter_context(nc.semaphore("sc_dd"))

    plan = Plan()

    def DMA1(q, out, in_):
        def f(e, sem, out=out, in_=in_):
            e.dma_start(out=out, in_=in_).then_inc(sem, 16)
            return 1
        plan.dma(q, f)

    def DMAT(q, out, in_):
        def f(e, sem, out=out, in_=in_):
            e.dma_start_transpose(out, in_).then_inc(sem, 16)
            return 1
        plan.dma(q, f)

    def DMAACC(out, in_, sem2=None, q="gpsimd"):
        def f(e, sem, out=out, in_=in_, sem2=sem2):
            if sem2 is not None:
                e.dma_start(out=out, in_=in_, accum_op=ALU.add).then_inc(sem2, 16)
                return 0
            e.dma_start(out=out, in_=in_, accum_op=ALU.add).then_inc(sem, 16)
            return 1
        plan.dma(q, f)

    def scalar_from_col(col_ap, dst_idx):
        plan.t(lambda e, c=col_ap: e.transpose(PB[6][0:1, 0:128], c, IDF[:, :]))
        plan.bar()

        def vf(e, i=dst_idx):
            e.reduce_max(SR[0:1, i:i + 1], PB[6][0:1, 0:128], axis=AX.X)
        plan.v(vf)
        plan.bar()

    def bcast_scalar(src_idx, dst_col):
        plan.t(lambda e, i=src_idx: e.matmul(
            PB[6][:, 0:1], ONESF[0:1, :], SR[0:1, i:i + 1], start=True, stop=True))
        plan.bar()

        def vf(e, d=dst_col):
            e.tensor_copy(SC[:, d:d + 1], PB[6][:, 0:1])
        plan.v(vf)
        plan.bar()

    def emit_allgather(k):
        DMA1("gpsimd", d_agi[k], SR[0:1, 0:2])
        plan.bar()
        plan.coll(lambda e, k=k: e.collective_compute(
            "AllGather", ALU.bypass, replica_groups=[list(range(8))],
            ins=[d_agi[k]], outs=[d_ago[k]]))
        plan.bar()
        DMA1("gpsimd", AGR[0:1, 0:16], d_ago[k])
        plan.bar()

    AG2 = AGR.ap()[0:1, 0:16].rearrange("a (r v) -> a r v", v=2)

    DMA1("gpsimd", IDB[:, :], din["ident"])
    DMA1("sync", IDF[:, :], din["ident"])
    DMA1("gpsimd", PSW[:, :], din["pswap"])
    DMA1("gpsimd", ONEHS.ap().rearrange("p c m -> p (c m)"), din["oneh"])

    def initc(e):
        e.memset(ONESF[:, :], 1.0)
        e.memset(SR[:, :], 0.0)
    plan.v(initc)
    plan.bar()

    def emit_transpose_tok_to_fm(src_tok, dst_fm):
        for w in range(NT + 1):
            if w < NT:
                pbb = PBB[w % 2]

                def tw(e, w=w, pbb=pbb):
                    for c in range(NC):
                        e.transpose(pbb[:, 128 * c:128 * c + 128],
                                    src_tok[:, w, 128 * c:128 * c + 128],
                                    IDB[:, :])
                plan.t(tw)
            wv = w - 1
            if wv >= 0:
                pbb = PBB[wv % 2]

                def ew(e, wv=wv, pbb=pbb):
                    for c in range(NC):
                        e.copy(dst_fm[:, c, 128 * wv:128 * wv + 128],
                               pbb[:, 128 * c:128 * c + 128])
                plan.s(ew)
            plan.bar()

    def emit_layernorm(src_tiled, copy_y=False):
        for t in range(NT):
            DMA1("sync" if t % 2 == 0 else "gpsimd", LNX[:, t, :], src_tiled[:, t, :])
        plan.bar()
        if copy_y:
            for t in range(NT):
                DMA1("gpsimd" if t % 2 == 0 else "scalar",
                     d_yt[:, t, :], LNX[:, t, :])

        def stats(e):
            for t in range(NT):
                e.reduce_sum(SX[:, t:t + 1], LNX[:, t, :], axis=AX.X)
                e.tensor_mul(EVG[:, :], LNX[:, t, :], LNX[:, t, :])
                e.reduce_sum(SX2[:, t:t + 1], EVG[:, :], axis=AX.X)
        plan.v(stats)
        plan.bar()

        def smalls(e):
            e.tensor_scalar(out=SM1[:, :], in0=SX[:, :], scalar1=1.0 / D,
                            scalar2=None, op0=ALU.mult)
            e.tensor_scalar(out=SM2[:, :], in0=SX2[:, :], scalar1=1.0 / D,
                            scalar2=None, op0=ALU.mult)
            e.tensor_mul(SM3[:, :], SM1[:, :], SM1[:, :])
            e.tensor_sub(SM2[:, :], SM2[:, :], SM3[:, :])
            e.tensor_scalar(out=SM2[:, :], in0=SM2[:, :], scalar1=EPS,
                            scalar2=None, op0=ALU.add)
        plan.v(smalls)
        plan.bar()
        plan.s(lambda e: e.activation(SM3[:, :], SM2[:, :], AF.Sqrt))
        plan.bar()

        def newton(e):
            e.reciprocal(SM4[:, :], SM3[:, :])
            e.tensor_mul(SM3[:, :], SM4[:, :], SM4[:, :])
            e.tensor_mul(SM3[:, :], SM3[:, :], SM2[:, :])
            e.tensor_scalar(out=SM3[:, :], in0=SM3[:, :], scalar1=-0.5,
                            scalar2=1.5, op0=ALU.mult, op1=ALU.add)
            e.tensor_mul(SM4[:, :], SM4[:, :], SM3[:, :])
        plan.v(newton)
        plan.bar()

        def apply(e):
            for t in range(NT):
                e.tensor_scalar(out=XNT[:, t, :], in0=LNX[:, t, :],
                                scalar1=SM1[:, t:t + 1], scalar2=SM4[:, t:t + 1],
                                op0=ALU.subtract, op1=ALU.mult)
        plan.v(apply)
        plan.bar()
        emit_transpose_tok_to_fm(XNT, XNF)

    def dequant_v(e, wsrc, wdst):
        w3 = wsrc.ap().rearrange("p (b k) -> p b k", k=32)
        e.reduce_max(EVG[:, 0:32], w3, axis=AX.X, apply_absolute_value=True)
        e.tensor_scalar(out=EVG[:, 32:64], in0=EVG[:, 0:32], scalar1=1.0 / 7.0,
                        scalar2=1e-12, op0=ALU.mult, op1=ALU.max)
        e.reciprocal(EVG[:, 64:96], EVG[:, 32:64])
        e.tensor_mul(WQT.ap().rearrange("p (b k) -> p b k", k=32),
                     w3, EVG.ap()[:, 64:96, None].broadcast_to([128, 32, 32]))
        e.tensor_scalar(out=WQT[:, :], in0=WQT[:, :], scalar1=MAGIC,
                        scalar2=MAGIC, op0=ALU.add, op1=ALU.subtract)
        e.tensor_mul(wdst.ap().rearrange("p (b k) -> p b k", k=32),
                     WQT.ap().rearrange("p (b k) -> p b k", k=32),
                     EVG.ap()[:, 32:64, None].broadcast_to([128, 32, 32]))

    def emit_upgate(w_dram, spill, maxcol):
        for m in range(NI + 5):
            if m < NI:
                DMA1("sync", (WUF0 if m % 2 == 0 else WUF1)[:, :],
                     w_dram[128 * m:128 * m + 128, :])
            md = m - 1
            if 0 <= md < NI:
                plan.v(lambda e, md=md: dequant_v(
                    e, WUF0 if md % 2 == 0 else WUF1,
                    WUB0 if md % 2 == 0 else WUB1))
            mt = m - 2
            if 0 <= mt < NI:
                wsrc = WUB0 if mt % 2 == 0 else WUB1
                wdst = WLT0 if mt % 2 == 0 else WLT1
                for c in range(NC):
                    DMAT("scalar", wdst[:, c, :], wsrc[:, 128 * c:128 * c + 128])
            mm_ = m - 3
            if 0 <= mm_ < NI:
                wsrc = WLT0 if mm_ % 2 == 0 else WLT1
                pbase = (mm_ % 2) * 4

                def mf(e, wsrc=wsrc, pbase=pbase):
                    for nt in range(4):
                        for kc in range(NC):
                            e.matmul(PB[pbase + nt][:, 0:512], wsrc[:, kc, :],
                                     XQF[:, kc, 512 * nt:512 * nt + 512],
                                     start=(kc == 0), stop=(kc == NC - 1))
                if not os.environ.get("MK_NOMM"):
                    plan.t(mf)
            me = m - 4
            if 0 <= me < NI:
                pbase = (me % 2) * 4
                ebuf = EB0 if me % 2 == 0 else EB1

                def evf(e, pbase=pbase, ebuf=ebuf):
                    for nt in range(4):
                        e.copy(ebuf[:, 512 * nt:512 * nt + 512],
                               PB[pbase + nt][:, 0:512])
                def evfv(e, pbase=pbase, ebuf=ebuf):
                    for nt in range(4):
                        e.tensor_copy(ebuf[:, 512 * nt:512 * nt + 512],
                                      PB[pbase + nt][:, 0:512])
                if os.environ.get("MK_EVACS"):
                    plan.s(evf)
                elif not os.environ.get("MK_NOEVAC"):
                    plan.v(evfv)
            ms = m - 5
            if 0 <= ms < NI:
                ebuf = EB0 if ms % 2 == 0 else EB1

                def rf(e, ebuf=ebuf, ms=ms, maxcol=maxcol):
                    e.reduce_max(maxcol[:, ms:ms + 1], ebuf[:, :], axis=AX.X,
                                 apply_absolute_value=True)
                if not os.environ.get("MK_NORF"):
                    plan.v(rf)
                if not os.environ.get("MK_NOSPILL"):
                    DMA1("gpsimd", spill[ms, :, :], ebuf[:, :])
            plan.bar()

    def emit_wd_dequant():
        for m in range(NI + 5):
            if m < NI:
                for r in range(8):
                    DMA1("sync",
                         (WUF0 if m % 2 == 0 else WUF1)[:, 128 * r:128 * r + 128],
                         din["l0_wd"][128 * r:128 * r + 128, 128 * m:128 * m + 128])
            md = m - 1
            if 0 <= md < NI:
                plan.v(lambda e, md=md: dequant_v(
                    e, WUF0 if md % 2 == 0 else WUF1,
                    WUB0 if md % 2 == 0 else WUB1))
            mt = m - 2
            if 0 <= mt < NI:
                wsrc = WUB0 if mt % 2 == 0 else WUB1

                def twd(e, wsrc=wsrc, mt=mt):
                    pbb = PBB[mt % 2]
                    for r in range(8):
                        e.transpose(pbb[:, 128 * r:128 * r + 128],
                                    wsrc[:, 128 * r:128 * r + 128], IDB[:, :])
                plan.t(twd)
            mw = m - 3
            if 0 <= mw < NI:
                wdst = WLT0 if mw % 2 == 0 else WLT1

                def ewd(e, wdst=wdst, mw=mw):
                    pbb = PBB[mw % 2]
                    for r in range(8):
                        e.copy(wdst[:, r, :], pbb[:, 128 * r:128 * r + 128])
                plan.s(ewd)
            msv = m - 4
            if 0 <= msv < NI:
                wsrc = WLT0 if msv % 2 == 0 else WLT1
                DMA1("gpsimd", d_wdT[msv, :, :],
                     wsrc.ap().rearrange("p r k -> p (r k)"))
            plan.bar()

    attn_cnt = {"sc": 0, "ex": 0}

    def emit_attn_l0_ubound():
        for gi in range(10):
            def vup(e, gi=gi):
                if gi < 8:
                    qsrc = QF if gi < 4 else KF
                    nt = gi % 4
                    sq = SQS if gi % 2 == 0 else SQS2
                    for c in range(NC):
                        e.tensor_mul(sq[:, c, :],
                                     qsrc[:, c, 512 * nt:512 * nt + 512],
                                     qsrc[:, c, 512 * nt:512 * nt + 512])
                gr = gi - 2
                if gr >= 0:
                    e.reduce_max(SX[0:16, gr:gr + 1],
                                 PB[4 if gr % 2 == 0 else 5][0:16, 0:512],
                                 axis=AX.X)
            plan.v(vup)
            gm = gi - 1
            if 0 <= gm < 8:
                def msq(e, gm=gm):
                    sq = SQS if gm % 2 == 0 else SQS2
                    pb = PB[4 if gm % 2 == 0 else 5]
                    for c in range(NC):
                        e.matmul(pb[0:16, 0:512], ONEHS[:, c, :], sq[:, c, :],
                                 start=(c == 0), stop=(c == NC - 1))
                plan.t(msq)
            plan.bar()

        def vredu(e):
            e.reduce_max(SC[0:16, 20:21], SX[0:16, 0:4], axis=AX.X)
            e.reduce_max(SC[0:16, 21:22], SX[0:16, 4:8], axis=AX.X)
            e.tensor_mul(SC[0:16, 22:23], SC[0:16, 20:21], SC[0:16, 21:22])
        plan.v(vredu)
        plan.bar()
        plan.t(lambda e: e.transpose(PB[6][0:1, 0:16], SC[0:16, 22:23],
                                     IDF[0:16, 0:16]))
        plan.bar()

        def vU(e):
            e.reduce_max(SR[0:1, 20:21], PB[6][0:1, 0:16], axis=AX.X)
        plan.v(vU)
        plan.bar()
        plan.s(lambda e: e.activation(SR[0:1, 21:22], SR[0:1, 20:21], AF.Sqrt))
        plan.bar()

        def vnU(e):
            e.tensor_scalar(out=SR[0:1, 22:23], in0=SR[0:1, 21:22],
                            scalar1=-0.125, scalar2=None, op0=ALU.mult)
        plan.v(vnU)
        plan.bar()
        bcast_scalar(22, 23)

    def emit_proj_fm(dst_fm, w_ap, col0, rope, qscale=None):
        for half in range(2):
            DMA1("gpsimd", (WB if half == 0 else WB2).ap(),
                 w_ap[:, col0 + 512 * half: col0 + 512 * (half + 1)]
                 .rearrange("(c p) n -> p c n", p=128))
        plan.bar()
        for gmc in range(9):
            if gmc < 8:
                wb = WB if gmc < 4 else WB2
                mc = gmc % 4
                pbase = (gmc % 2) * 4

                def mf(e, wb=wb, mc=mc, pbase=pbase):
                    for nt in range(4):
                        for kc in range(NC):
                            e.matmul(PB[pbase + nt][:, 0:512],
                                     wb[:, kc, 128 * mc:128 * mc + 128],
                                     XNF[:, kc, 512 * nt:512 * nt + 512],
                                     start=(kc == 0), stop=(kc == NC - 1))
                plan.t(mf)
            ge = gmc - 1
            if ge >= 0:
                pbase = (ge % 2) * 4

                def evf(e, pbase=pbase, ge=ge, dst=dst_fm, qs=qscale):
                    for nt in range(4):
                        if qs is None:
                            e.copy(dst[:, ge, 512 * nt:512 * nt + 512],
                                   PB[pbase + nt][:, 0:512])
                        else:
                            e.activation(dst[:, ge, 512 * nt:512 * nt + 512],
                                         PB[pbase + nt][:, 0:512], AF.Copy,
                                         scale=qs)
                plan.s(evf)
            plan.bar()
        if rope:
            for c in range(NC + 1):
                if c < NC:
                    def mro(e, c=c, dst=dst_fm):
                        pbase = (c % 2) * 4
                        for nt in range(4):
                            e.matmul(PB[pbase + nt][:, 0:512], PSW[:, :],
                                     dst[:, c, 512 * nt:512 * nt + 512],
                                     start=True, stop=True)
                    plan.t(mro)

                def vro(e, c=c, dst=dst_fm):
                    if c < NC:
                        t1 = ROPET if c % 2 == 0 else ROPET2
                        for nt in range(4):
                            e.tensor_mul(t1[:, nt, :],
                                         dst[:, c, 512 * nt:512 * nt + 512],
                                         CCT[:, 512 * nt:512 * nt + 512])
                    cm = c - 1
                    if cm >= 0:
                        t1m = ROPET if cm % 2 == 0 else ROPET2
                        pbase = (cm % 2) * 4
                        for nt in range(4):
                            e.tensor_mul(EVG[:, 0:512], PB[pbase + nt][:, 0:512],
                                         SST[:, 512 * nt:512 * nt + 512])
                            e.tensor_add(dst[:, cm, 512 * nt:512 * nt + 512],
                                         t1m[:, nt, :], EVG[:, 0:512])
                plan.v(vro)
                plan.bar()

    def _pipe_tok(mm_fn, ev_fn):
        for tt in range(NT + 1):
            if tt < NT:
                plan.t(lambda e, tt=tt: mm_fn(e, tt))
            if tt - 1 >= 0:
                plan.s(lambda e, tt=tt: ev_fn(e, tt - 1))
            plan.bar()

    def _kt_scatter(e, tt, swapped):
        pb0, pb1 = PB[(tt % 2) * 2], PB[(tt % 2) * 2 + 1]
        for b5, pb in ((0, pb0), (1, pb1)):
            srcv = pb.ap()[:, 0:512].rearrange(
                "p (g par ih il d) -> p par ih il g d",
                g=4, par=2, ih=2, il=8, d=4)
            for ph2 in range(2):
                r = 2 * tt + ((1 - ph2) if swapped else ph2)
                for ih, buf in ((0, KTTD), (1, KTTA)):
                    s4 = srcv[:, ph2, ih]
                    dst = buf.ap().rearrange(
                        "p s (G p2 d r) -> p s G p2 d r",
                        G=8, p2=2, d=4, r=32)
                    e.copy(dst[:, :, 4 * b5:4 * b5 + 4, ph2, :, r], s4)

    def emit_proj_tok_KT():
        def mm_fn(e, tt):
            pb0, pb1 = PB[(tt % 2) * 2], PB[(tt % 2) * 2 + 1]
            for kc in range(NC):
                e.matmul(pb0[:, 0:512], XNF[:, kc, 128 * tt:128 * tt + 128],
                         WB[:, kc, 0:512], start=(kc == 0), stop=(kc == NC - 1))
            for kc in range(NC):
                e.matmul(pb1[:, 0:512], XNF[:, kc, 128 * tt:128 * tt + 128],
                         WB2[:, kc, 0:512], start=(kc == 0), stop=(kc == NC - 1))

        def ev_fn(e, tt):
            _kt_scatter(e, tt, swapped=False)
        _pipe_tok(mm_fn, ev_fn)

    def emit_proj_tok_KTS():
        def mm_fn(e, tt):
            pb0, pb1 = PB[(tt % 2) * 2], PB[(tt % 2) * 2 + 1]
            for half in range(2):
                src_sl = slice(128 * tt + 64 * (1 - half),
                               128 * tt + 64 * (1 - half) + 64)
                for kc in range(NC):
                    e.matmul(pb0[64 * half:64 * half + 64, 0:512],
                             XNF[:, kc, src_sl], WB[:, kc, 0:512],
                             start=(kc == 0), stop=(kc == NC - 1))
                for kc in range(NC):
                    e.matmul(pb1[64 * half:64 * half + 64, 0:512],
                             XNF[:, kc, src_sl], WB2[:, kc, 0:512],
                             start=(kc == 0), stop=(kc == NC - 1))

        def ev_fn(e, tt):
            _kt_scatter(e, tt, swapped=True)
        _pipe_tok(mm_fn, ev_fn)

    def emit_proj_tok_VS():
        def mm_fn(e, blk):
            pb0, pb1 = PB[(blk % 2) * 2], PB[(blk % 2) * 2 + 1]
            for slq in range(2):
                lh = XNF.ap()[:, :, 2 * blk + slq: 2 * blk + slq + 32 * 63 + 1:32]
                for kc in range(NC):
                    e.matmul(pb0[64 * slq:64 * slq + 64, 0:512], lh[:, kc, :],
                             WB[:, kc, 0:512], start=(kc == 0), stop=(kc == NC - 1))
                for kc in range(NC):
                    e.matmul(pb1[64 * slq:64 * slq + 64, 0:512], lh[:, kc, :],
                             WB2[:, kc, 0:512], start=(kc == 0), stop=(kc == NC - 1))

        def ev_fn(e, blk):
            pb0, pb1 = PB[(blk % 2) * 2], PB[(blk % 2) * 2 + 1]
            e.copy(VA[:, blk, 0:8, 0:64], pb0[:, 0:512])
            e.copy(VA[:, blk, 8:16, 0:64], pb1[:, 0:512])
        _pipe_tok(mm_fn, ev_fn)

    def emit_attn_l1s_ubound():
        for gi in range(6):
            def vup(e, gi=gi):
                if gi < 4:
                    sq = SQKQ if gi % 2 == 0 else SQKQ2
                    for c in range(NC):
                        e.tensor_mul(sq[:, c, :],
                                     QF[:, c, 512 * gi:512 * gi + 512],
                                     QF[:, c, 512 * gi:512 * gi + 512])
                gr = gi - 2
                if gr >= 0:
                    e.reduce_max(SX[0:16, gr:gr + 1],
                                 PB[4 if gr % 2 == 0 else 5][0:16, 0:512],
                                 axis=AX.X)
            plan.v(vup)
            gm = gi - 1
            if 0 <= gm < 4:
                def msq(e, gm=gm):
                    sq = SQKQ if gm % 2 == 0 else SQKQ2
                    pb = PB[4 if gm % 2 == 0 else 5]
                    for c in range(NC):
                        e.matmul(pb[0:16, 0:512], ONEHS[:, c, :], sq[:, c, :],
                                 start=(c == 0), stop=(c == NC - 1))
                plan.t(msq)
            plan.bar()
        for tt in range(NT + 2):
            def vkp(e, tt=tt):
                if tt < NT:
                    sq = SQKQ if tt % 2 == 0 else SQKQ2
                    kd = (KTTD if tt < 8 else KTTA).ap()[:, tt % 8, :] \
                        .rearrange("p (a b) -> p a b", b=256)
                    for c4, cl in ((0, 0), (1, 192)):
                        e.tensor_mul(sq[:, c4, :].rearrange(
                                         "p (a b) -> p a b", b=64),
                                     kd[:, :, cl:cl + 64], kd[:, :, cl:cl + 64])
                tr = tt - 2
                if tr >= 0:
                    e.reduce_max(K2M[0:2, tr:tr + 1],
                                 PB[5 if tr % 2 == 0 else 6][0:2, 0:512],
                                 axis=AX.X)
            plan.v(vkp)
            tm = tt - 1
            if 0 <= tm < NT:
                def mks(e, tm=tm):
                    sq = SQKQ if tm % 2 == 0 else SQKQ2
                    pb = PB[5 if tm % 2 == 0 else 6]
                    e.matmul(pb[0:2, 0:512], ONES2[:, :], sq[:, 0, :],
                             start=True, stop=False)
                    e.matmul(pb[0:2, 0:512], ONES2[:, :], sq[:, 1, :],
                             start=False, stop=True)
                plan.t(mks)
            plan.bar()

        def vkred(e):
            e.reduce_max(SC[0:16, 20:21], SX[0:16, 0:4], axis=AX.X)
            e.reduce_max(SC[0:2, 21:22], K2M[0:2, 0:16], axis=AX.X)
        plan.v(vkred)
        plan.bar()
        plan.t(lambda e: e.transpose(PB[6][0:1, 0:16], SC[0:16, 20:21],
                                     IDF[0:16, 0:16]))
        plan.bar()

        def vq2(e):
            e.reduce_max(SR[0:1, 20:21], PB[6][0:1, 0:16], axis=AX.X)
        plan.v(vq2)
        plan.bar()
        plan.t(lambda e: e.transpose(PB[6][0:1, 0:2], SC[0:2, 21:22],
                                     IDF[0:2, 0:2]))
        plan.bar()

        def vk2(e):
            e.reduce_max(SR[0:1, 21:22], PB[6][0:1, 0:2], axis=AX.X)
            e.tensor_mul(SR[0:1, 22:23], SR[0:1, 20:21], SR[0:1, 21:22])
        plan.v(vk2)
        plan.bar()
        plan.s(lambda e: e.activation(SR[0:1, 23:24], SR[0:1, 22:23], AF.Sqrt))
        plan.bar()

        def vnU(e):
            e.tensor_scalar(out=SR[0:1, 24:25], in0=SR[0:1, 23:24],
                            scalar1=-1.0, scalar2=None, op0=ALU.mult)
        plan.v(vnU)
        plan.bar()
        bcast_scalar(24, 23)

    def emit_attention_cz(kslice, out_buf):
        blocks = [(h, jq) for h in range(H) for jq in range(4)]
        nb = len(blocks)
        av_b = attn_cnt.setdefault("av", 0)
        ev_b = attn_cnt.setdefault("ev", 0)
        attn_cnt["av"] += nb
        attn_cnt["ev"] += nb
        for bi in range(nb + 2):
            if bi < nb:
                h, jq = blocks[bi]
                n_kt = 4 * jq + 4
                pav = PB[4 + bi % 3]
                sc_b = attn_cnt["sc"]
                ex_b = attn_cnt["ex"]
                attn_cnt["sc"] += n_kt
                attn_cnt["ex"] += n_kt

                def tf(e, h=h, jq=jq, bi=bi, n_kt=n_kt, pav=pav,
                       sc_b=sc_b, ex_b=ex_b):
                    hb = 64 * (h % 2)

                    def av(j, start, stop):
                        for cx in range(4):
                            mm = e.matmul(pav[:, 65 * cx:65 * cx + 65],
                                          PTS[:, j % 4, 128 * cx:128 * cx + 128],
                                          VA[:, j, h, :],
                                          start=(start and cx == 0),
                                          stop=(stop and cx == 3))
                        return mm

                    for i in range(n_kt):
                        ps = PB[i % 4]
                        if ex_b + i - 3 > 0:
                            e.wait_ge(sem_ex, ex_b + i - 3)
                        d = i - 4 * jq
                        if d >= 0:
                            e.matmul(ps[0:128, 0:512], kslice(h, i),
                                     QF[hb:hb + 64, h // 2,
                                        512 * jq:512 * jq + 512],
                                     start=True, stop=False)
                            e.matmul(ps[0:128, 0:512], IDB[:, :], MNEG[:, d, :],
                                     start=False, stop=True).then_inc(sem_sc, 1)
                        else:
                            e.matmul(ps[0:128, 0:512], kslice(h, i),
                                     QF[hb:hb + 64, h // 2,
                                        512 * jq:512 * jq + 512],
                                     start=True, stop=True).then_inc(sem_sc, 1)
                        if i >= 3:
                            if i == 3 and bi >= 3:
                                e.wait_ge(sem_ev, ev_b + bi - 2)
                            e.wait_ge(sem_ex, ex_b + i - 2)
                            av(i - 3, start=(i == 3), stop=False)
                    for j in (n_kt - 3, n_kt - 2, n_kt - 1):
                        e.wait_ge(sem_ex, ex_b + j + 1)
                        mm = av(j, start=False, stop=(j == n_kt - 1))
                    mm.then_inc(sem_av, 1)
                plan.t(tf)

                def sf(e, n_kt=n_kt, sc_b=sc_b):
                    for i in range(n_kt):
                        e.wait_ge(sem_sc, sc_b + i + 1)
                        e.activation(PTS[:, i % 4, :], PB[i % 4][:, 0:512], AF.Exp,
                                     bias=SC[:, 23:24], scale=1.0
                                     ).then_inc(sem_ex, 1)
                plan.s(sf)
            bj = bi - 1
            if 0 <= bj < nb:
                hj, jqj = blocks[bj]
                pavj = PB[4 + bj % 3]
                pav3 = pavj.ap()[:, 0:260].rearrange("p (c j) -> p c j", c=4)
                rcpj = RCP[:, 0:4] if bj % 2 == 0 else RCP[:, 4:8]
                hslot = out_buf.ap()[:, 4 * jqj:4 * jqj + 4,
                                     64 * hj:64 * hj + 64]

                def ev(e, bj=bj, pav3=pav3, rcpj=rcpj, hslot=hslot):
                    e.wait_ge(sem_av, av_b + bj + 1)
                    e.reciprocal(rcpj, pav3[:, :, 64:65])
                    e.tensor_mul(hslot, pav3[:, :, 0:64],
                                 rcpj[:, :, None].broadcast_to([128, 4, 64]))
                    e.drain().then_inc(sem_ev, 1)
                plan.v(ev)
        plan.bar()

    def emit_attention_l1s():
        emit_attention_cz(
            lambda h, i: (KTTD if i < 8 else KTTA)
            [64 * (h % 2):64 * (h % 2) + 64, i % 8, 128 * h:128 * h + 128],
            HOUT)
        for tt in range(NT):
            DMAACC(d_yt[:, tt, :], HOUT[:, tt, :])
        plan.bar()

    def emit_proj_tok_V():
        def mm_fn(e, tt):
            pb0, pb1 = PB[(tt % 2) * 2], PB[(tt % 2) * 2 + 1]
            for kc in range(NC):
                e.matmul(pb0[:, 0:512], XNF[:, kc, 128 * tt:128 * tt + 128],
                         WB[:, kc, 0:512], start=(kc == 0), stop=(kc == NC - 1))
            for kc in range(NC):
                e.matmul(pb1[:, 0:512], XNF[:, kc, 128 * tt:128 * tt + 128],
                         WB2[:, kc, 0:512], start=(kc == 0), stop=(kc == NC - 1))

        def ev_fn(e, tt):
            pb0, pb1 = PB[(tt % 2) * 2], PB[(tt % 2) * 2 + 1]
            e.copy(VA[:, tt, 0:8, 0:64], pb0[:, 0:512])
            e.copy(VA[:, tt, 8:16, 0:64], pb1[:, 0:512])
        _pipe_tok(mm_fn, ev_fn)

    def load_wb_halves(w_ap, col0):
        DMA1("gpsimd", WB.ap(),
             w_ap[:, col0:col0 + 512].rearrange("(c p) n -> p c n", p=128))
        DMA1("gpsimd", WB2.ap(),
             w_ap[:, col0 + 512:col0 + 1024].rearrange("(c p) n -> p c n", p=128))

    DMA1("gpsimd", d_y, d_x)
    plan.bar()
    if SUBST >= 0.2:
        emit_layernorm(d_x.rearrange("(t p) d -> p t d", p=128))

    def _noop(e):
        pass

    def xnmax(e):
        for t in range(NT):
            e.reduce_max(SX[:, t:t + 1], XNT[:, t, :], axis=AX.X,
                         apply_absolute_value=True)
        e.reduce_max(SC[:, 0:1], SX[:, :], axis=AX.X)
    if SUBST >= 0.3:
        plan.v(xnmax)
        plan.bar()
        scalar_from_col(SC[:, 0:1], 0)
        plan.v(lambda e: (e.tensor_copy(SR[0:1, 1:2], SR[0:1, 0:1]), e.drain()))
        plan.bar()
        emit_allgather(0)

    def mkscale1(e):
        e.reduce_max(SR[0:1, 0:1], AG2[:, :, 0:1], axis=AX.XY)
        e.tensor_scalar(out=SR[0:1, 1:2], in0=SR[0:1, 0:1], scalar1=1.0 / QMAX,
                        scalar2=1e-12, op0=ALU.mult, op1=ALU.max)
        e.reciprocal(SR[0:1, 2:3], SR[0:1, 1:2])
    if SUBST >= 0.3:
        plan.v(mkscale1)
        plan.bar()
        bcast_scalar(1, 1)
        bcast_scalar(2, 2)

    def qxn(e):
        for c in range(NC):
            e.tensor_scalar(out=EV0[:, :], in0=XNF[:, c, :], scalar1=SC[:, 2:3],
                            scalar2=MAGIC, op0=ALU.mult, op1=ALU.add)
            e.tensor_scalar(out=XQF[:, c, :], in0=EV0[:, :], scalar1=MAGIC,
                            scalar2=SC[:, 1:2], op0=ALU.subtract, op1=ALU.mult)
    if SUBST >= 0.4:
        plan.v(qxn)
        plan.bar()
    if SUBST >= 0.5:
        emit_upgate(din["l0_wu"], d_up, UXM)
    if SUBST >= 0.6:
        emit_upgate(din["l0_wg"], d_gate, GXM)
        emit_wd_dequant()

    def upgmax(e):
        e.reduce_max(SC[:, 0:1], UXM[:, :], axis=AX.X)
        e.reduce_max(SC[:, 3:4], GXM[:, :], axis=AX.X)
    if SUBST < 0.7:
        plan.bar()

        def _finish0(e):
            pass
        plan.v(_finish0)
    _guard = SUBST >= 0.7
    if _guard:
        plan.v(upgmax)
        plan.bar()
        scalar_from_col(SC[:, 0:1], 0)
        scalar_from_col(SC[:, 3:4], 1)
        emit_allgather(1)

    def mkscale23(e):
        e.reduce_max(SR[0:1, 0:1], AG2[:, :, 0:1], axis=AX.XY)
        e.reduce_max(SR[0:1, 3:4], AG2[:, :, 1:2], axis=AX.XY)
        e.tensor_scalar(out=SR[0:1, 1:2], in0=SR[0:1, 0:1], scalar1=1.0 / QMAX,
                        scalar2=1e-12, op0=ALU.mult, op1=ALU.max)
        e.tensor_scalar(out=SR[0:1, 4:5], in0=SR[0:1, 3:4], scalar1=1.0 / QMAX,
                        scalar2=1e-12, op0=ALU.mult, op1=ALU.max)
        e.reciprocal(SR[0:1, 2:3], SR[0:1, 1:2])
        e.reciprocal(SR[0:1, 5:6], SR[0:1, 4:5])
        e.tensor_copy(SR[0:1, 17:18], SR[0:1, 1:2])
    plan.v(mkscale23)
    plan.bar()
    bcast_scalar(1, 3)
    bcast_scalar(2, 4)
    bcast_scalar(4, 5)
    bcast_scalar(5, 6)

    for m in range(NI + 4):
        if m < NI:
            DMA1("sync", (EB0 if m % 2 == 0 else EB1)[:, :], d_gate[m, :, :])
        ma = m - 1
        if 0 <= ma < NI:
            ebuf = EB0 if ma % 2 == 0 else EB1
            gq = GQ[ma % 3]

            def cfa(e, ebuf=ebuf, gq=gq):
                e.tensor_scalar(out=EV5[:, :], in0=ebuf[:, :], scalar1=SC[:, 6:7],
                                scalar2=MAGIC, op0=ALU.mult, op1=ALU.add)
                e.tensor_scalar(out=gq[:, :], in0=EV5[:, :], scalar1=MAGIC,
                                scalar2=None, op0=ALU.subtract)
            plan.v(cfa)
        mb = m - 2
        if 0 <= mb < NI:
            gq = GQ[mb % 3]
            sg = EV3 if mb % 2 == 0 else EV4
            plan.s(lambda e, gq=gq, sg=sg: e.activation(
                sg[:, :], gq[:, :], AF.Sigmoid, scale=SC[:, 5:6]))
        mc_ = m - 3
        if 0 <= mc_ < NI:
            gq = GQ[mc_ % 3]
            sg = EV3 if mc_ % 2 == 0 else EV4
            tout = EV0 if mc_ % 2 == 0 else EV1

            def cfc(e, gq=gq, sg=sg, tout=tout, mc_=mc_):
                e.tensor_scalar(out=EV2[:, :], in0=sg[:, :], scalar1=65536.0,
                                scalar2=MAGIC, op0=ALU.mult, op1=ALU.add)
                e.tensor_scalar(out=EV2[:, :], in0=EV2[:, :], scalar1=MAGIC,
                                scalar2=65535.0, op0=ALU.subtract, op1=ALU.min)
                e.tensor_mul(tout[:, :], gq[:, :], EV2[:, :])
                e.reduce_max(TXM[:, mc_:mc_ + 1], tout[:, :], axis=AX.X,
                             apply_absolute_value=True)
            plan.v(cfc)
        msp = m - 4
        if 0 <= msp < NI:
            tsrc = EV0 if msp % 2 == 0 else EV1
            DMA1("scalar", d_t[msp, :, :], tsrc[:, :])
        plan.bar()

    def tmaxv(e):
        e.reduce_max(SC[:, 0:1], TXM[:, :], axis=AX.X)
    plan.v(tmaxv)
    plan.bar()
    scalar_from_col(SC[:, 0:1], 0)
    plan.v(lambda e: (e.tensor_copy(SR[0:1, 1:2], SR[0:1, 0:1]), e.drain()))
    plan.bar()
    emit_allgather(2)

    def mkscale4(e):
        e.reduce_max(SR[0:1, 0:1], AG2[:, :, 0:1], axis=AX.XY)
        e.tensor_scalar(out=SR[0:1, 6:7], in0=SR[0:1, 4:5], scalar1=1.0 / 65536.0,
                        scalar2=None, op0=ALU.mult)
        e.tensor_mul(SR[0:1, 7:8], SR[0:1, 0:1], SR[0:1, 6:7])
        e.tensor_scalar(out=SR[0:1, 11:12], in0=SR[0:1, 7:8], scalar1=1.0 / QMAX,
                        scalar2=1e-12, op0=ALU.mult, op1=ALU.max)
        e.reciprocal(SR[0:1, 18:19], SR[0:1, 11:12])
        e.tensor_mul(SR[0:1, 12:13], SR[0:1, 6:7], SR[0:1, 18:19])
    plan.v(mkscale4)
    plan.bar()
    bcast_scalar(11, 7)
    bcast_scalar(12, 8)

    EVT3 = (EV3, EV4, EV2)
    EV56 = (EV5, EV6)
    for m in range(NI + 4):
        if m < NI:
            DMA1("sync", (EV0 if m % 2 == 0 else EV1)[:, :], d_t[m, :, :])
            DMA1("scalar", (EB0 if m % 2 == 0 else EB1)[:, :], d_up[m, :, :])
        ma = m - 1
        if 0 <= ma < NI:
            tsrc = EV0 if ma % 2 == 0 else EV1
            usrc = EB0 if ma % 2 == 0 else EB1

            def cfb1(e, tsrc=tsrc, tb=EVT3[ma % 3]):
                e.tensor_scalar(out=tb[:, :], in0=tsrc[:, :], scalar1=SC[:, 8:9],
                                scalar2=MAGIC, op0=ALU.mult, op1=ALU.add)
            plan.v(cfb1)

            def cfb2(e, usrc=usrc, ub=EV56[ma % 2]):
                e.tensor_scalar(out=ub[:, :], in0=usrc[:, :], scalar1=SC[:, 4:5],
                                scalar2=MAGIC, op0=ALU.mult, op1=ALU.add)
            plan.g(cfb2)
        mb = m - 2
        if 0 <= mb < NI:

            def cfb3(e, tb=EVT3[mb % 3], ub=EV56[mb % 2], mb=mb):
                e.tensor_scalar(out=EV7[:, :], in0=ub[:, :], scalar1=MAGIC,
                                scalar2=None, op0=ALU.subtract)
                e.scalar_tensor_tensor(out=tb[:, :], in0=tb[:, :], scalar=MAGIC,
                                       in1=EV7[:, :], op0=ALU.subtract,
                                       op1=ALU.mult)
                e.reduce_max(T2XM[:, mb:mb + 1], tb[:, :], axis=AX.X,
                             apply_absolute_value=True)
            plan.v(cfb3)
        msp = m - 3
        if 0 <= msp < NI:
            DMA1("scalar", d_t2[msp, :, :], EVT3[msp % 3][:, :])
        plan.bar()

    def t2maxv(e):
        e.reduce_max(SC[:, 0:1], T2XM[:, :], axis=AX.X)
    plan.v(t2maxv)
    plan.bar()
    scalar_from_col(SC[:, 0:1], 0)
    plan.v(lambda e: (e.tensor_copy(SR[0:1, 1:2], SR[0:1, 0:1]), e.drain()))
    plan.bar()
    emit_allgather(3)

    def mkscale5(e):
        e.reduce_max(SR[0:1, 0:1], AG2[:, :, 0:1], axis=AX.XY)
        e.tensor_mul(SR[0:1, 13:14], SR[0:1, 17:18], SR[0:1, 11:12])
        e.tensor_mul(SR[0:1, 14:15], SR[0:1, 0:1], SR[0:1, 13:14])
        e.tensor_scalar(out=SR[0:1, 15:16], in0=SR[0:1, 14:15], scalar1=1.0 / QMAX,
                        scalar2=1e-12, op0=ALU.mult, op1=ALU.max)
        e.reciprocal(SR[0:1, 18:19], SR[0:1, 15:16])
        e.tensor_mul(SR[0:1, 16:17], SR[0:1, 13:14], SR[0:1, 18:19])
    plan.v(mkscale5)
    plan.bar()
    bcast_scalar(15, 9)
    bcast_scalar(16, 10)

    def obf(i):
        if i < 8:
            return OB_A[:, i, :]
        if i < 16:
            return OB_D[:, i - 8, :]
        return OB_E[:, i - 16, :]

    for m in range(NI + 1):
        if m < NI:
            DMA1("sync", (EV0 if m % 2 == 0 else EV1)[:, :], d_t2[m, :, :])
        ma = m - 1
        if 0 <= ma < NI:
            tsrc = EV0 if ma % 2 == 0 else EV1

            def cfo(e, tsrc=tsrc, ma=ma):
                e.tensor_scalar(out=EV2[:, :], in0=tsrc[:, :], scalar1=SC[:, 10:11],
                                scalar2=MAGIC, op0=ALU.mult, op1=ALU.add)
                e.tensor_scalar(out=obf(ma), in0=EV2[:, :], scalar1=MAGIC,
                                scalar2=SC[:, 9:10], op0=ALU.subtract, op1=ALU.mult)
            plan.v(cfo)
        plan.bar()

    for g in range(4):
        for mi in range(NI // 2 + 1):
            if mi < NI // 2:
                wl = (WDR0, WDR1) if mi % 2 == 0 else (WDR2, WDR3)
                DMA1("sync", wl[0][:, :], d_wdT[2 * mi, :, :])
                DMA1("sync", wl[1][:, :], d_wdT[2 * mi + 1, :, :])
            mmi = mi - 1
            if mmi >= 0:
                wl = (WDR0, WDR1) if mmi % 2 == 0 else (WDR2, WDR3)

                def mfd(e, wl=wl, g=g, mmi=mmi):
                    for k in range(2):
                        mm_ = 2 * mmi + k
                        for tt in range(4):
                            gt = 4 * g + tt
                            for hh in range(2):
                                e.matmul(PB[2 * tt + hh][:, 0:512],
                                         obf(mm_)[:, 128 * gt:128 * gt + 128],
                                         wl[k][:, 512 * hh:512 * hh + 512],
                                         start=(mm_ == 0), stop=(mm_ == NI - 1))
                plan.t(mfd)
            plan.bar()
        for tt in range(4):
            gt = 4 * g + tt

            def evd(e, tt=tt):
                e.copy(EVG[:, 0:512], PB[2 * tt][:, 0:512])
                e.copy(EVG[:, 512:1024], PB[2 * tt + 1][:, 0:512])
            plan.s(evd)
            plan.bar()
            DMAACC(d_yt[:, gt, :], EVG[:, :])
            plan.bar()

    if STAGE >= 2:
        DMA1("gpsimd", CCT[:, :], din["cc_tab"])
        DMA1("gpsimd", SST[:, :], din["ss_tab"])
        DMA1("gpsimd", MNEG.ap().rearrange("p a b -> p (a b)"), din["mneg"])
        plan.bar()
        emit_proj_fm(QF, din["l0_wq"], 0, rope=True, qscale=0.125)
        emit_proj_fm(KF, din["l0_wk"], 0, rope=True)
        load_wb_halves(din["l0_wv"], 0)

        def vone(e):
            e.memset(VA[:, :, :, 64:65], 1.0)
        plan.v(vone)
        plan.bar()
        emit_proj_tok_V()
        print("[kernel] bars before L0 attention:", plan.nbars())
        emit_attn_l0_ubound()
        emit_attention_cz(
            lambda h, i: KF[64 * (h % 2):64 * (h % 2) + 64, h // 2,
                            128 * i:128 * i + 128],
            HT)
        print("[kernel] bars after L0 attention:", plan.nbars())

        DMA1("gpsimd", WOBR[0:1, :], din["l0_wob"])
        load_wb_halves(din["l0_wo"], 0)
        emit_transpose_tok_to_fm(HT, QF)
        for tt in range(NT + 2):
            if tt < NT:
                pb0, pb1 = PB[(tt % 2) * 2], PB[(tt % 2) * 2 + 1]

                def mfo(e, pb0=pb0, pb1=pb1, tt=tt):
                    for kc in range(NC):
                        e.matmul(pb0[:, 0:512], QF[:, kc, 128 * tt:128 * tt + 128],
                                 WB[:, kc, 0:512], start=(kc == 0), stop=False)
                    e.matmul(pb0[:, 0:512], ONESF[0:1, :], WOBR[0:1, 0:512],
                             start=False, stop=True)
                    for kc in range(NC):
                        e.matmul(pb1[:, 0:512], QF[:, kc, 128 * tt:128 * tt + 128],
                                 WB2[:, kc, 0:512], start=(kc == 0), stop=False)
                    e.matmul(pb1[:, 0:512], ONESF[0:1, :], WOBR[0:1, 512:1024],
                             start=False, stop=True)
                plan.t(mfo)
            te = tt - 1
            if 0 <= te < NT:
                pb0, pb1 = PB[(te % 2) * 2], PB[(te % 2) * 2 + 1]
                evg = EVG if te % 2 == 0 else EVG2

                def evo(e, pb0=pb0, pb1=pb1, evg=evg):
                    e.copy(evg[:, 0:512], pb0[:, 0:512])
                    e.copy(evg[:, 512:1024], pb1[:, 0:512])
                plan.s(evo)
            ta = tt - 2
            if ta >= 0:
                evg = EVG if ta % 2 == 0 else EVG2
                DMAACC(d_yt[:, ta, :], evg[:, :])
            plan.bar()

    if STAGE >= 3:
        emit_layernorm(d_yt)

        load_wb_halves(din["l1_bw_t"], 0)
        DMA1("gpsimd", WBL[0:64, :], din["l1_B"])
        DMA1("gpsimd", WPK.ap(),
             din["l1_pack"].rearrange("(c p) n -> p c n", p=128))
        DMA1("sync", RB16[0:1, 0:16], din["l1_rb16"])
        DMA1("sync", BBR[0:1, :], din["l1_bb"])
        plan.bar()

        for tt in range(NT + 1):
            if tt < NT:
                pb = PB[4 + tt % 2]

                def mfl(e, pb=pb, tt=tt):
                    for kc in range(NC):
                        e.matmul(pb[:, 0:16], XNF[:, kc, 128 * tt:128 * tt + 128],
                                 WPK[:, kc, :], start=(kc == 0), stop=False)
                    e.matmul(pb[:, 0:16], ONESF[0:1, :], RB16[0:1, 0:16],
                             start=False, stop=True)
                plan.t(mfl)
            te = tt - 1
            if te >= 0:
                pbe = PB[4 + te % 2]
                plan.s(lambda e, pbe=pbe, te=te: e.copy(LG[:, te, :], pbe[:, 0:16]))
            plan.bar()

        def vmax_all(e):
            for tt in range(NT):
                e.reduce_max(SM1[:, tt:tt + 1], LG[:, tt, 0:8], axis=AX.X,
                             negate=True)
                e.tensor_scalar(out=M2T[:, tt, :], in0=LG[:, tt, 8:16],
                                scalar1=2.0, scalar2=None, op0=ALU.mult)
        plan.v(vmax_all)
        plan.bar()

        def sexp_all(e):
            for tt in range(NT):
                e.activation(LG[:, tt, 0:8], LG[:, tt, 0:8], AF.Exp,
                             bias=SM1[:, tt:tt + 1], scale=1.0,
                             accum_out=SM2[:, tt:tt + 1])
        plan.s(sexp_all)
        plan.bar()

        def vz_all(e):
            e.reciprocal(SM3[:, :], SM2[:, :])
            for tt in range(NT):
                e.tensor_scalar(out=GT[:, tt, :], in0=LG[:, tt, 0:8],
                                scalar1=SM3[:, tt:tt + 1], scalar2=None,
                                op0=ALU.mult)
            for tt in range(NT):
                e.tensor_mul(
                    EVG.ap()[:, 64 * tt:64 * tt + 64]
                    .rearrange("p (a b) -> p a b", a=8),
                    GT[:, tt, :, None].broadcast_to([128, 8, 8]),
                    M2T[:, tt, None, :].broadcast_to([128, 8, 8]))
        plan.v(vz_all)
        plan.bar()

        def tz_all(e):
            for tt in range(NT):
                e.transpose(PB[4 + tt // 4][0:64, 128 * (tt % 4):128 * (tt % 4) + 128],
                            EVG[:, 64 * tt:64 * tt + 64], IDF[:, :])
        plan.t(tz_all)
        plan.bar()

        def sz_all(e):
            for tt in range(NT):
                e.copy(ZF[0:64, tt, :],
                       PB[4 + tt // 4][0:64, 128 * (tt % 4):128 * (tt % 4) + 128])
        plan.s(sz_all)
        plan.bar()

        for tt in range(NT + 2):
            if tt < NT:
                pb0, pb1 = PB[(tt % 2) * 2], PB[(tt % 2) * 2 + 1]

                def mfb(e, pb0=pb0, pb1=pb1, tt=tt):
                    for kc in range(NC):
                        e.matmul(pb0[:, 0:512], XNF[:, kc, 128 * tt:128 * tt + 128],
                                 WB[:, kc, 0:512], start=(kc == 0), stop=False)
                    e.matmul(pb0[:, 0:512], ZF[0:64, tt, :], WBL[0:64, 0:512],
                             start=False, stop=False)
                    e.matmul(pb0[:, 0:512], ONESF[0:1, :], BBR[0:1, 0:512],
                             start=False, stop=True)
                    for kc in range(NC):
                        e.matmul(pb1[:, 0:512], XNF[:, kc, 128 * tt:128 * tt + 128],
                                 WB2[:, kc, 0:512], start=(kc == 0), stop=False)
                    e.matmul(pb1[:, 0:512], ZF[0:64, tt, :], WBL[0:64, 512:1024],
                             start=False, stop=False)
                    e.matmul(pb1[:, 0:512], ONESF[0:1, :], BBR[0:1, 512:1024],
                             start=False, stop=True)
                plan.t(mfb)
            te = tt - 1
            if 0 <= te < NT:
                pb0, pb1 = PB[(te % 2) * 2], PB[(te % 2) * 2 + 1]
                evg = EVG if te % 2 == 0 else EVG2

                def evb(e, pb0=pb0, pb1=pb1, evg=evg):
                    e.copy(evg[:, 0:512], pb0[:, 0:512])
                    e.copy(evg[:, 512:1024], pb1[:, 0:512])
                plan.s(evb)
            ta = tt - 2
            if ta >= 0:
                evg = EVG if ta % 2 == 0 else EVG2
                DMAACC(d_yt[:, ta, :], evg[:, :])
            plan.bar()

    if STAGE >= 4:
        DMA1("gpsimd", ONES2[:, :], din["ones2"])
        plan.bar()
        emit_proj_fm(QF, din["l1_qkv"], 0, rope=False, qscale=0.125)
        load_wb_halves(din["l1_qkv"], 1024)
        plan.bar()
        emit_proj_tok_KT()
        print("[kernel] bars after KT:", plan.nbars())
        emit_proj_tok_KTS()
        print("[kernel] bars after KTS:", plan.nbars())
        emit_attn_l1s_ubound()
        print("[kernel] bars after ubound:", plan.nbars())
        DMA1("gpsimd", MNEG.ap().rearrange("p a b -> p (a b)"), din["mneg"])
        load_wb_halves(din["l1_qkv"], 2048)

        def vone2(e):
            e.memset(VA[:, :, :, 64:65], 1.0)
        plan.v(vone2)
        plan.bar()
        emit_proj_tok_V()
        print("[kernel] bars after V:", plan.nbars())
        emit_attention_l1s()

    plan.bar()

    bar_limit = int(os.environ.get("MK_TRUNC", "999999"))

    def _ap_range(ap):
        try:
            t = ap.tensor
            off = ap.offset
            dims = list(ap.ap)
            if len(dims) >= 2:
                pitch, pcount = dims[0]
                if pitch <= 0:
                    pitch = 1
                p0 = off // pitch
                f0 = off % pitch
                span = 0
                for step, count in dims[1:]:
                    if count > 1:
                        span += abs(step) * (count - 1)
                return (t.name, p0, p0 + pcount, f0, f0 + span + 1)
            span = 0
            for step, count in dims:
                if count > 1:
                    span += abs(step) * (count - 1)
            return (t.name, 0, 1 << 30, off, off + span + 1)
        except Exception:
            return None

    _DRAINSTAT = [0, 0, 0]

    def _is_ap(x):
        return hasattr(x, "tensor") and hasattr(x, "ap")

    class AutoDrain:
        _PASS = ("wait_ge", "nop", "sem_inc", "dma_start",
                 "dma_start_transpose", "collective_compute", "load_library")

        def __init__(self, e):
            object.__setattr__(self, "_e", e)
            object.__setattr__(self, "_hr", [])
            object.__setattr__(self, "_hw", [])

        def drain(self):
            self._hr.clear()
            self._hw.clear()
            return self._e.drain()

        def _emit(self, name, a, args, kw):
            e = self._e
            aps = []
            for i, x in enumerate(args):
                if _is_ap(x):
                    aps.append(("out" if i == 0 else "in", x))
            for k, x in kw.items():
                if _is_ap(x):
                    kind = "out" if (k.startswith("out") or k == "accum_out"
                                     or k == "dest") else "in"
                    aps.append((kind, x))
            writes = [r for r in (_ap_range(x) for k, x in aps if k == "out") if r]
            reads = [r for r in (_ap_range(x) for k, x in aps if k == "in") if r]
            if name == "memset":
                writes, reads = writes + reads, []

            def overlap(r1, r2):
                return (r1[0] == r2[0] and r1[1] < r2[2] and r2[1] < r1[2]
                        and r1[3] < r2[4] and r2[3] < r1[4])

            hr, hw = self._hr, self._hw
            conflict = (any(overlap(w, p) for w in writes for p in hr)
                        or any(overlap(w, p) for w in writes for p in hw)
                        or any(overlap(r, p) for r in reads for p in hw))
            if conflict:
                e.drain()
                _DRAINSTAT[0] += 1
                hr.clear()
                hw.clear()
            _DRAINSTAT[1] += 1
            _DRAINSTAT[2] += len(writes) + len(reads)
            res = a(*args, **kw)
            hr.extend(reads)
            hw.extend(writes)
            return res

        def __getattr__(self, n):
            a = getattr(self._e, n)
            if callable(a) and n not in self._PASS and not n.startswith("_"):
                def w(*args, _n=n, _a=a, **kw):
                    return self._emit(_n, _a, args, kw)
                return w
            return a

    def make_body(eng_name):
        def body(e):
            if eng_name in ("vector", "scalar", "gpsimd"):
                e = AutoDrain(e)
            dma_count = 0
            cc_count = 0
            bar_i = 0
            my_dma_sem = dma_sems.get(eng_name)
            for it in plan.items:
                if bar_i >= bar_limit:
                    break
                if it[0] == "op":
                    _, oeng, fn = it
                    if oeng != eng_name:
                        continue
                    if isinstance(fn, tuple):
                        kind, f = fn
                        if kind == "DMA":
                            dma_count += f(e, my_dma_sem)
                        else:
                            f(e).then_inc(cc_sem, 1)
                            cc_count += 1
                    else:
                        fn(e)
                else:
                    bar_i += 1
                    if my_dma_sem is not None and dma_count > 0:
                        e.wait_ge(my_dma_sem, dma_count * 16)
                    if eng_name == "gpsimd" and cc_count > 0:
                        e.wait_ge(cc_sem, cc_count)
                    e.drain().then_inc(phase_sems[eng_name], 1)
                    for other in ENGINES:
                        if other != eng_name:
                            e.wait_ge(phase_sems[other], bar_i)
        return body

    nbars = sum(1 for it in plan.items if it[0] == "bar")
    print(f"[kernel] plan: {len(plan.items)} items, {nbars} barriers")
    with nc.Block() as block:
        block.sync(make_body("sync"))
        block.tensor(make_body("tensor"))
        block.vector(make_body("vector"))
        block.scalar(make_body("scalar"))
        block.gpsimd(make_body("gpsimd"))
    st.close()
    print("[kernel] proxy: drains", _DRAINSTAT[0], "ops", _DRAINSTAT[1],
          "aps", _DRAINSTAT[2])
    return nc


_cache = {}


def kernel(**inputs):
    cst = host_constants()
    wd = {
        "l0_wq": inputs["L0_attn_wq"],
        "l0_wk": inputs["L0_attn_wk"],
        "l0_wv": inputs["L0_attn_wv"],
        "l0_wo": inputs["L0_attn_wo_w"],
        "l0_wob": inputs["L0_attn_wo_b"],
        "l0_wg": inputs["L0_ffn_w_gate"],
        "l0_wu": inputs["L0_ffn_w_up"],
        "l0_wd": inputs["L0_ffn_w_down"],
        "l1_qkv": inputs["L1_attn_qkv_weights"],
        "l1_bw_t": np.asarray(inputs["L1_ffn_base_w"]).T,
        "l1_bb": inputs["L1_ffn_base_b"],
        "l1_pack": np.concatenate([np.asarray(inputs["L1_ffn_router_w"]).T,
                                   np.asarray(inputs["L1_ffn_lora_A"])], axis=1),
        "l1_rb16": np.concatenate([np.asarray(inputs["L1_ffn_router_b"]),
                                   np.zeros(8, np.float32)]),
        "l1_B": np.asarray(inputs["L1_ffn_lora_B"]).reshape(64, D),
    }
    wd = {k: np.ascontiguousarray(np.asarray(v, dtype=np.float32))
          for k, v in wd.items()}
    x_full = np.ascontiguousarray(np.asarray(inputs["L0_attn_x"], dtype=np.float32))

    if "nc" not in _cache:
        _cache["nc"] = build_nc()
    nc = _cache["nc"]

    in_maps = []
    for b in range(B):
        m = {"x": np.ascontiguousarray(x_full[b])}
        m.update(wd)
        m.update(cst)
        in_maps.append(m)
    res = run_bass_kernel_spmd(nc, in_maps, core_ids=list(range(8)))
    out = np.stack([res.results[b]["y"] for b in range(B)], axis=0)
    return out.astype(np.float32)


# version 27
# speedup vs baseline: 1.1652x; 1.0046x over previous
import numpy as np
from contextlib import ExitStack

import concourse.bass as bass
import concourse.mybir as mybir
from concourse.bass_utils import run_bass_kernel_spmd

F32 = mybir.dt.float32
BF16 = mybir.dt.bfloat16
I16 = mybir.dt.int16
ALU = mybir.AluOpType
AF = mybir.ActivationFunctionType
AX = mybir.AxisListType

B, S, D = 8, 2048, 1024
H, HD = 16, 64
I3 = 3072
NT = S // 128
NC = D // 128
NI = I3 // 128
MAGIC = 12582912.0
QMAX = 32767.0
EPS = 1e-5

import os
STAGE = int(os.environ.get("MK_STAGE", "4"))
SUBST = float(os.environ.get("MK_SUBST", "99"))


def host_constants():
    c = {}
    inv = 1.0 / (10000.0 ** (np.arange(0, HD, 2, dtype=np.float64) / HD))
    ang = np.outer(np.arange(S, dtype=np.float64), inv)
    cosv, sinv = np.cos(ang), np.sin(ang)
    cc = np.zeros((128, S), dtype=np.float64)
    ss = np.zeros((128, S), dtype=np.float64)
    for p in range(128):
        i = (p % 64) // 2
        cc[p] = cosv[:, i]
        ss[p] = sinv[:, i] * (1.0 if (p % 2) else -1.0)
    c["cc_tab"] = cc.astype(np.float32)
    c["ss_tab"] = ss.astype(np.float32)
    c["ident"] = np.eye(128, dtype=np.float32)
    psw = np.zeros((128, 128), dtype=np.float32)
    for m in range(128):
        psw[m ^ 1, m] = 1.0
    c["pswap"] = psw
    mn = np.zeros((128, 4, 512), dtype=np.float32)
    for d in range(4):
        for p in range(128):
            mn[p, d, : min(512, p + 128 * d)] = -1.0e9
    c["mneg"] = mn.reshape(128, 4 * 512)
    oneh = np.zeros((128, NC, 16), dtype=np.float32)
    for ch in range(NC):
        oneh[0:64, ch, 2 * ch] = 1.0
        oneh[64:128, ch, 2 * ch + 1] = 1.0
    c["oneh"] = oneh.reshape(128, NC * 16)
    mr = np.zeros((64, 544), dtype=np.float32)
    for kk in range(32):
        mr[kk, : 32 * (kk // 2) + (kk % 2) + 30] = -1.0e9
    mr[32, :] = -1.0e9
    c["mskr"] = np.ascontiguousarray(np.vstack([mr] * 2))
    ml = np.zeros((64, 4 * 128), dtype=np.float32)
    for jq in range(4):
        for j in range(128):
            mm = j % 64 - 16 * jq
            if 0 <= mm < 16:
                ml[2 * mm + (j // 64), 128 * jq + j] = 1.0
            elif mm >= 16:
                ml[32, 128 * jq + j] = 1.0
    c["mskl"] = np.ascontiguousarray(np.vstack([ml] * 2))
    o2 = np.zeros((128, 2), dtype=np.float32)
    o2[0:64, 0] = 1.0
    o2[64:128, 1] = 1.0
    c["ones2"] = o2
    return c


ENGINES = ("sync", "tensor", "vector", "scalar", "gpsimd")


class Plan:
    def __init__(self):
        self.items = []

    def op(self, eng, fn):
        self.items.append(("op", eng, fn))

    def t(self, fn): self.op("tensor", fn)
    def v(self, fn): self.op("vector", fn)
    def s(self, fn): self.op("scalar", fn)
    def g(self, fn): self.op("gpsimd", fn)

    def dma(self, q, fn):
        self.op(q, ("DMA", fn))

    def coll(self, fn):
        self.op("gpsimd", ("COLL", fn))

    def bar(self):
        import sys as _s
        f = _s._getframe(1)
        self.items.append(("bar", f"{f.f_lineno}"))

    def nbars(self):
        return sum(1 for it in self.items if it[0] == "bar")

    def ndma(self, q):
        return sum(1 for it in self.items
                   if it[0] == "op" and it[1] == q
                   and isinstance(it[2], tuple) and it[2][0] == "DMA")


def build_nc():
    nc = bass.Bass("TRN2", target_bir_lowering=False)
    st = ExitStack()

    d_x = nc.dram_tensor("x", [S, D], F32, kind="ExternalInput").ap()
    d_y = nc.dram_tensor("y", [S, D], F32, kind="ExternalOutput").ap()
    d_yt = d_y.rearrange("(t p) d -> p t d", p=128)

    din = {}
    for nm, shp in [
        ("l0_wq", [D, D]), ("l0_wk", [D, D]), ("l0_wv", [D, D]),
        ("l0_wo", [D, D]), ("l0_wob", [D]),
        ("l0_wg", [I3, D]), ("l0_wu", [I3, D]), ("l0_wd", [D, I3]),
        ("l1_qkv", [D, 3 * D]),
        ("l1_bw_t", [D, D]), ("l1_bb", [D]),
        ("l1_pack", [D, 16]), ("l1_rb16", [16]), ("l1_B", [64, D]),
        ("cc_tab", [128, S]), ("ss_tab", [128, S]),
        ("ident", [128, 128]), ("pswap", [128, 128]),
        ("mneg", [128, 4 * 512]), ("oneh", [128, NC * 16]),
        ("mskr", [128, 544]), ("mskl", [128, 4 * 128]),
        ("ones2", [128, 2]),
    ]:
        din[nm] = nc.dram_tensor(nm, shp, F32, kind="ExternalInput").ap()

    d_up = nc.dram_tensor("up_sp", [NI, 128, S], BF16).ap()
    d_gate = nc.dram_tensor("gate_sp", [NI, 128, S], BF16).ap()
    d_t = nc.dram_tensor("t_sp", [NI, 128, S], F32).ap()
    d_t2 = nc.dram_tensor("t2_sp", [NI, 128, S], F32).ap()
    d_wdT = nc.dram_tensor("wdT_sp", [NI, 128, D], BF16).ap()
    d_agi = [nc.dram_tensor(f"agi{k}", [1, 2], F32).ap() for k in range(4)]
    d_ago = [nc.dram_tensor(f"ago{k}", [8, 2], F32).ap() for k in range(4)]

    SB_BASE = 0x4400

    def at(name, off, shape, dtype):
        return nc.alloc_sbuf_tensor_at(name, shape, dtype, offset=SB_BASE + off)

    A0, B0, C0, D0, E0 = 0x0000, 0x8000, 0x10000, 0x18000, 0x20000
    F0, G0, H0, H1 = 0x28200, 0x2C200, 0x2E200, 0x30200
    K0 = 0x32200

    XNT = at("XNT", A0, [128, NT, 1024], BF16)
    OB_A = at("OB_A", A0, [128, 8, S], BF16)
    CCT = at("CCT", A0, [128, S], BF16)
    SST = at("SST", A0 + 0x1000, [128, S], BF16)
    SQS = at("SQS", A0 + 0x2000, [128, NC, 512], BF16)
    ROPET = at("ROPET", A0 + 0x4000, [128, 4, 512], BF16)
    ROPET2 = at("ROPET2", A0 + 0x5000, [128, 4, 512], BF16)
    SQS2 = at("SQS2", A0 + 0x6000, [128, NC, 512], BF16)
    HT = at("HT", A0, [128, NT, 1024], BF16)
    KTTA = at("KTTA", A0, [128, 8, 2048], BF16)
    XNF = at("XNF", B0, [128, NC, S], BF16)
    HOUT = at("HOUT", B0, [128, NT, 1024], BF16)
    LNX = at("LNX", C0, [128, NT, 1024], F32)
    QF = at("QF", C0, [128, NC, S], BF16)
    EV0 = at("EV0", C0 + 0x0000, [128, S], F32)
    EV1 = at("EV1", C0 + 0x2000, [128, S], F32)
    EV2 = at("EV2", C0 + 0x4000, [128, S], F32)
    EB0 = at("EB0", C0 + 0x6000, [128, S], BF16)
    EB1 = at("EB1", C0 + 0x7000, [128, S], BF16)
    EV5 = at("EV5", A0 + 0x0000, [128, S], F32)
    EV6 = at("EV6", A0 + 0x2000, [128, S], F32)
    EV7 = at("EV7", A0 + 0x4000, [128, S], F32)
    EVC = at("EVC", C0 + 0x6000, [128, S], F32)
    KF = at("KF", D0, [128, NC, S], BF16)
    KTTD = at("KTTD", D0, [128, 8, 2048], BF16)
    XQF = at("XQF", D0, [128, NC, S], BF16)
    EV3 = at("EV3", D0 + 0x0000, [128, S], F32)
    EV4 = at("EV4", D0 + 0x2000, [128, S], F32)
    GQ = [at(f"GQ{i}", D0 + 0x4000 + i * 0x1000, [128, S], I16) for i in range(3)]
    OB_D = at("OB_D", D0, [128, 8, S], BF16)
    VA = at("VA", E0, [128, NT, H, 65], BF16)
    OB_E = at("OB_E", E0, [128, 8, S], BF16)
    ZF = at("ZF", E0 + 0x4000, [64, NT, 128], BF16)
    WBL = at("WBL", E0 + 0x5000, [64, 1024], BF16)
    WUF0 = at("WUF0", F0, [128, 1024], F32)
    WUF1 = at("WUF1", F0 + 0x1000, [128, 1024], F32)
    WQT = at("WQT", F0 + 0x2000, [128, 1024], F32)
    WUB0 = at("WUB0", F0 + 0x3000, [128, 1024], BF16)
    WUB1 = at("WUB1", F0 + 0x3800, [128, 1024], BF16)
    WOBR = at("WOBR", F0, [1, 1024], F32)
    BBR = at("BBR", F0 + 0x1000, [1, 1024], F32)
    MNEG = at("MNEG", F0 + 0x2000, [128, 4, 512], BF16)
    PTS = at("PTS", F0 + 0x3000, [128, 4, 512], BF16)
    LG = at("LG", F0 + 0x3800, [128, NT, 16], F32)
    GT = at("GT", F0 + 0x3C00, [128, NT, 8], BF16)
    M2T = at("M2T", F0 + 0x3D00, [128, NT, 8], F32)
    SQKQ = at("SQKQ", F0, [128, NC, 512], BF16)
    SQKQ2 = at("SQKQ2", F0 + 0x2000, [128, NC, 512], BF16)
    WLT0 = at("WLT0", G0, [128, 8, 128], BF16)
    WLT1 = at("WLT1", G0 + 0x800, [128, 8, 128], BF16)
    WDR0 = at("WDR0", G0, [128, 1024], BF16)
    WDR1 = at("WDR1", G0 + 0x800, [128, 1024], BF16)
    EVG = at("EVG", G0 + 0x1000, [128, 1024], F32)
    EVG2 = at("EVG2", G0 + 0x0000, [128, 1024], F32)
    WB = at("WB", H0, [128, NC, 512], BF16)
    WB2 = at("WB2", H1, [128, NC, 512], BF16)
    WDR2 = at("WDR2", H0, [128, 1024], BF16)
    WDR3 = at("WDR3", H0 + 0x800, [128, 1024], BF16)
    IDB = at("IDB", K0, [128, 128], BF16)
    IDF = at("IDF", K0 + 0x100, [128, 128], F32)
    PSW = at("PSW", K0 + 0x300, [128, 128], BF16)
    ONESF = at("ONESF", K0 + 0x400, [1, 128], F32)
    ONEHS = at("ONEHS", K0 + 0x600, [128, NC, 16], BF16)
    WPK = at("WPK", K0 + 0x700, [128, NC, 16], BF16)
    RB16 = at("RB16", K0 + 0x800, [1, 16], F32)
    SC = at("SC", K0 + 0x880, [128, 32], F32)
    SR = at("SR", K0 + 0x900, [1, 64], F32)
    SX = at("SX", K0 + 0xA00, [128, 16], F32)
    SX2 = at("SX2", K0 + 0xA40, [128, 16], F32)
    SM1 = at("SM1", K0 + 0xA80, [128, 16], F32)
    SM2 = at("SM2", K0 + 0xAC0, [128, 16], F32)
    SM3 = at("SM3", K0 + 0xB00, [128, 16], F32)
    SM4 = at("SM4", K0 + 0xB40, [128, 16], F32)
    UXM = at("UXM", K0 + 0xB80, [128, 24], F32)
    GXM = at("GXM", K0 + 0xBE0, [128, 24], F32)
    TXM = at("TXM", K0 + 0xC40, [128, 24], F32)
    T2XM = at("T2XM", K0 + 0xCA0, [128, 24], F32)
    AGR = at("AGR", K0 + 0xD00, [1, 16], F32)
    RCP = at("RCP", K0 + 0xD40, [128, 8], F32)
    MSKR = at("MSKR", K0 + 0xD60, [128, 544], BF16)
    MSKL = at("MSKL", K0 + 0x1460, [128, 4 * 128], BF16)
    ONES2 = at("ONES2", K0 + 0x1400, [128, 2], BF16)
    K2M = at("K2M", K0 + 0x1420, [128, 16], F32)
    assert 0x4400 + K0 + 0x18A0 <= 0x38000

    PB = [st.enter_context(nc.psum_tensor(f"PB{i}", [128, 512], F32))
          for i in range(8)]
    PBB = [pb.bitcast(BF16) for pb in PB]

    phase_sems = {e: st.enter_context(nc.semaphore(f"ph_{e}")) for e in ENGINES}
    dma_sems = {q: st.enter_context(nc.semaphore(f"dq_{q}"))
                for q in ("sync", "gpsimd", "scalar")}
    cc_sem = st.enter_context(nc.semaphore("ccs"))
    sem_sc = st.enter_context(nc.semaphore("sc_sc"))
    sem_ex = st.enter_context(nc.semaphore("sc_ex"))
    sem_av = st.enter_context(nc.semaphore("sc_av"))
    sem_ev = st.enter_context(nc.semaphore("sc_ev"))
    sem_dd = st.enter_context(nc.semaphore("sc_dd"))

    plan = Plan()

    def DMA1(q, out, in_):
        def f(e, sem, out=out, in_=in_):
            e.dma_start(out=out, in_=in_).then_inc(sem, 16)
            return 1
        plan.dma(q, f)

    def DMAT(q, out, in_):
        def f(e, sem, out=out, in_=in_):
            e.dma_start_transpose(out, in_).then_inc(sem, 16)
            return 1
        plan.dma(q, f)

    def DMAACC(out, in_, sem2=None, q="gpsimd"):
        def f(e, sem, out=out, in_=in_, sem2=sem2):
            if sem2 is not None:
                e.dma_start(out=out, in_=in_, accum_op=ALU.add).then_inc(sem2, 16)
                return 0
            e.dma_start(out=out, in_=in_, accum_op=ALU.add).then_inc(sem, 16)
            return 1
        plan.dma(q, f)

    def scalar_from_col(col_ap, dst_idx):
        plan.t(lambda e, c=col_ap: e.transpose(PB[6][0:1, 0:128], c, IDF[:, :]))
        plan.bar()

        def vf(e, i=dst_idx):
            e.reduce_max(SR[0:1, i:i + 1], PB[6][0:1, 0:128], axis=AX.X)
        plan.v(vf)
        plan.bar()

    def bcast_scalar(src_idx, dst_col):
        plan.t(lambda e, i=src_idx: e.matmul(
            PB[6][:, 0:1], ONESF[0:1, :], SR[0:1, i:i + 1], start=True, stop=True))
        plan.bar()

        def vf(e, d=dst_col):
            e.tensor_copy(SC[:, d:d + 1], PB[6][:, 0:1])
        plan.v(vf)
        plan.bar()

    def emit_allgather(k):
        DMA1("gpsimd", d_agi[k], SR[0:1, 0:2])
        plan.bar()
        plan.coll(lambda e, k=k: e.collective_compute(
            "AllGather", ALU.bypass, replica_groups=[list(range(8))],
            ins=[d_agi[k]], outs=[d_ago[k]]))
        plan.bar()
        DMA1("gpsimd", AGR[0:1, 0:16], d_ago[k])
        plan.bar()

    AG2 = AGR.ap()[0:1, 0:16].rearrange("a (r v) -> a r v", v=2)

    DMA1("gpsimd", IDB[:, :], din["ident"])
    DMA1("sync", IDF[:, :], din["ident"])
    DMA1("gpsimd", PSW[:, :], din["pswap"])
    DMA1("gpsimd", ONEHS.ap().rearrange("p c m -> p (c m)"), din["oneh"])

    def initc(e):
        e.memset(ONESF[:, :], 1.0)
        e.memset(SR[:, :], 0.0)
    plan.v(initc)
    plan.bar()

    def emit_transpose_tok_to_fm(src_tok, dst_fm):
        for w in range(NT + 1):
            if w < NT:
                pbb = PBB[w % 2]

                def tw(e, w=w, pbb=pbb):
                    for c in range(NC):
                        e.transpose(pbb[:, 128 * c:128 * c + 128],
                                    src_tok[:, w, 128 * c:128 * c + 128],
                                    IDB[:, :])
                plan.t(tw)
            wv = w - 1
            if wv >= 0:
                pbb = PBB[wv % 2]

                def ew(e, wv=wv, pbb=pbb):
                    for c in range(NC):
                        e.copy(dst_fm[:, c, 128 * wv:128 * wv + 128],
                               pbb[:, 128 * c:128 * c + 128])
                plan.s(ew)
            plan.bar()

    def emit_layernorm(src_tiled, copy_y=False):
        for t in range(NT):
            DMA1("sync" if t % 2 == 0 else "gpsimd", LNX[:, t, :], src_tiled[:, t, :])
        plan.bar()
        if copy_y:
            for t in range(NT):
                DMA1("gpsimd" if t % 2 == 0 else "scalar",
                     d_yt[:, t, :], LNX[:, t, :])

        def stats(e):
            for t in range(NT):
                e.reduce_sum(SX[:, t:t + 1], LNX[:, t, :], axis=AX.X)
                e.tensor_mul(EVG[:, :], LNX[:, t, :], LNX[:, t, :])
                e.reduce_sum(SX2[:, t:t + 1], EVG[:, :], axis=AX.X)
        plan.v(stats)
        plan.bar()

        def smalls(e):
            e.tensor_scalar(out=SM1[:, :], in0=SX[:, :], scalar1=1.0 / D,
                            scalar2=None, op0=ALU.mult)
            e.tensor_scalar(out=SM2[:, :], in0=SX2[:, :], scalar1=1.0 / D,
                            scalar2=None, op0=ALU.mult)
            e.tensor_mul(SM3[:, :], SM1[:, :], SM1[:, :])
            e.tensor_sub(SM2[:, :], SM2[:, :], SM3[:, :])
            e.tensor_scalar(out=SM2[:, :], in0=SM2[:, :], scalar1=EPS,
                            scalar2=None, op0=ALU.add)
        plan.v(smalls)
        plan.bar()
        plan.s(lambda e: e.activation(SM3[:, :], SM2[:, :], AF.Sqrt))
        plan.bar()

        def newton(e):
            e.reciprocal(SM4[:, :], SM3[:, :])
            e.tensor_mul(SM3[:, :], SM4[:, :], SM4[:, :])
            e.tensor_mul(SM3[:, :], SM3[:, :], SM2[:, :])
            e.tensor_scalar(out=SM3[:, :], in0=SM3[:, :], scalar1=-0.5,
                            scalar2=1.5, op0=ALU.mult, op1=ALU.add)
            e.tensor_mul(SM4[:, :], SM4[:, :], SM3[:, :])
        plan.v(newton)
        plan.bar()

        def apply(e):
            for t in range(NT):
                e.tensor_scalar(out=XNT[:, t, :], in0=LNX[:, t, :],
                                scalar1=SM1[:, t:t + 1], scalar2=SM4[:, t:t + 1],
                                op0=ALU.subtract, op1=ALU.mult)
        plan.v(apply)
        plan.bar()
        emit_transpose_tok_to_fm(XNT, XNF)

    def dequant_v(e, wsrc, wdst):
        w3 = wsrc.ap().rearrange("p (b k) -> p b k", k=32)
        e.reduce_max(EVG[:, 0:32], w3, axis=AX.X, apply_absolute_value=True)
        e.tensor_scalar(out=EVG[:, 32:64], in0=EVG[:, 0:32], scalar1=1.0 / 7.0,
                        scalar2=1e-12, op0=ALU.mult, op1=ALU.max)
        e.reciprocal(EVG[:, 64:96], EVG[:, 32:64])
        e.tensor_mul(WQT.ap().rearrange("p (b k) -> p b k", k=32),
                     w3, EVG.ap()[:, 64:96, None].broadcast_to([128, 32, 32]))
        e.tensor_scalar(out=WQT[:, :], in0=WQT[:, :], scalar1=MAGIC,
                        scalar2=MAGIC, op0=ALU.add, op1=ALU.subtract)
        e.tensor_mul(wdst.ap().rearrange("p (b k) -> p b k", k=32),
                     WQT.ap().rearrange("p (b k) -> p b k", k=32),
                     EVG.ap()[:, 32:64, None].broadcast_to([128, 32, 32]))

    def emit_upgate(w_dram, spill, maxcol):
        for m in range(NI + 5):
            if m < NI:
                DMA1("sync", (WUF0 if m % 2 == 0 else WUF1)[:, :],
                     w_dram[128 * m:128 * m + 128, :])
            md = m - 1
            if 0 <= md < NI:
                plan.v(lambda e, md=md: dequant_v(
                    e, WUF0 if md % 2 == 0 else WUF1,
                    WUB0 if md % 2 == 0 else WUB1))
            mt = m - 2
            if 0 <= mt < NI:
                wsrc = WUB0 if mt % 2 == 0 else WUB1
                wdst = WLT0 if mt % 2 == 0 else WLT1
                for c in range(NC):
                    DMAT("scalar", wdst[:, c, :], wsrc[:, 128 * c:128 * c + 128])
            mm_ = m - 3
            if 0 <= mm_ < NI:
                wsrc = WLT0 if mm_ % 2 == 0 else WLT1
                pbase = (mm_ % 2) * 4

                def mf(e, wsrc=wsrc, pbase=pbase):
                    for nt in range(4):
                        for kc in range(NC):
                            e.matmul(PB[pbase + nt][:, 0:512], wsrc[:, kc, :],
                                     XQF[:, kc, 512 * nt:512 * nt + 512],
                                     start=(kc == 0), stop=(kc == NC - 1))
                if not os.environ.get("MK_NOMM"):
                    plan.t(mf)
            me = m - 4
            if 0 <= me < NI:
                pbase = (me % 2) * 4
                ebuf = EB0 if me % 2 == 0 else EB1

                def evf(e, pbase=pbase, ebuf=ebuf):
                    for nt in range(4):
                        e.copy(ebuf[:, 512 * nt:512 * nt + 512],
                               PB[pbase + nt][:, 0:512])
                def evfv(e, pbase=pbase, ebuf=ebuf):
                    for nt in range(4):
                        e.tensor_copy(ebuf[:, 512 * nt:512 * nt + 512],
                                      PB[pbase + nt][:, 0:512])
                if os.environ.get("MK_EVACS"):
                    plan.s(evf)
                elif not os.environ.get("MK_NOEVAC"):
                    plan.v(evfv)
            ms = m - 5
            if 0 <= ms < NI:
                ebuf = EB0 if ms % 2 == 0 else EB1

                def rf(e, ebuf=ebuf, ms=ms, maxcol=maxcol):
                    e.reduce_max(maxcol[:, ms:ms + 1], ebuf[:, :], axis=AX.X,
                                 apply_absolute_value=True)
                if not os.environ.get("MK_NORF"):
                    plan.v(rf)
                if not os.environ.get("MK_NOSPILL"):
                    DMA1("gpsimd", spill[ms, :, :], ebuf[:, :])
            plan.bar()

    def emit_wd_dequant():
        for m in range(NI + 5):
            if m < NI:
                for r in range(8):
                    DMA1("sync",
                         (WUF0 if m % 2 == 0 else WUF1)[:, 128 * r:128 * r + 128],
                         din["l0_wd"][128 * r:128 * r + 128, 128 * m:128 * m + 128])
            md = m - 1
            if 0 <= md < NI:
                plan.v(lambda e, md=md: dequant_v(
                    e, WUF0 if md % 2 == 0 else WUF1,
                    WUB0 if md % 2 == 0 else WUB1))
            mt = m - 2
            if 0 <= mt < NI:
                wsrc = WUB0 if mt % 2 == 0 else WUB1

                def twd(e, wsrc=wsrc, mt=mt):
                    pbb = PBB[mt % 2]
                    for r in range(8):
                        e.transpose(pbb[:, 128 * r:128 * r + 128],
                                    wsrc[:, 128 * r:128 * r + 128], IDB[:, :])
                plan.t(twd)
            mw = m - 3
            if 0 <= mw < NI:
                wdst = WLT0 if mw % 2 == 0 else WLT1

                def ewd(e, wdst=wdst, mw=mw):
                    pbb = PBB[mw % 2]
                    for r in range(8):
                        e.copy(wdst[:, r, :], pbb[:, 128 * r:128 * r + 128])
                plan.s(ewd)
            msv = m - 4
            if 0 <= msv < NI:
                wsrc = WLT0 if msv % 2 == 0 else WLT1
                DMA1("gpsimd", d_wdT[msv, :, :],
                     wsrc.ap().rearrange("p r k -> p (r k)"))
            plan.bar()

    attn_cnt = {"sc": 0, "ex": 0}

    def emit_attn_l0_ubound():
        for gi in range(10):
            def vup(e, gi=gi):
                if gi < 8:
                    qsrc = QF if gi < 4 else KF
                    nt = gi % 4
                    sq = SQS if gi % 2 == 0 else SQS2
                    for c in range(NC):
                        e.tensor_mul(sq[:, c, :],
                                     qsrc[:, c, 512 * nt:512 * nt + 512],
                                     qsrc[:, c, 512 * nt:512 * nt + 512])
                gr = gi - 2
                if gr >= 0:
                    e.reduce_max(SX[0:16, gr:gr + 1],
                                 PB[4 if gr % 2 == 0 else 5][0:16, 0:512],
                                 axis=AX.X)
            plan.v(vup)
            gm = gi - 1
            if 0 <= gm < 8:
                def msq(e, gm=gm):
                    sq = SQS if gm % 2 == 0 else SQS2
                    pb = PB[4 if gm % 2 == 0 else 5]
                    for c in range(NC):
                        e.matmul(pb[0:16, 0:512], ONEHS[:, c, :], sq[:, c, :],
                                 start=(c == 0), stop=(c == NC - 1))
                plan.t(msq)
            plan.bar()

        def vredu(e):
            e.reduce_max(SC[0:16, 20:21], SX[0:16, 0:4], axis=AX.X)
            e.reduce_max(SC[0:16, 21:22], SX[0:16, 4:8], axis=AX.X)
            e.tensor_mul(SC[0:16, 22:23], SC[0:16, 20:21], SC[0:16, 21:22])
        plan.v(vredu)
        plan.bar()
        plan.t(lambda e: e.transpose(PB[6][0:1, 0:16], SC[0:16, 22:23],
                                     IDF[0:16, 0:16]))
        plan.bar()

        def vU(e):
            e.reduce_max(SR[0:1, 20:21], PB[6][0:1, 0:16], axis=AX.X)
        plan.v(vU)
        plan.bar()
        plan.s(lambda e: e.activation(SR[0:1, 21:22], SR[0:1, 20:21], AF.Sqrt))
        plan.bar()

        def vnU(e):
            e.tensor_scalar(out=SR[0:1, 22:23], in0=SR[0:1, 21:22],
                            scalar1=-0.125, scalar2=None, op0=ALU.mult)
        plan.v(vnU)
        plan.bar()
        bcast_scalar(22, 23)

    def emit_proj_fm(dst_fm, w_ap, col0, rope, qscale=None):
        for half in range(2):
            DMA1("gpsimd", (WB if half == 0 else WB2).ap(),
                 w_ap[:, col0 + 512 * half: col0 + 512 * (half + 1)]
                 .rearrange("(c p) n -> p c n", p=128))
        plan.bar()
        for gmc in range(9):
            if gmc < 8:
                wb = WB if gmc < 4 else WB2
                mc = gmc % 4
                pbase = (gmc % 2) * 4

                def mf(e, wb=wb, mc=mc, pbase=pbase):
                    for nt in range(4):
                        for kc in range(NC):
                            e.matmul(PB[pbase + nt][:, 0:512],
                                     wb[:, kc, 128 * mc:128 * mc + 128],
                                     XNF[:, kc, 512 * nt:512 * nt + 512],
                                     start=(kc == 0), stop=(kc == NC - 1))
                plan.t(mf)
            ge = gmc - 1
            if ge >= 0:
                pbase = (ge % 2) * 4

                def evf(e, pbase=pbase, ge=ge, dst=dst_fm, qs=qscale):
                    for nt in range(4):
                        if qs is None:
                            e.copy(dst[:, ge, 512 * nt:512 * nt + 512],
                                   PB[pbase + nt][:, 0:512])
                        else:
                            e.activation(dst[:, ge, 512 * nt:512 * nt + 512],
                                         PB[pbase + nt][:, 0:512], AF.Copy,
                                         scale=qs)
                plan.s(evf)
            plan.bar()
        if rope:
            for c in range(NC + 1):
                if c < NC:
                    def mro(e, c=c, dst=dst_fm):
                        pbase = (c % 2) * 4
                        for nt in range(4):
                            e.matmul(PB[pbase + nt][:, 0:512], PSW[:, :],
                                     dst[:, c, 512 * nt:512 * nt + 512],
                                     start=True, stop=True)
                    plan.t(mro)

                def vro(e, c=c, dst=dst_fm):
                    if c < NC:
                        t1 = ROPET if c % 2 == 0 else ROPET2
                        for nt in range(4):
                            e.tensor_mul(t1[:, nt, :],
                                         dst[:, c, 512 * nt:512 * nt + 512],
                                         CCT[:, 512 * nt:512 * nt + 512])
                    cm = c - 1
                    if cm >= 0:
                        t1m = ROPET if cm % 2 == 0 else ROPET2
                        pbase = (cm % 2) * 4
                        for nt in range(4):
                            e.tensor_mul(EVG[:, 0:512], PB[pbase + nt][:, 0:512],
                                         SST[:, 512 * nt:512 * nt + 512])
                            e.tensor_add(dst[:, cm, 512 * nt:512 * nt + 512],
                                         t1m[:, nt, :], EVG[:, 0:512])
                plan.v(vro)
                plan.bar()

    def _pipe_tok(mm_fn, ev_fn):
        for tt in range(NT + 1):
            if tt < NT:
                plan.t(lambda e, tt=tt: mm_fn(e, tt))
            if tt - 1 >= 0:
                plan.s(lambda e, tt=tt: ev_fn(e, tt - 1))
            plan.bar()

    def _kt_scatter(e, tt, swapped):
        pb0, pb1 = PB[(tt % 2) * 2], PB[(tt % 2) * 2 + 1]
        for b5, pb in ((0, pb0), (1, pb1)):
            srcv = pb.ap()[:, 0:512].rearrange(
                "p (g par ih il d) -> p par ih il g d",
                g=4, par=2, ih=2, il=8, d=4)
            for ph2 in range(2):
                r = 2 * tt + ((1 - ph2) if swapped else ph2)
                for ih, buf in ((0, KTTD), (1, KTTA)):
                    s4 = srcv[:, ph2, ih]
                    dst = buf.ap().rearrange(
                        "p s (G p2 d r) -> p s G p2 d r",
                        G=8, p2=2, d=4, r=32)
                    e.copy(dst[:, :, 4 * b5:4 * b5 + 4, ph2, :, r], s4)

    def emit_proj_tok_KT():
        def mm_fn(e, tt):
            pb0, pb1 = PB[(tt % 2) * 2], PB[(tt % 2) * 2 + 1]
            for kc in range(NC):
                e.matmul(pb0[:, 0:512], XNF[:, kc, 128 * tt:128 * tt + 128],
                         WB[:, kc, 0:512], start=(kc == 0), stop=(kc == NC - 1))
            for kc in range(NC):
                e.matmul(pb1[:, 0:512], XNF[:, kc, 128 * tt:128 * tt + 128],
                         WB2[:, kc, 0:512], start=(kc == 0), stop=(kc == NC - 1))

        def ev_fn(e, tt):
            _kt_scatter(e, tt, swapped=False)
        _pipe_tok(mm_fn, ev_fn)

    def emit_proj_tok_KTS():
        def mm_fn(e, tt):
            pb0, pb1 = PB[(tt % 2) * 2], PB[(tt % 2) * 2 + 1]
            for half in range(2):
                src_sl = slice(128 * tt + 64 * (1 - half),
                               128 * tt + 64 * (1 - half) + 64)
                for kc in range(NC):
                    e.matmul(pb0[64 * half:64 * half + 64, 0:512],
                             XNF[:, kc, src_sl], WB[:, kc, 0:512],
                             start=(kc == 0), stop=(kc == NC - 1))
                for kc in range(NC):
                    e.matmul(pb1[64 * half:64 * half + 64, 0:512],
                             XNF[:, kc, src_sl], WB2[:, kc, 0:512],
                             start=(kc == 0), stop=(kc == NC - 1))

        def ev_fn(e, tt):
            _kt_scatter(e, tt, swapped=True)
        _pipe_tok(mm_fn, ev_fn)

    def emit_proj_tok_VS():
        def mm_fn(e, blk):
            pb0, pb1 = PB[(blk % 2) * 2], PB[(blk % 2) * 2 + 1]
            for slq in range(2):
                lh = XNF.ap()[:, :, 2 * blk + slq: 2 * blk + slq + 32 * 63 + 1:32]
                for kc in range(NC):
                    e.matmul(pb0[64 * slq:64 * slq + 64, 0:512], lh[:, kc, :],
                             WB[:, kc, 0:512], start=(kc == 0), stop=(kc == NC - 1))
                for kc in range(NC):
                    e.matmul(pb1[64 * slq:64 * slq + 64, 0:512], lh[:, kc, :],
                             WB2[:, kc, 0:512], start=(kc == 0), stop=(kc == NC - 1))

        def ev_fn(e, blk):
            pb0, pb1 = PB[(blk % 2) * 2], PB[(blk % 2) * 2 + 1]
            e.copy(VA[:, blk, 0:8, 0:64], pb0[:, 0:512])
            e.copy(VA[:, blk, 8:16, 0:64], pb1[:, 0:512])
        _pipe_tok(mm_fn, ev_fn)

    def emit_attn_l1s_ubound():
        for gi in range(6):
            def vup(e, gi=gi):
                if gi < 4:
                    sq = SQKQ if gi % 2 == 0 else SQKQ2
                    for c in range(NC):
                        e.tensor_mul(sq[:, c, :],
                                     QF[:, c, 512 * gi:512 * gi + 512],
                                     QF[:, c, 512 * gi:512 * gi + 512])
                gr = gi - 2
                if gr >= 0:
                    e.reduce_max(SX[0:16, gr:gr + 1],
                                 PB[4 if gr % 2 == 0 else 5][0:16, 0:512],
                                 axis=AX.X)
            plan.v(vup)
            gm = gi - 1
            if 0 <= gm < 4:
                def msq(e, gm=gm):
                    sq = SQKQ if gm % 2 == 0 else SQKQ2
                    pb = PB[4 if gm % 2 == 0 else 5]
                    for c in range(NC):
                        e.matmul(pb[0:16, 0:512], ONEHS[:, c, :], sq[:, c, :],
                                 start=(c == 0), stop=(c == NC - 1))
                plan.t(msq)
            plan.bar()
        for tt in range(NT + 2):
            def vkp(e, tt=tt):
                if tt < NT:
                    sq = SQKQ if tt % 2 == 0 else SQKQ2
                    kd = (KTTD if tt < 8 else KTTA).ap()[:, tt % 8, :] \
                        .rearrange("p (a b) -> p a b", b=256)
                    for c4, cl in ((0, 0), (1, 192)):
                        e.tensor_mul(sq[:, c4, :].rearrange(
                                         "p (a b) -> p a b", b=64),
                                     kd[:, :, cl:cl + 64], kd[:, :, cl:cl + 64])
                tr = tt - 2
                if tr >= 0:
                    e.reduce_max(K2M[0:2, tr:tr + 1],
                                 PB[5 if tr % 2 == 0 else 6][0:2, 0:512],
                                 axis=AX.X)
            plan.v(vkp)
            tm = tt - 1
            if 0 <= tm < NT:
                def mks(e, tm=tm):
                    sq = SQKQ if tm % 2 == 0 else SQKQ2
                    pb = PB[5 if tm % 2 == 0 else 6]
                    e.matmul(pb[0:2, 0:512], ONES2[:, :], sq[:, 0, :],
                             start=True, stop=False)
                    e.matmul(pb[0:2, 0:512], ONES2[:, :], sq[:, 1, :],
                             start=False, stop=True)
                plan.t(mks)
            plan.bar()

        def vkred(e):
            e.reduce_max(SC[0:16, 20:21], SX[0:16, 0:4], axis=AX.X)
            e.reduce_max(SC[0:2, 21:22], K2M[0:2, 0:16], axis=AX.X)
        plan.v(vkred)
        plan.bar()
        plan.t(lambda e: e.transpose(PB[6][0:1, 0:16], SC[0:16, 20:21],
                                     IDF[0:16, 0:16]))
        plan.bar()

        def vq2(e):
            e.reduce_max(SR[0:1, 20:21], PB[6][0:1, 0:16], axis=AX.X)
        plan.v(vq2)
        plan.bar()
        plan.t(lambda e: e.transpose(PB[6][0:1, 0:2], SC[0:2, 21:22],
                                     IDF[0:2, 0:2]))
        plan.bar()

        def vk2(e):
            e.reduce_max(SR[0:1, 21:22], PB[6][0:1, 0:2], axis=AX.X)
            e.tensor_mul(SR[0:1, 22:23], SR[0:1, 20:21], SR[0:1, 21:22])
        plan.v(vk2)
        plan.bar()
        plan.s(lambda e: e.activation(SR[0:1, 23:24], SR[0:1, 22:23], AF.Sqrt))
        plan.bar()

        def vnU(e):
            e.tensor_scalar(out=SR[0:1, 24:25], in0=SR[0:1, 23:24],
                            scalar1=-1.0, scalar2=None, op0=ALU.mult)
        plan.v(vnU)
        plan.bar()
        bcast_scalar(24, 23)

    def emit_attention_cz(kslice, out_buf):
        blocks = [(h, jq) for h in range(H) for jq in range(4)]
        nb = len(blocks)
        av_b = attn_cnt.setdefault("av", 0)
        ev_b = attn_cnt.setdefault("ev", 0)
        attn_cnt["av"] += nb
        attn_cnt["ev"] += nb
        for bi in range(nb + 2):
            if bi < nb:
                h, jq = blocks[bi]
                n_kt = 4 * jq + 4
                pav = PB[4 + bi % 3]
                sc_b = attn_cnt["sc"]
                ex_b = attn_cnt["ex"]
                attn_cnt["sc"] += n_kt
                attn_cnt["ex"] += n_kt

                def tf(e, h=h, jq=jq, bi=bi, n_kt=n_kt, pav=pav,
                       sc_b=sc_b, ex_b=ex_b):
                    hb = 64 * (h % 2)

                    def av(j, start, stop):
                        for cx in range(4):
                            mm = e.matmul(pav[:, 65 * cx:65 * cx + 65],
                                          PTS[:, j % 4, 128 * cx:128 * cx + 128],
                                          VA[:, j, h, :],
                                          start=(start and cx == 0),
                                          stop=(stop and cx == 3))
                        return mm

                    for i in range(n_kt):
                        ps = PB[i % 4]
                        if ex_b + i - 3 > 0:
                            e.wait_ge(sem_ex, ex_b + i - 3)
                        d = i - 4 * jq
                        if d >= 0:
                            e.matmul(ps[0:128, 0:512], kslice(h, i),
                                     QF[hb:hb + 64, h // 2,
                                        512 * jq:512 * jq + 512],
                                     start=True, stop=False)
                            e.matmul(ps[0:128, 0:512], IDB[:, :], MNEG[:, d, :],
                                     start=False, stop=True).then_inc(sem_sc, 1)
                        else:
                            e.matmul(ps[0:128, 0:512], kslice(h, i),
                                     QF[hb:hb + 64, h // 2,
                                        512 * jq:512 * jq + 512],
                                     start=True, stop=True).then_inc(sem_sc, 1)
                        if i >= 3:
                            if i == 3 and bi >= 3:
                                e.wait_ge(sem_ev, ev_b + bi - 2)
                            e.wait_ge(sem_ex, ex_b + i - 2)
                            av(i - 3, start=(i == 3), stop=False)
                    for j in (n_kt - 3, n_kt - 2, n_kt - 1):
                        e.wait_ge(sem_ex, ex_b + j + 1)
                        mm = av(j, start=False, stop=(j == n_kt - 1))
                    mm.then_inc(sem_av, 1)
                plan.t(tf)

                def sf(e, n_kt=n_kt, sc_b=sc_b):
                    for i in range(n_kt):
                        e.wait_ge(sem_sc, sc_b + i + 1)
                        e.activation(PTS[:, i % 4, :], PB[i % 4][:, 0:512], AF.Exp,
                                     bias=SC[:, 23:24], scale=1.0
                                     ).then_inc(sem_ex, 1)
                plan.s(sf)
            bj = bi - 1
            if 0 <= bj < nb:
                hj, jqj = blocks[bj]
                pavj = PB[4 + bj % 3]
                pav3 = pavj.ap()[:, 0:260].rearrange("p (c j) -> p c j", c=4)
                rcpj = RCP[:, 0:4] if bj % 2 == 0 else RCP[:, 4:8]
                hslot = out_buf.ap()[:, 4 * jqj:4 * jqj + 4,
                                     64 * hj:64 * hj + 64]

                def ev(e, bj=bj, pav3=pav3, rcpj=rcpj, hslot=hslot):
                    e.wait_ge(sem_av, av_b + bj + 1)
                    e.reciprocal(rcpj, pav3[:, :, 64:65])
                    e.tensor_mul(hslot, pav3[:, :, 0:64],
                                 rcpj[:, :, None].broadcast_to([128, 4, 64]))
                    e.drain().then_inc(sem_ev, 1)
                plan.v(ev)
        plan.bar()

    def emit_attention_l1s():
        emit_attention_cz(
            lambda h, i: (KTTD if i < 8 else KTTA)
            [64 * (h % 2):64 * (h % 2) + 64, i % 8, 128 * h:128 * h + 128],
            HOUT)
        for tt in range(NT):
            DMAACC(d_yt[:, tt, :], HOUT[:, tt, :])
        plan.bar()

    def emit_proj_tok_V():
        def mm_fn(e, tt):
            pb0, pb1 = PB[(tt % 2) * 2], PB[(tt % 2) * 2 + 1]
            for kc in range(NC):
                e.matmul(pb0[:, 0:512], XNF[:, kc, 128 * tt:128 * tt + 128],
                         WB[:, kc, 0:512], start=(kc == 0), stop=(kc == NC - 1))
            for kc in range(NC):
                e.matmul(pb1[:, 0:512], XNF[:, kc, 128 * tt:128 * tt + 128],
                         WB2[:, kc, 0:512], start=(kc == 0), stop=(kc == NC - 1))

        def ev_fn(e, tt):
            pb0, pb1 = PB[(tt % 2) * 2], PB[(tt % 2) * 2 + 1]
            e.copy(VA[:, tt, 0:8, 0:64], pb0[:, 0:512])
            e.copy(VA[:, tt, 8:16, 0:64], pb1[:, 0:512])
        _pipe_tok(mm_fn, ev_fn)

    def load_wb_halves(w_ap, col0):
        DMA1("gpsimd", WB.ap(),
             w_ap[:, col0:col0 + 512].rearrange("(c p) n -> p c n", p=128))
        DMA1("gpsimd", WB2.ap(),
             w_ap[:, col0 + 512:col0 + 1024].rearrange("(c p) n -> p c n", p=128))

    DMA1("gpsimd", d_y, d_x)
    plan.bar()
    if SUBST >= 0.2:
        emit_layernorm(d_x.rearrange("(t p) d -> p t d", p=128))

    def _noop(e):
        pass

    def xnmax(e):
        for t in range(NT):
            e.reduce_max(SX[:, t:t + 1], XNT[:, t, :], axis=AX.X,
                         apply_absolute_value=True)
        e.reduce_max(SC[:, 0:1], SX[:, :], axis=AX.X)
    if SUBST >= 0.3:
        plan.v(xnmax)
        plan.bar()
        scalar_from_col(SC[:, 0:1], 0)
        plan.v(lambda e: (e.tensor_copy(SR[0:1, 1:2], SR[0:1, 0:1]), e.drain()))
        plan.bar()
        emit_allgather(0)

    def mkscale1(e):
        e.reduce_max(SR[0:1, 0:1], AG2[:, :, 0:1], axis=AX.XY)
        e.tensor_scalar(out=SR[0:1, 1:2], in0=SR[0:1, 0:1], scalar1=1.0 / QMAX,
                        scalar2=1e-12, op0=ALU.mult, op1=ALU.max)
        e.reciprocal(SR[0:1, 2:3], SR[0:1, 1:2])
    if SUBST >= 0.3:
        plan.v(mkscale1)
        plan.bar()
        bcast_scalar(1, 1)
        bcast_scalar(2, 2)

    def qxn(e):
        for c in range(NC):
            e.tensor_scalar(out=EV0[:, :], in0=XNF[:, c, :], scalar1=SC[:, 2:3],
                            scalar2=MAGIC, op0=ALU.mult, op1=ALU.add)
            e.tensor_scalar(out=XQF[:, c, :], in0=EV0[:, :], scalar1=MAGIC,
                            scalar2=SC[:, 1:2], op0=ALU.subtract, op1=ALU.mult)
    if SUBST >= 0.4:
        plan.v(qxn)
        plan.bar()
    if SUBST >= 0.5:
        emit_upgate(din["l0_wu"], d_up, UXM)
    if SUBST >= 0.6:
        emit_upgate(din["l0_wg"], d_gate, GXM)
        emit_wd_dequant()

    def upgmax(e):
        e.reduce_max(SC[:, 0:1], UXM[:, :], axis=AX.X)
        e.reduce_max(SC[:, 3:4], GXM[:, :], axis=AX.X)
    if SUBST < 0.7:
        plan.bar()

        def _finish0(e):
            pass
        plan.v(_finish0)
    _guard = SUBST >= 0.7
    if _guard:
        plan.v(upgmax)
        plan.bar()
        scalar_from_col(SC[:, 0:1], 0)
        scalar_from_col(SC[:, 3:4], 1)
        emit_allgather(1)

    def mkscale23(e):
        e.reduce_max(SR[0:1, 0:1], AG2[:, :, 0:1], axis=AX.XY)
        e.reduce_max(SR[0:1, 3:4], AG2[:, :, 1:2], axis=AX.XY)
        e.tensor_scalar(out=SR[0:1, 1:2], in0=SR[0:1, 0:1], scalar1=1.0 / QMAX,
                        scalar2=1e-12, op0=ALU.mult, op1=ALU.max)
        e.tensor_scalar(out=SR[0:1, 4:5], in0=SR[0:1, 3:4], scalar1=1.0 / QMAX,
                        scalar2=1e-12, op0=ALU.mult, op1=ALU.max)
        e.reciprocal(SR[0:1, 2:3], SR[0:1, 1:2])
        e.reciprocal(SR[0:1, 5:6], SR[0:1, 4:5])
        e.tensor_copy(SR[0:1, 17:18], SR[0:1, 1:2])
    plan.v(mkscale23)
    plan.bar()
    bcast_scalar(1, 3)
    bcast_scalar(2, 4)
    bcast_scalar(4, 5)
    bcast_scalar(5, 6)

    for m in range(NI + 4):
        if m < NI:
            DMA1("sync", (EB0 if m % 2 == 0 else EB1)[:, :], d_gate[m, :, :])
        ma = m - 1
        if 0 <= ma < NI:
            ebuf = EB0 if ma % 2 == 0 else EB1
            gq = GQ[ma % 3]

            def cfa(e, ebuf=ebuf, gq=gq):
                e.tensor_scalar(out=EV5[:, :], in0=ebuf[:, :], scalar1=SC[:, 6:7],
                                scalar2=MAGIC, op0=ALU.mult, op1=ALU.add)
                e.tensor_scalar(out=gq[:, :], in0=EV5[:, :], scalar1=MAGIC,
                                scalar2=None, op0=ALU.subtract)
            plan.v(cfa)
        mb = m - 2
        if 0 <= mb < NI:
            gq = GQ[mb % 3]
            sg = EV3 if mb % 2 == 0 else EV4
            plan.s(lambda e, gq=gq, sg=sg: e.activation(
                sg[:, :], gq[:, :], AF.Sigmoid, scale=SC[:, 5:6]))
        mc_ = m - 3
        if 0 <= mc_ < NI:
            gq = GQ[mc_ % 3]
            sg = EV3 if mc_ % 2 == 0 else EV4
            tout = EV0 if mc_ % 2 == 0 else EV1

            def cfc(e, gq=gq, sg=sg, tout=tout, mc_=mc_):
                e.tensor_scalar(out=EV2[:, :], in0=sg[:, :], scalar1=65536.0,
                                scalar2=MAGIC, op0=ALU.mult, op1=ALU.add)
                e.tensor_scalar(out=EV2[:, :], in0=EV2[:, :], scalar1=MAGIC,
                                scalar2=65535.0, op0=ALU.subtract, op1=ALU.min)
                e.tensor_mul(tout[:, :], gq[:, :], EV2[:, :])
                e.reduce_max(TXM[:, mc_:mc_ + 1], tout[:, :], axis=AX.X,
                             apply_absolute_value=True)
            plan.v(cfc)
        msp = m - 4
        if 0 <= msp < NI:
            tsrc = EV0 if msp % 2 == 0 else EV1
            DMA1("scalar", d_t[msp, :, :], tsrc[:, :])
        plan.bar()

    def tmaxv(e):
        e.reduce_max(SC[:, 0:1], TXM[:, :], axis=AX.X)
    plan.v(tmaxv)
    plan.bar()
    scalar_from_col(SC[:, 0:1], 0)
    plan.v(lambda e: (e.tensor_copy(SR[0:1, 1:2], SR[0:1, 0:1]), e.drain()))
    plan.bar()
    emit_allgather(2)

    def mkscale4(e):
        e.reduce_max(SR[0:1, 0:1], AG2[:, :, 0:1], axis=AX.XY)
        e.tensor_scalar(out=SR[0:1, 6:7], in0=SR[0:1, 4:5], scalar1=1.0 / 65536.0,
                        scalar2=None, op0=ALU.mult)
        e.tensor_mul(SR[0:1, 7:8], SR[0:1, 0:1], SR[0:1, 6:7])
        e.tensor_scalar(out=SR[0:1, 11:12], in0=SR[0:1, 7:8], scalar1=1.0 / QMAX,
                        scalar2=1e-12, op0=ALU.mult, op1=ALU.max)
        e.reciprocal(SR[0:1, 18:19], SR[0:1, 11:12])
        e.tensor_mul(SR[0:1, 12:13], SR[0:1, 6:7], SR[0:1, 18:19])
    plan.v(mkscale4)
    plan.bar()
    bcast_scalar(11, 7)
    bcast_scalar(12, 8)

    EVT3 = (EV3, EV4, EV2)
    EV56 = (EV5, EV6)
    for m in range(NI + 4):
        if m < NI:
            DMA1("sync", (EV0 if m % 2 == 0 else EV1)[:, :], d_t[m, :, :])
            DMA1("scalar", (EB0 if m % 2 == 0 else EB1)[:, :], d_up[m, :, :])
        ma = m - 1
        if 0 <= ma < NI:
            tsrc = EV0 if ma % 2 == 0 else EV1
            usrc = EB0 if ma % 2 == 0 else EB1

            def cfb1(e, tsrc=tsrc, tb=EVT3[ma % 3]):
                e.tensor_scalar(out=tb[:, :], in0=tsrc[:, :], scalar1=SC[:, 8:9],
                                scalar2=MAGIC, op0=ALU.mult, op1=ALU.add)
            plan.v(cfb1)

            def cfb2(e, usrc=usrc, ub=EV56[ma % 2]):
                e.tensor_scalar(out=ub[:, :], in0=usrc[:, :], scalar1=SC[:, 4:5],
                                scalar2=MAGIC, op0=ALU.mult, op1=ALU.add)
            plan.g(cfb2)
        mb = m - 2
        if 0 <= mb < NI:

            def cfb3(e, tb=EVT3[mb % 3], ub=EV56[mb % 2], mb=mb):
                e.tensor_scalar(out=EV7[:, :], in0=ub[:, :], scalar1=MAGIC,
                                scalar2=None, op0=ALU.subtract)
                e.scalar_tensor_tensor(out=tb[:, :], in0=tb[:, :], scalar=MAGIC,
                                       in1=EV7[:, :], op0=ALU.subtract,
                                       op1=ALU.mult)
                e.reduce_max(T2XM[:, mb:mb + 1], tb[:, :], axis=AX.X,
                             apply_absolute_value=True)
            plan.v(cfb3)
        msp = m - 3
        if 0 <= msp < NI:
            DMA1("scalar", d_t2[msp, :, :], EVT3[msp % 3][:, :])
        plan.bar()

    def t2maxv(e):
        e.reduce_max(SC[:, 0:1], T2XM[:, :], axis=AX.X)
    plan.v(t2maxv)
    plan.bar()
    scalar_from_col(SC[:, 0:1], 0)
    plan.v(lambda e: (e.tensor_copy(SR[0:1, 1:2], SR[0:1, 0:1]), e.drain()))
    plan.bar()
    emit_allgather(3)

    def mkscale5(e):
        e.reduce_max(SR[0:1, 0:1], AG2[:, :, 0:1], axis=AX.XY)
        e.tensor_mul(SR[0:1, 13:14], SR[0:1, 17:18], SR[0:1, 11:12])
        e.tensor_mul(SR[0:1, 14:15], SR[0:1, 0:1], SR[0:1, 13:14])
        e.tensor_scalar(out=SR[0:1, 15:16], in0=SR[0:1, 14:15], scalar1=1.0 / QMAX,
                        scalar2=1e-12, op0=ALU.mult, op1=ALU.max)
        e.reciprocal(SR[0:1, 18:19], SR[0:1, 15:16])
        e.tensor_mul(SR[0:1, 16:17], SR[0:1, 13:14], SR[0:1, 18:19])
    plan.v(mkscale5)
    plan.bar()
    bcast_scalar(15, 9)
    bcast_scalar(16, 10)

    def obf(i):
        if i < 8:
            return OB_A[:, i, :]
        if i < 16:
            return OB_D[:, i - 8, :]
        return OB_E[:, i - 16, :]

    for m in range(NI + 2):
        if m < NI:
            DMA1("sync" if m % 2 == 0 else "scalar",
                 (EV0 if m % 2 == 0 else EV1)[:, :], d_t2[m, :, :])
        ma = m - 1
        if 0 <= ma < NI:
            tsrc = EV0 if ma % 2 == 0 else EV1
            tb = EV2 if ma % 2 == 0 else EVC

            def cfo1(e, tsrc=tsrc, tb=tb):
                e.tensor_scalar(out=tb[:, :], in0=tsrc[:, :], scalar1=SC[:, 10:11],
                                scalar2=MAGIC, op0=ALU.mult, op1=ALU.add)
            plan.g(cfo1)
        mb = m - 2
        if 0 <= mb < NI:
            tb = EV2 if mb % 2 == 0 else EVC

            def cfo2(e, tb=tb, mb=mb):
                e.tensor_scalar(out=obf(mb), in0=tb[:, :], scalar1=MAGIC,
                                scalar2=SC[:, 9:10], op0=ALU.subtract, op1=ALU.mult)
            plan.v(cfo2)
        plan.bar()

    for g in range(4):
        for mi in range(NI // 2 + 1):
            if mi < NI // 2:
                wl = (WDR0, WDR1) if mi % 2 == 0 else (WDR2, WDR3)
                DMA1("sync", wl[0][:, :], d_wdT[2 * mi, :, :])
                DMA1("sync", wl[1][:, :], d_wdT[2 * mi + 1, :, :])
            mmi = mi - 1
            if mmi >= 0:
                wl = (WDR0, WDR1) if mmi % 2 == 0 else (WDR2, WDR3)

                def mfd(e, wl=wl, g=g, mmi=mmi):
                    for k in range(2):
                        mm_ = 2 * mmi + k
                        for tt in range(4):
                            gt = 4 * g + tt
                            for hh in range(2):
                                e.matmul(PB[2 * tt + hh][:, 0:512],
                                         obf(mm_)[:, 128 * gt:128 * gt + 128],
                                         wl[k][:, 512 * hh:512 * hh + 512],
                                         start=(mm_ == 0), stop=(mm_ == NI - 1))
                plan.t(mfd)
            plan.bar()
        for tt in range(4):
            gt = 4 * g + tt

            def evd(e, tt=tt):
                e.copy(EVG[:, 0:512], PB[2 * tt][:, 0:512])
                e.copy(EVG[:, 512:1024], PB[2 * tt + 1][:, 0:512])
            plan.s(evd)
            plan.bar()
            DMAACC(d_yt[:, gt, :], EVG[:, :])
            plan.bar()

    if STAGE >= 2:
        DMA1("gpsimd", CCT[:, :], din["cc_tab"])
        DMA1("gpsimd", SST[:, :], din["ss_tab"])
        DMA1("gpsimd", MNEG.ap().rearrange("p a b -> p (a b)"), din["mneg"])
        plan.bar()
        emit_proj_fm(QF, din["l0_wq"], 0, rope=True, qscale=0.125)
        emit_proj_fm(KF, din["l0_wk"], 0, rope=True)
        load_wb_halves(din["l0_wv"], 0)

        def vone(e):
            e.memset(VA[:, :, :, 64:65], 1.0)
        plan.v(vone)
        plan.bar()
        emit_proj_tok_V()
        print("[kernel] bars before L0 attention:", plan.nbars())
        emit_attn_l0_ubound()
        emit_attention_cz(
            lambda h, i: KF[64 * (h % 2):64 * (h % 2) + 64, h // 2,
                            128 * i:128 * i + 128],
            HT)
        print("[kernel] bars after L0 attention:", plan.nbars())

        DMA1("gpsimd", WOBR[0:1, :], din["l0_wob"])
        load_wb_halves(din["l0_wo"], 0)
        emit_transpose_tok_to_fm(HT, QF)
        for tt in range(NT + 2):
            if tt < NT:
                pb0, pb1 = PB[(tt % 2) * 2], PB[(tt % 2) * 2 + 1]

                def mfo(e, pb0=pb0, pb1=pb1, tt=tt):
                    for kc in range(NC):
                        e.matmul(pb0[:, 0:512], QF[:, kc, 128 * tt:128 * tt + 128],
                                 WB[:, kc, 0:512], start=(kc == 0), stop=False)
                    e.matmul(pb0[:, 0:512], ONESF[0:1, :], WOBR[0:1, 0:512],
                             start=False, stop=True)
                    for kc in range(NC):
                        e.matmul(pb1[:, 0:512], QF[:, kc, 128 * tt:128 * tt + 128],
                                 WB2[:, kc, 0:512], start=(kc == 0), stop=False)
                    e.matmul(pb1[:, 0:512], ONESF[0:1, :], WOBR[0:1, 512:1024],
                             start=False, stop=True)
                plan.t(mfo)
            te = tt - 1
            if 0 <= te < NT:
                pb0, pb1 = PB[(te % 2) * 2], PB[(te % 2) * 2 + 1]
                evg = EVG if te % 2 == 0 else EVG2

                def evo(e, pb0=pb0, pb1=pb1, evg=evg):
                    e.copy(evg[:, 0:512], pb0[:, 0:512])
                    e.copy(evg[:, 512:1024], pb1[:, 0:512])
                plan.s(evo)
            ta = tt - 2
            if ta >= 0:
                evg = EVG if ta % 2 == 0 else EVG2
                DMAACC(d_yt[:, ta, :], evg[:, :])
            plan.bar()

    if STAGE >= 3:
        emit_layernorm(d_yt)

        load_wb_halves(din["l1_bw_t"], 0)
        DMA1("gpsimd", WBL[0:64, :], din["l1_B"])
        DMA1("gpsimd", WPK.ap(),
             din["l1_pack"].rearrange("(c p) n -> p c n", p=128))
        DMA1("sync", RB16[0:1, 0:16], din["l1_rb16"])
        DMA1("sync", BBR[0:1, :], din["l1_bb"])
        plan.bar()

        for tt in range(NT + 1):
            if tt < NT:
                pb = PB[4 + tt % 2]

                def mfl(e, pb=pb, tt=tt):
                    for kc in range(NC):
                        e.matmul(pb[:, 0:16], XNF[:, kc, 128 * tt:128 * tt + 128],
                                 WPK[:, kc, :], start=(kc == 0), stop=False)
                    e.matmul(pb[:, 0:16], ONESF[0:1, :], RB16[0:1, 0:16],
                             start=False, stop=True)
                plan.t(mfl)
            te = tt - 1
            if te >= 0:
                pbe = PB[4 + te % 2]
                plan.s(lambda e, pbe=pbe, te=te: e.copy(LG[:, te, :], pbe[:, 0:16]))
            plan.bar()

        def vmax_all(e):
            for tt in range(NT):
                e.reduce_max(SM1[:, tt:tt + 1], LG[:, tt, 0:8], axis=AX.X,
                             negate=True)
                e.tensor_scalar(out=M2T[:, tt, :], in0=LG[:, tt, 8:16],
                                scalar1=2.0, scalar2=None, op0=ALU.mult)
        plan.v(vmax_all)
        plan.bar()

        def sexp_all(e):
            for tt in range(NT):
                e.activation(LG[:, tt, 0:8], LG[:, tt, 0:8], AF.Exp,
                             bias=SM1[:, tt:tt + 1], scale=1.0,
                             accum_out=SM2[:, tt:tt + 1])
        plan.s(sexp_all)
        plan.bar()

        def vz_all(e):
            e.reciprocal(SM3[:, :], SM2[:, :])
            for tt in range(NT):
                e.tensor_scalar(out=GT[:, tt, :], in0=LG[:, tt, 0:8],
                                scalar1=SM3[:, tt:tt + 1], scalar2=None,
                                op0=ALU.mult)
            for tt in range(NT):
                e.tensor_mul(
                    EVG.ap()[:, 64 * tt:64 * tt + 64]
                    .rearrange("p (a b) -> p a b", a=8),
                    GT[:, tt, :, None].broadcast_to([128, 8, 8]),
                    M2T[:, tt, None, :].broadcast_to([128, 8, 8]))
        plan.v(vz_all)
        plan.bar()

        def tz_all(e):
            for tt in range(NT):
                e.transpose(PB[4 + tt // 4][0:64, 128 * (tt % 4):128 * (tt % 4) + 128],
                            EVG[:, 64 * tt:64 * tt + 64], IDF[:, :])
        plan.t(tz_all)
        plan.bar()

        def sz_all(e):
            for tt in range(NT):
                e.copy(ZF[0:64, tt, :],
                       PB[4 + tt // 4][0:64, 128 * (tt % 4):128 * (tt % 4) + 128])
        plan.s(sz_all)
        plan.bar()

        for tt in range(NT + 2):
            if tt < NT:
                pb0, pb1 = PB[(tt % 2) * 2], PB[(tt % 2) * 2 + 1]

                def mfb(e, pb0=pb0, pb1=pb1, tt=tt):
                    for kc in range(NC):
                        e.matmul(pb0[:, 0:512], XNF[:, kc, 128 * tt:128 * tt + 128],
                                 WB[:, kc, 0:512], start=(kc == 0), stop=False)
                    e.matmul(pb0[:, 0:512], ZF[0:64, tt, :], WBL[0:64, 0:512],
                             start=False, stop=False)
                    e.matmul(pb0[:, 0:512], ONESF[0:1, :], BBR[0:1, 0:512],
                             start=False, stop=True)
                    for kc in range(NC):
                        e.matmul(pb1[:, 0:512], XNF[:, kc, 128 * tt:128 * tt + 128],
                                 WB2[:, kc, 0:512], start=(kc == 0), stop=False)
                    e.matmul(pb1[:, 0:512], ZF[0:64, tt, :], WBL[0:64, 512:1024],
                             start=False, stop=False)
                    e.matmul(pb1[:, 0:512], ONESF[0:1, :], BBR[0:1, 512:1024],
                             start=False, stop=True)
                plan.t(mfb)
            te = tt - 1
            if 0 <= te < NT:
                pb0, pb1 = PB[(te % 2) * 2], PB[(te % 2) * 2 + 1]
                evg = EVG if te % 2 == 0 else EVG2

                def evb(e, pb0=pb0, pb1=pb1, evg=evg):
                    e.copy(evg[:, 0:512], pb0[:, 0:512])
                    e.copy(evg[:, 512:1024], pb1[:, 0:512])
                plan.s(evb)
            ta = tt - 2
            if ta >= 0:
                evg = EVG if ta % 2 == 0 else EVG2
                DMAACC(d_yt[:, ta, :], evg[:, :])
            plan.bar()

    if STAGE >= 4:
        DMA1("gpsimd", ONES2[:, :], din["ones2"])
        plan.bar()
        emit_proj_fm(QF, din["l1_qkv"], 0, rope=False, qscale=0.125)
        load_wb_halves(din["l1_qkv"], 1024)
        plan.bar()
        emit_proj_tok_KT()
        print("[kernel] bars after KT:", plan.nbars())
        emit_proj_tok_KTS()
        print("[kernel] bars after KTS:", plan.nbars())
        emit_attn_l1s_ubound()
        print("[kernel] bars after ubound:", plan.nbars())
        DMA1("gpsimd", MNEG.ap().rearrange("p a b -> p (a b)"), din["mneg"])
        load_wb_halves(din["l1_qkv"], 2048)

        def vone2(e):
            e.memset(VA[:, :, :, 64:65], 1.0)
        plan.v(vone2)
        plan.bar()
        emit_proj_tok_V()
        print("[kernel] bars after V:", plan.nbars())
        emit_attention_l1s()

    plan.bar()

    bar_limit = int(os.environ.get("MK_TRUNC", "999999"))

    def _ap_range(ap):
        try:
            t = ap.tensor
            off = ap.offset
            dims = list(ap.ap)
            if len(dims) >= 2:
                pitch, pcount = dims[0]
                if pitch <= 0:
                    pitch = 1
                p0 = off // pitch
                f0 = off % pitch
                span = 0
                for step, count in dims[1:]:
                    if count > 1:
                        span += abs(step) * (count - 1)
                return (t.name, p0, p0 + pcount, f0, f0 + span + 1)
            span = 0
            for step, count in dims:
                if count > 1:
                    span += abs(step) * (count - 1)
            return (t.name, 0, 1 << 30, off, off + span + 1)
        except Exception:
            return None

    _DRAINSTAT = [0, 0, 0]

    def _is_ap(x):
        return hasattr(x, "tensor") and hasattr(x, "ap")

    class AutoDrain:
        _PASS = ("wait_ge", "nop", "sem_inc", "dma_start",
                 "dma_start_transpose", "collective_compute", "load_library")

        def __init__(self, e):
            object.__setattr__(self, "_e", e)
            object.__setattr__(self, "_hr", [])
            object.__setattr__(self, "_hw", [])

        def drain(self):
            self._hr.clear()
            self._hw.clear()
            return self._e.drain()

        def _emit(self, name, a, args, kw):
            e = self._e
            aps = []
            for i, x in enumerate(args):
                if _is_ap(x):
                    aps.append(("out" if i == 0 else "in", x))
            for k, x in kw.items():
                if _is_ap(x):
                    kind = "out" if (k.startswith("out") or k == "accum_out"
                                     or k == "dest") else "in"
                    aps.append((kind, x))
            writes = [r for r in (_ap_range(x) for k, x in aps if k == "out") if r]
            reads = [r for r in (_ap_range(x) for k, x in aps if k == "in") if r]
            if name == "memset":
                writes, reads = writes + reads, []

            def overlap(r1, r2):
                return (r1[0] == r2[0] and r1[1] < r2[2] and r2[1] < r1[2]
                        and r1[3] < r2[4] and r2[3] < r1[4])

            hr, hw = self._hr, self._hw
            conflict = (any(overlap(w, p) for w in writes for p in hr)
                        or any(overlap(w, p) for w in writes for p in hw)
                        or any(overlap(r, p) for r in reads for p in hw))
            if conflict:
                e.drain()
                _DRAINSTAT[0] += 1
                hr.clear()
                hw.clear()
            _DRAINSTAT[1] += 1
            _DRAINSTAT[2] += len(writes) + len(reads)
            res = a(*args, **kw)
            hr.extend(reads)
            hw.extend(writes)
            return res

        def __getattr__(self, n):
            a = getattr(self._e, n)
            if callable(a) and n not in self._PASS and not n.startswith("_"):
                def w(*args, _n=n, _a=a, **kw):
                    return self._emit(_n, _a, args, kw)
                return w
            return a

    def make_body(eng_name):
        def body(e):
            if eng_name in ("vector", "scalar", "gpsimd"):
                e = AutoDrain(e)
            dma_count = 0
            cc_count = 0
            bar_i = 0
            my_dma_sem = dma_sems.get(eng_name)
            for it in plan.items:
                if bar_i >= bar_limit:
                    break
                if it[0] == "op":
                    _, oeng, fn = it
                    if oeng != eng_name:
                        continue
                    if isinstance(fn, tuple):
                        kind, f = fn
                        if kind == "DMA":
                            dma_count += f(e, my_dma_sem)
                        else:
                            f(e).then_inc(cc_sem, 1)
                            cc_count += 1
                    else:
                        fn(e)
                else:
                    bar_i += 1
                    if my_dma_sem is not None and dma_count > 0:
                        e.wait_ge(my_dma_sem, dma_count * 16)
                    if eng_name == "gpsimd" and cc_count > 0:
                        e.wait_ge(cc_sem, cc_count)
                    e.drain().then_inc(phase_sems[eng_name], 1)
                    for other in ENGINES:
                        if other != eng_name:
                            e.wait_ge(phase_sems[other], bar_i)
        return body

    nbars = sum(1 for it in plan.items if it[0] == "bar")
    print(f"[kernel] plan: {len(plan.items)} items, {nbars} barriers")
    with nc.Block() as block:
        block.sync(make_body("sync"))
        block.tensor(make_body("tensor"))
        block.vector(make_body("vector"))
        block.scalar(make_body("scalar"))
        block.gpsimd(make_body("gpsimd"))
    st.close()
    print("[kernel] proxy: drains", _DRAINSTAT[0], "ops", _DRAINSTAT[1],
          "aps", _DRAINSTAT[2])
    return nc


_cache = {}


def kernel(**inputs):
    cst = host_constants()
    wd = {
        "l0_wq": inputs["L0_attn_wq"],
        "l0_wk": inputs["L0_attn_wk"],
        "l0_wv": inputs["L0_attn_wv"],
        "l0_wo": inputs["L0_attn_wo_w"],
        "l0_wob": inputs["L0_attn_wo_b"],
        "l0_wg": inputs["L0_ffn_w_gate"],
        "l0_wu": inputs["L0_ffn_w_up"],
        "l0_wd": inputs["L0_ffn_w_down"],
        "l1_qkv": inputs["L1_attn_qkv_weights"],
        "l1_bw_t": np.asarray(inputs["L1_ffn_base_w"]).T,
        "l1_bb": inputs["L1_ffn_base_b"],
        "l1_pack": np.concatenate([np.asarray(inputs["L1_ffn_router_w"]).T,
                                   np.asarray(inputs["L1_ffn_lora_A"])], axis=1),
        "l1_rb16": np.concatenate([np.asarray(inputs["L1_ffn_router_b"]),
                                   np.zeros(8, np.float32)]),
        "l1_B": np.asarray(inputs["L1_ffn_lora_B"]).reshape(64, D),
    }
    wd = {k: np.ascontiguousarray(np.asarray(v, dtype=np.float32))
          for k, v in wd.items()}
    x_full = np.ascontiguousarray(np.asarray(inputs["L0_attn_x"], dtype=np.float32))

    if "nc" not in _cache:
        _cache["nc"] = build_nc()
    nc = _cache["nc"]

    in_maps = []
    for b in range(B):
        m = {"x": np.ascontiguousarray(x_full[b])}
        m.update(wd)
        m.update(cst)
        in_maps.append(m)
    res = run_bass_kernel_spmd(nc, in_maps, core_ids=list(range(8)))
    out = np.stack([res.results[b]["y"] for b in range(B)], axis=0)
    return out.astype(np.float32)


# version 28
# speedup vs baseline: 1.1666x; 1.0012x over previous
import numpy as np
from contextlib import ExitStack

import concourse.bass as bass
import concourse.mybir as mybir
from concourse.bass_utils import run_bass_kernel_spmd

F32 = mybir.dt.float32
BF16 = mybir.dt.bfloat16
I16 = mybir.dt.int16
ALU = mybir.AluOpType
AF = mybir.ActivationFunctionType
AX = mybir.AxisListType

B, S, D = 8, 2048, 1024
H, HD = 16, 64
I3 = 3072
NT = S // 128
NC = D // 128
NI = I3 // 128
MAGIC = 12582912.0
QMAX = 32767.0
EPS = 1e-5

import os
STAGE = int(os.environ.get("MK_STAGE", "4"))
SUBST = float(os.environ.get("MK_SUBST", "99"))


def host_constants():
    c = {}
    inv = 1.0 / (10000.0 ** (np.arange(0, HD, 2, dtype=np.float64) / HD))
    ang = np.outer(np.arange(S, dtype=np.float64), inv)
    cosv, sinv = np.cos(ang), np.sin(ang)
    cc = np.zeros((128, S), dtype=np.float64)
    ss = np.zeros((128, S), dtype=np.float64)
    for p in range(128):
        i = (p % 64) // 2
        cc[p] = cosv[:, i]
        ss[p] = sinv[:, i] * (1.0 if (p % 2) else -1.0)
    c["cc_tab"] = cc.astype(np.float32)
    c["ss_tab"] = ss.astype(np.float32)
    c["ident"] = np.eye(128, dtype=np.float32)
    psw = np.zeros((128, 128), dtype=np.float32)
    for m in range(128):
        psw[m ^ 1, m] = 1.0
    c["pswap"] = psw
    mn = np.zeros((128, 4, 512), dtype=np.float32)
    for d in range(4):
        for p in range(128):
            mn[p, d, : min(512, p + 128 * d)] = -1.0e9
    c["mneg"] = mn.reshape(128, 4 * 512)
    oneh = np.zeros((128, NC, 16), dtype=np.float32)
    for ch in range(NC):
        oneh[0:64, ch, 2 * ch] = 1.0
        oneh[64:128, ch, 2 * ch + 1] = 1.0
    c["oneh"] = oneh.reshape(128, NC * 16)
    mr = np.zeros((64, 544), dtype=np.float32)
    for kk in range(32):
        mr[kk, : 32 * (kk // 2) + (kk % 2) + 30] = -1.0e9
    mr[32, :] = -1.0e9
    c["mskr"] = np.ascontiguousarray(np.vstack([mr] * 2))
    ml = np.zeros((64, 4 * 128), dtype=np.float32)
    for jq in range(4):
        for j in range(128):
            mm = j % 64 - 16 * jq
            if 0 <= mm < 16:
                ml[2 * mm + (j // 64), 128 * jq + j] = 1.0
            elif mm >= 16:
                ml[32, 128 * jq + j] = 1.0
    c["mskl"] = np.ascontiguousarray(np.vstack([ml] * 2))
    o2 = np.zeros((128, 2), dtype=np.float32)
    o2[0:64, 0] = 1.0
    o2[64:128, 1] = 1.0
    c["ones2"] = o2
    return c


ENGINES = ("sync", "tensor", "vector", "scalar", "gpsimd")


class Plan:
    def __init__(self):
        self.items = []

    def op(self, eng, fn):
        self.items.append(("op", eng, fn))

    def t(self, fn): self.op("tensor", fn)
    def v(self, fn): self.op("vector", fn)
    def s(self, fn): self.op("scalar", fn)
    def g(self, fn): self.op("gpsimd", fn)

    def dma(self, q, fn):
        self.op(q, ("DMA", fn))

    def coll(self, fn):
        self.op("gpsimd", ("COLL", fn))

    def bar(self):
        import sys as _s
        f = _s._getframe(1)
        self.items.append(("bar", f"{f.f_lineno}"))

    def nbars(self):
        return sum(1 for it in self.items if it[0] == "bar")

    def ndma(self, q):
        return sum(1 for it in self.items
                   if it[0] == "op" and it[1] == q
                   and isinstance(it[2], tuple) and it[2][0] == "DMA")


def build_nc():
    nc = bass.Bass("TRN2", target_bir_lowering=False)
    st = ExitStack()

    d_x = nc.dram_tensor("x", [S, D], F32, kind="ExternalInput").ap()
    d_y = nc.dram_tensor("y", [S, D], F32, kind="ExternalOutput").ap()
    d_yt = d_y.rearrange("(t p) d -> p t d", p=128)

    din = {}
    for nm, shp in [
        ("l0_wq", [D, D]), ("l0_wk", [D, D]), ("l0_wv", [D, D]),
        ("l0_wo", [D, D]), ("l0_wob", [D]),
        ("l0_wg", [I3, D]), ("l0_wu", [I3, D]), ("l0_wd", [D, I3]),
        ("l1_qkv", [D, 3 * D]),
        ("l1_bw_t", [D, D]), ("l1_bb", [D]),
        ("l1_pack", [D, 16]), ("l1_rb16", [16]), ("l1_B", [64, D]),
        ("cc_tab", [128, S]), ("ss_tab", [128, S]),
        ("ident", [128, 128]), ("pswap", [128, 128]),
        ("mneg", [128, 4 * 512]), ("oneh", [128, NC * 16]),
        ("mskr", [128, 544]), ("mskl", [128, 4 * 128]),
        ("ones2", [128, 2]),
    ]:
        din[nm] = nc.dram_tensor(nm, shp, F32, kind="ExternalInput").ap()

    d_up = nc.dram_tensor("up_sp", [NI, 128, S], BF16).ap()
    d_gate = nc.dram_tensor("gate_sp", [NI, 128, S], BF16).ap()
    d_t = nc.dram_tensor("t_sp", [NI, 128, S], F32).ap()
    d_t2 = nc.dram_tensor("t2_sp", [NI, 128, S], F32).ap()
    d_wdT = nc.dram_tensor("wdT_sp", [NI, 128, D], BF16).ap()
    d_agi = [nc.dram_tensor(f"agi{k}", [1, 2], F32).ap() for k in range(4)]
    d_ago = [nc.dram_tensor(f"ago{k}", [8, 2], F32).ap() for k in range(4)]

    SB_BASE = 0x4400

    def at(name, off, shape, dtype):
        return nc.alloc_sbuf_tensor_at(name, shape, dtype, offset=SB_BASE + off)

    A0, B0, C0, D0, E0 = 0x0000, 0x8000, 0x10000, 0x18000, 0x20000
    F0, G0, H0, H1 = 0x28200, 0x2C200, 0x2E200, 0x30200
    K0 = 0x32200

    XNT = at("XNT", A0, [128, NT, 1024], BF16)
    OB_A = at("OB_A", A0, [128, 8, S], BF16)
    CCT = at("CCT", A0, [128, S], BF16)
    SST = at("SST", A0 + 0x1000, [128, S], BF16)
    SQS = at("SQS", A0 + 0x2000, [128, NC, 512], BF16)
    ROPET = at("ROPET", A0 + 0x4000, [128, 4, 512], BF16)
    ROPET2 = at("ROPET2", A0 + 0x5000, [128, 4, 512], BF16)
    SQS2 = at("SQS2", A0 + 0x6000, [128, NC, 512], BF16)
    HT = at("HT", A0, [128, NT, 1024], BF16)
    KTTA = at("KTTA", A0, [128, 8, 2048], BF16)
    XNF = at("XNF", B0, [128, NC, S], BF16)
    HOUT = at("HOUT", B0, [128, NT, 1024], BF16)
    LNX = at("LNX", C0, [128, NT, 1024], F32)
    QF = at("QF", C0, [128, NC, S], BF16)
    EV0 = at("EV0", C0 + 0x0000, [128, S], F32)
    EV1 = at("EV1", C0 + 0x2000, [128, S], F32)
    EV2 = at("EV2", C0 + 0x4000, [128, S], F32)
    EB0 = at("EB0", C0 + 0x6000, [128, S], BF16)
    EB1 = at("EB1", C0 + 0x7000, [128, S], BF16)
    EV5 = at("EV5", A0 + 0x0000, [128, S], F32)
    EV6 = at("EV6", A0 + 0x2000, [128, S], F32)
    EV7 = at("EV7", A0 + 0x4000, [128, S], F32)
    KF = at("KF", D0, [128, NC, S], BF16)
    KTTD = at("KTTD", D0, [128, 8, 2048], BF16)
    XQF = at("XQF", D0, [128, NC, S], BF16)
    EV3 = at("EV3", D0 + 0x0000, [128, S], F32)
    EV4 = at("EV4", D0 + 0x2000, [128, S], F32)
    GQ = [at(f"GQ{i}", D0 + 0x4000 + i * 0x1000, [128, S], I16) for i in range(3)]
    OB_D = at("OB_D", D0, [128, 8, S], BF16)
    VA = at("VA", E0, [128, NT, H, 65], BF16)
    OB_E = at("OB_E", E0, [128, 8, S], BF16)
    ZF = at("ZF", E0 + 0x4000, [64, NT, 128], BF16)
    WBL = at("WBL", E0 + 0x5000, [64, 1024], BF16)
    WUF0 = at("WUF0", F0, [128, 1024], F32)
    WUF1 = at("WUF1", F0 + 0x1000, [128, 1024], F32)
    WQT = at("WQT", F0 + 0x2000, [128, 1024], F32)
    WUB0 = at("WUB0", F0 + 0x3000, [128, 1024], BF16)
    WUB1 = at("WUB1", F0 + 0x3800, [128, 1024], BF16)
    WOBR = at("WOBR", F0, [1, 1024], F32)
    BBR = at("BBR", F0 + 0x1000, [1, 1024], F32)
    MNEG = at("MNEG", F0 + 0x2000, [128, 4, 512], BF16)
    PTS = at("PTS", F0 + 0x3000, [128, 4, 512], BF16)
    LG = at("LG", F0 + 0x3800, [128, NT, 16], F32)
    GT = at("GT", F0 + 0x3C00, [128, NT, 8], BF16)
    M2T = at("M2T", F0 + 0x3D00, [128, NT, 8], F32)
    SQKQ = at("SQKQ", F0, [128, NC, 512], BF16)
    SQKQ2 = at("SQKQ2", F0 + 0x2000, [128, NC, 512], BF16)
    WLT0 = at("WLT0", G0, [128, 8, 128], BF16)
    WLT1 = at("WLT1", G0 + 0x800, [128, 8, 128], BF16)
    WDR0 = at("WDR0", G0, [128, 1024], BF16)
    WDR1 = at("WDR1", G0 + 0x800, [128, 1024], BF16)
    EVG = at("EVG", G0 + 0x1000, [128, 1024], F32)
    EVG2 = at("EVG2", G0 + 0x0000, [128, 1024], F32)
    WB = at("WB", H0, [128, NC, 512], BF16)
    WB2 = at("WB2", H1, [128, NC, 512], BF16)
    WDR2 = at("WDR2", H0, [128, 1024], BF16)
    WDR3 = at("WDR3", H0 + 0x800, [128, 1024], BF16)
    IDB = at("IDB", K0, [128, 128], BF16)
    IDF = at("IDF", K0 + 0x100, [128, 128], F32)
    PSW = at("PSW", K0 + 0x300, [128, 128], BF16)
    ONESF = at("ONESF", K0 + 0x400, [1, 128], F32)
    ONEHS = at("ONEHS", K0 + 0x600, [128, NC, 16], BF16)
    WPK = at("WPK", K0 + 0x700, [128, NC, 16], BF16)
    RB16 = at("RB16", K0 + 0x800, [1, 16], F32)
    SC = at("SC", K0 + 0x880, [128, 32], F32)
    SR = at("SR", K0 + 0x900, [1, 64], F32)
    SX = at("SX", K0 + 0xA00, [128, 16], F32)
    SX2 = at("SX2", K0 + 0xA40, [128, 16], F32)
    SM1 = at("SM1", K0 + 0xA80, [128, 16], F32)
    SM2 = at("SM2", K0 + 0xAC0, [128, 16], F32)
    SM3 = at("SM3", K0 + 0xB00, [128, 16], F32)
    SM4 = at("SM4", K0 + 0xB40, [128, 16], F32)
    UXM = at("UXM", K0 + 0xB80, [128, 24], F32)
    GXM = at("GXM", K0 + 0xBE0, [128, 24], F32)
    TXM = at("TXM", K0 + 0xC40, [128, 24], F32)
    T2XM = at("T2XM", K0 + 0xCA0, [128, 24], F32)
    AGR = at("AGR", K0 + 0xD00, [1, 16], F32)
    RCP = at("RCP", K0 + 0xD40, [128, 8], F32)
    MSKR = at("MSKR", K0 + 0xD60, [128, 544], BF16)
    MSKL = at("MSKL", K0 + 0x1460, [128, 4 * 128], BF16)
    ONES2 = at("ONES2", K0 + 0x1400, [128, 2], BF16)
    K2M = at("K2M", K0 + 0x1420, [128, 16], F32)
    assert 0x4400 + K0 + 0x18A0 <= 0x38000

    PB = [st.enter_context(nc.psum_tensor(f"PB{i}", [128, 512], F32))
          for i in range(8)]
    PBB = [pb.bitcast(BF16) for pb in PB]

    phase_sems = {e: st.enter_context(nc.semaphore(f"ph_{e}")) for e in ENGINES}
    dma_sems = {q: st.enter_context(nc.semaphore(f"dq_{q}"))
                for q in ("sync", "gpsimd", "scalar")}
    cc_sem = st.enter_context(nc.semaphore("ccs"))
    sem_sc = st.enter_context(nc.semaphore("sc_sc"))
    sem_ex = st.enter_context(nc.semaphore("sc_ex"))
    sem_av = st.enter_context(nc.semaphore("sc_av"))
    sem_ev = st.enter_context(nc.semaphore("sc_ev"))
    sem_dd = st.enter_context(nc.semaphore("sc_dd"))

    plan = Plan()

    def DMA1(q, out, in_):
        def f(e, sem, out=out, in_=in_):
            e.dma_start(out=out, in_=in_).then_inc(sem, 16)
            return 1
        plan.dma(q, f)

    def DMAT(q, out, in_):
        def f(e, sem, out=out, in_=in_):
            e.dma_start_transpose(out, in_).then_inc(sem, 16)
            return 1
        plan.dma(q, f)

    def DMAACC(out, in_, sem2=None, q="gpsimd"):
        def f(e, sem, out=out, in_=in_, sem2=sem2):
            if sem2 is not None:
                e.dma_start(out=out, in_=in_, accum_op=ALU.add).then_inc(sem2, 16)
                return 0
            e.dma_start(out=out, in_=in_, accum_op=ALU.add).then_inc(sem, 16)
            return 1
        plan.dma(q, f)

    def scalar_from_col(col_ap, dst_idx):
        plan.t(lambda e, c=col_ap: e.transpose(PB[6][0:1, 0:128], c, IDF[:, :]))
        plan.bar()

        def vf(e, i=dst_idx):
            e.reduce_max(SR[0:1, i:i + 1], PB[6][0:1, 0:128], axis=AX.X)
        plan.v(vf)
        plan.bar()

    def bcast_scalar(src_idx, dst_col):
        plan.t(lambda e, i=src_idx: e.matmul(
            PB[6][:, 0:1], ONESF[0:1, :], SR[0:1, i:i + 1], start=True, stop=True))
        plan.bar()

        def vf(e, d=dst_col):
            e.tensor_copy(SC[:, d:d + 1], PB[6][:, 0:1])
        plan.v(vf)
        plan.bar()

    def emit_allgather(k):
        DMA1("gpsimd", d_agi[k], SR[0:1, 0:2])
        plan.bar()
        plan.coll(lambda e, k=k: e.collective_compute(
            "AllGather", ALU.bypass, replica_groups=[list(range(8))],
            ins=[d_agi[k]], outs=[d_ago[k]]))
        plan.bar()
        DMA1("gpsimd", AGR[0:1, 0:16], d_ago[k])
        plan.bar()

    AG2 = AGR.ap()[0:1, 0:16].rearrange("a (r v) -> a r v", v=2)

    DMA1("gpsimd", IDB[:, :], din["ident"])
    DMA1("sync", IDF[:, :], din["ident"])
    DMA1("gpsimd", PSW[:, :], din["pswap"])
    DMA1("gpsimd", ONEHS.ap().rearrange("p c m -> p (c m)"), din["oneh"])

    def initc(e):
        e.memset(ONESF[:, :], 1.0)
        e.memset(SR[:, :], 0.0)
    plan.v(initc)
    plan.bar()

    def emit_transpose_tok_to_fm(src_tok, dst_fm):
        for w in range(NT + 1):
            if w < NT:
                pbb = PBB[w % 2]

                def tw(e, w=w, pbb=pbb):
                    for c in range(NC):
                        e.transpose(pbb[:, 128 * c:128 * c + 128],
                                    src_tok[:, w, 128 * c:128 * c + 128],
                                    IDB[:, :])
                plan.t(tw)
            wv = w - 1
            if wv >= 0:
                pbb = PBB[wv % 2]

                def ew(e, wv=wv, pbb=pbb):
                    for c in range(NC):
                        e.copy(dst_fm[:, c, 128 * wv:128 * wv + 128],
                               pbb[:, 128 * c:128 * c + 128])
                plan.s(ew)
            plan.bar()

    def emit_layernorm(src_tiled, copy_y=False):
        for t in range(NT):
            DMA1("sync" if t % 2 == 0 else "gpsimd", LNX[:, t, :], src_tiled[:, t, :])
        plan.bar()
        if copy_y:
            for t in range(NT):
                DMA1("gpsimd" if t % 2 == 0 else "scalar",
                     d_yt[:, t, :], LNX[:, t, :])

        def stats(e):
            for t in range(NT):
                e.reduce_sum(SX[:, t:t + 1], LNX[:, t, :], axis=AX.X)
                e.tensor_mul(EVG[:, :], LNX[:, t, :], LNX[:, t, :])
                e.reduce_sum(SX2[:, t:t + 1], EVG[:, :], axis=AX.X)
        plan.v(stats)
        plan.bar()

        def smalls(e):
            e.tensor_scalar(out=SM1[:, :], in0=SX[:, :], scalar1=1.0 / D,
                            scalar2=None, op0=ALU.mult)
            e.tensor_scalar(out=SM2[:, :], in0=SX2[:, :], scalar1=1.0 / D,
                            scalar2=None, op0=ALU.mult)
            e.tensor_mul(SM3[:, :], SM1[:, :], SM1[:, :])
            e.tensor_sub(SM2[:, :], SM2[:, :], SM3[:, :])
            e.tensor_scalar(out=SM2[:, :], in0=SM2[:, :], scalar1=EPS,
                            scalar2=None, op0=ALU.add)
        plan.v(smalls)
        plan.bar()
        plan.s(lambda e: e.activation(SM3[:, :], SM2[:, :], AF.Sqrt))
        plan.bar()

        def newton(e):
            e.reciprocal(SM4[:, :], SM3[:, :])
            e.tensor_mul(SM3[:, :], SM4[:, :], SM4[:, :])
            e.tensor_mul(SM3[:, :], SM3[:, :], SM2[:, :])
            e.tensor_scalar(out=SM3[:, :], in0=SM3[:, :], scalar1=-0.5,
                            scalar2=1.5, op0=ALU.mult, op1=ALU.add)
            e.tensor_mul(SM4[:, :], SM4[:, :], SM3[:, :])
        plan.v(newton)
        plan.bar()

        def apply(e):
            for t in range(NT):
                e.tensor_scalar(out=XNT[:, t, :], in0=LNX[:, t, :],
                                scalar1=SM1[:, t:t + 1], scalar2=SM4[:, t:t + 1],
                                op0=ALU.subtract, op1=ALU.mult)
        plan.v(apply)
        plan.bar()
        emit_transpose_tok_to_fm(XNT, XNF)

    def dequant_v(e, wsrc, wdst):
        w3 = wsrc.ap().rearrange("p (b k) -> p b k", k=32)
        e.reduce_max(EVG[:, 0:32], w3, axis=AX.X, apply_absolute_value=True)
        e.tensor_scalar(out=EVG[:, 32:64], in0=EVG[:, 0:32], scalar1=1.0 / 7.0,
                        scalar2=1e-12, op0=ALU.mult, op1=ALU.max)
        e.reciprocal(EVG[:, 64:96], EVG[:, 32:64])
        e.tensor_mul(WQT.ap().rearrange("p (b k) -> p b k", k=32),
                     w3, EVG.ap()[:, 64:96, None].broadcast_to([128, 32, 32]))
        e.tensor_scalar(out=WQT[:, :], in0=WQT[:, :], scalar1=MAGIC,
                        scalar2=MAGIC, op0=ALU.add, op1=ALU.subtract)
        e.tensor_mul(wdst.ap().rearrange("p (b k) -> p b k", k=32),
                     WQT.ap().rearrange("p (b k) -> p b k", k=32),
                     EVG.ap()[:, 32:64, None].broadcast_to([128, 32, 32]))

    def emit_upgate(w_dram, spill, maxcol):
        for m in range(NI + 5):
            if m < NI:
                DMA1("sync", (WUF0 if m % 2 == 0 else WUF1)[:, :],
                     w_dram[128 * m:128 * m + 128, :])
            md = m - 1
            if 0 <= md < NI:
                plan.v(lambda e, md=md: dequant_v(
                    e, WUF0 if md % 2 == 0 else WUF1,
                    WUB0 if md % 2 == 0 else WUB1))
            mt = m - 2
            if 0 <= mt < NI:
                wsrc = WUB0 if mt % 2 == 0 else WUB1
                wdst = WLT0 if mt % 2 == 0 else WLT1
                for c in range(NC):
                    DMAT("scalar", wdst[:, c, :], wsrc[:, 128 * c:128 * c + 128])
            mm_ = m - 3
            if 0 <= mm_ < NI:
                wsrc = WLT0 if mm_ % 2 == 0 else WLT1
                pbase = (mm_ % 2) * 4

                def mf(e, wsrc=wsrc, pbase=pbase):
                    for nt in range(4):
                        for kc in range(NC):
                            e.matmul(PB[pbase + nt][:, 0:512], wsrc[:, kc, :],
                                     XQF[:, kc, 512 * nt:512 * nt + 512],
                                     start=(kc == 0), stop=(kc == NC - 1))
                if not os.environ.get("MK_NOMM"):
                    plan.t(mf)
            me = m - 4
            if 0 <= me < NI:
                pbase = (me % 2) * 4
                ebuf = EB0 if me % 2 == 0 else EB1

                def evf(e, pbase=pbase, ebuf=ebuf):
                    for nt in range(4):
                        e.copy(ebuf[:, 512 * nt:512 * nt + 512],
                               PB[pbase + nt][:, 0:512])
                def evfv(e, pbase=pbase, ebuf=ebuf):
                    for nt in range(4):
                        e.tensor_copy(ebuf[:, 512 * nt:512 * nt + 512],
                                      PB[pbase + nt][:, 0:512])
                if os.environ.get("MK_EVACS"):
                    plan.s(evf)
                elif not os.environ.get("MK_NOEVAC"):
                    plan.v(evfv)
            ms = m - 5
            if 0 <= ms < NI:
                ebuf = EB0 if ms % 2 == 0 else EB1

                def rf(e, ebuf=ebuf, ms=ms, maxcol=maxcol):
                    e.reduce_max(maxcol[:, ms:ms + 1], ebuf[:, :], axis=AX.X,
                                 apply_absolute_value=True)
                if not os.environ.get("MK_NORF"):
                    plan.v(rf)
                if not os.environ.get("MK_NOSPILL"):
                    DMA1("gpsimd", spill[ms, :, :], ebuf[:, :])
            plan.bar()

    def emit_wd_dequant():
        for m in range(NI + 5):
            if m < NI:
                for r in range(8):
                    DMA1("sync",
                         (WUF0 if m % 2 == 0 else WUF1)[:, 128 * r:128 * r + 128],
                         din["l0_wd"][128 * r:128 * r + 128, 128 * m:128 * m + 128])
            md = m - 1
            if 0 <= md < NI:
                plan.v(lambda e, md=md: dequant_v(
                    e, WUF0 if md % 2 == 0 else WUF1,
                    WUB0 if md % 2 == 0 else WUB1))
            mt = m - 2
            if 0 <= mt < NI:
                wsrc = WUB0 if mt % 2 == 0 else WUB1

                def twd(e, wsrc=wsrc, mt=mt):
                    pbb = PBB[mt % 2]
                    for r in range(8):
                        e.transpose(pbb[:, 128 * r:128 * r + 128],
                                    wsrc[:, 128 * r:128 * r + 128], IDB[:, :])
                plan.t(twd)
            mw = m - 3
            if 0 <= mw < NI:
                wdst = WLT0 if mw % 2 == 0 else WLT1

                def ewd(e, wdst=wdst, mw=mw):
                    pbb = PBB[mw % 2]
                    for r in range(8):
                        e.copy(wdst[:, r, :], pbb[:, 128 * r:128 * r + 128])
                plan.s(ewd)
            msv = m - 4
            if 0 <= msv < NI:
                wsrc = WLT0 if msv % 2 == 0 else WLT1
                DMA1("gpsimd", d_wdT[msv, :, :],
                     wsrc.ap().rearrange("p r k -> p (r k)"))
            plan.bar()

    attn_cnt = {"sc": 0, "ex": 0}

    def emit_attn_l0_ubound():
        for gi in range(10):
            def vup(e, gi=gi):
                if gi < 8:
                    qsrc = QF if gi < 4 else KF
                    nt = gi % 4
                    sq = SQS if gi % 2 == 0 else SQS2
                    for c in range(NC):
                        e.tensor_mul(sq[:, c, :],
                                     qsrc[:, c, 512 * nt:512 * nt + 512],
                                     qsrc[:, c, 512 * nt:512 * nt + 512])
                gr = gi - 2
                if gr >= 0:
                    e.reduce_max(SX[0:16, gr:gr + 1],
                                 PB[4 if gr % 2 == 0 else 5][0:16, 0:512],
                                 axis=AX.X)
            plan.v(vup)
            gm = gi - 1
            if 0 <= gm < 8:
                def msq(e, gm=gm):
                    sq = SQS if gm % 2 == 0 else SQS2
                    pb = PB[4 if gm % 2 == 0 else 5]
                    for c in range(NC):
                        e.matmul(pb[0:16, 0:512], ONEHS[:, c, :], sq[:, c, :],
                                 start=(c == 0), stop=(c == NC - 1))
                plan.t(msq)
            plan.bar()

        def vredu(e):
            e.reduce_max(SC[0:16, 20:21], SX[0:16, 0:4], axis=AX.X)
            e.reduce_max(SC[0:16, 21:22], SX[0:16, 4:8], axis=AX.X)
            e.tensor_mul(SC[0:16, 22:23], SC[0:16, 20:21], SC[0:16, 21:22])
        plan.v(vredu)
        plan.bar()
        plan.t(lambda e: e.transpose(PB[6][0:1, 0:16], SC[0:16, 22:23],
                                     IDF[0:16, 0:16]))
        plan.bar()

        def vU(e):
            e.reduce_max(SR[0:1, 20:21], PB[6][0:1, 0:16], axis=AX.X)
        plan.v(vU)
        plan.bar()
        plan.s(lambda e: e.activation(SR[0:1, 21:22], SR[0:1, 20:21], AF.Sqrt))
        plan.bar()

        def vnU(e):
            e.tensor_scalar(out=SR[0:1, 22:23], in0=SR[0:1, 21:22],
                            scalar1=-0.125, scalar2=None, op0=ALU.mult)
        plan.v(vnU)
        plan.bar()
        bcast_scalar(22, 23)

    def emit_proj_fm(dst_fm, w_ap, col0, rope, qscale=None):
        for half in range(2):
            DMA1("gpsimd", (WB if half == 0 else WB2).ap(),
                 w_ap[:, col0 + 512 * half: col0 + 512 * (half + 1)]
                 .rearrange("(c p) n -> p c n", p=128))
        plan.bar()
        for gmc in range(9):
            if gmc < 8:
                wb = WB if gmc < 4 else WB2
                mc = gmc % 4
                pbase = (gmc % 2) * 4

                def mf(e, wb=wb, mc=mc, pbase=pbase):
                    for nt in range(4):
                        for kc in range(NC):
                            e.matmul(PB[pbase + nt][:, 0:512],
                                     wb[:, kc, 128 * mc:128 * mc + 128],
                                     XNF[:, kc, 512 * nt:512 * nt + 512],
                                     start=(kc == 0), stop=(kc == NC - 1))
                plan.t(mf)
            ge = gmc - 1
            if ge >= 0:
                pbase = (ge % 2) * 4

                def evf(e, pbase=pbase, ge=ge, dst=dst_fm, qs=qscale):
                    for nt in range(4):
                        if qs is None:
                            e.copy(dst[:, ge, 512 * nt:512 * nt + 512],
                                   PB[pbase + nt][:, 0:512])
                        else:
                            e.activation(dst[:, ge, 512 * nt:512 * nt + 512],
                                         PB[pbase + nt][:, 0:512], AF.Copy,
                                         scale=qs)
                plan.s(evf)
            plan.bar()
        if rope:
            for c in range(NC + 1):
                if c < NC:
                    def mro(e, c=c, dst=dst_fm):
                        pbase = (c % 2) * 4
                        for nt in range(4):
                            e.matmul(PB[pbase + nt][:, 0:512], PSW[:, :],
                                     dst[:, c, 512 * nt:512 * nt + 512],
                                     start=True, stop=True)
                    plan.t(mro)

                def vro(e, c=c, dst=dst_fm):
                    if c < NC:
                        t1 = ROPET if c % 2 == 0 else ROPET2
                        for nt in range(4):
                            e.tensor_mul(t1[:, nt, :],
                                         dst[:, c, 512 * nt:512 * nt + 512],
                                         CCT[:, 512 * nt:512 * nt + 512])
                    cm = c - 1
                    if cm >= 0:
                        t1m = ROPET if cm % 2 == 0 else ROPET2
                        pbase = (cm % 2) * 4
                        for nt in range(4):
                            e.tensor_mul(EVG[:, 0:512], PB[pbase + nt][:, 0:512],
                                         SST[:, 512 * nt:512 * nt + 512])
                            e.tensor_add(dst[:, cm, 512 * nt:512 * nt + 512],
                                         t1m[:, nt, :], EVG[:, 0:512])
                plan.v(vro)
                plan.bar()

    def _pipe_tok(mm_fn, ev_fn):
        for tt in range(NT + 1):
            if tt < NT:
                plan.t(lambda e, tt=tt: mm_fn(e, tt))
            if tt - 1 >= 0:
                plan.s(lambda e, tt=tt: ev_fn(e, tt - 1))
            plan.bar()

    def _kt_scatter(e, tt, swapped):
        pb0, pb1 = PB[(tt % 2) * 2], PB[(tt % 2) * 2 + 1]
        for b5, pb in ((0, pb0), (1, pb1)):
            srcv = pb.ap()[:, 0:512].rearrange(
                "p (g par ih il d) -> p par ih il g d",
                g=4, par=2, ih=2, il=8, d=4)
            for ph2 in range(2):
                r = 2 * tt + ((1 - ph2) if swapped else ph2)
                for ih, buf in ((0, KTTD), (1, KTTA)):
                    s4 = srcv[:, ph2, ih]
                    dst = buf.ap().rearrange(
                        "p s (G p2 d r) -> p s G p2 d r",
                        G=8, p2=2, d=4, r=32)
                    e.copy(dst[:, :, 4 * b5:4 * b5 + 4, ph2, :, r], s4)

    def emit_proj_tok_KT():
        def mm_fn(e, tt):
            pb0, pb1 = PB[(tt % 2) * 2], PB[(tt % 2) * 2 + 1]
            for kc in range(NC):
                e.matmul(pb0[:, 0:512], XNF[:, kc, 128 * tt:128 * tt + 128],
                         WB[:, kc, 0:512], start=(kc == 0), stop=(kc == NC - 1))
            for kc in range(NC):
                e.matmul(pb1[:, 0:512], XNF[:, kc, 128 * tt:128 * tt + 128],
                         WB2[:, kc, 0:512], start=(kc == 0), stop=(kc == NC - 1))

        def ev_fn(e, tt):
            _kt_scatter(e, tt, swapped=False)
        _pipe_tok(mm_fn, ev_fn)

    def emit_proj_tok_KTS():
        def mm_fn(e, tt):
            pb0, pb1 = PB[(tt % 2) * 2], PB[(tt % 2) * 2 + 1]
            for half in range(2):
                src_sl = slice(128 * tt + 64 * (1 - half),
                               128 * tt + 64 * (1 - half) + 64)
                for kc in range(NC):
                    e.matmul(pb0[64 * half:64 * half + 64, 0:512],
                             XNF[:, kc, src_sl], WB[:, kc, 0:512],
                             start=(kc == 0), stop=(kc == NC - 1))
                for kc in range(NC):
                    e.matmul(pb1[64 * half:64 * half + 64, 0:512],
                             XNF[:, kc, src_sl], WB2[:, kc, 0:512],
                             start=(kc == 0), stop=(kc == NC - 1))

        def ev_fn(e, tt):
            _kt_scatter(e, tt, swapped=True)
        _pipe_tok(mm_fn, ev_fn)

    def emit_proj_tok_VS():
        def mm_fn(e, blk):
            pb0, pb1 = PB[(blk % 2) * 2], PB[(blk % 2) * 2 + 1]
            for slq in range(2):
                lh = XNF.ap()[:, :, 2 * blk + slq: 2 * blk + slq + 32 * 63 + 1:32]
                for kc in range(NC):
                    e.matmul(pb0[64 * slq:64 * slq + 64, 0:512], lh[:, kc, :],
                             WB[:, kc, 0:512], start=(kc == 0), stop=(kc == NC - 1))
                for kc in range(NC):
                    e.matmul(pb1[64 * slq:64 * slq + 64, 0:512], lh[:, kc, :],
                             WB2[:, kc, 0:512], start=(kc == 0), stop=(kc == NC - 1))

        def ev_fn(e, blk):
            pb0, pb1 = PB[(blk % 2) * 2], PB[(blk % 2) * 2 + 1]
            e.copy(VA[:, blk, 0:8, 0:64], pb0[:, 0:512])
            e.copy(VA[:, blk, 8:16, 0:64], pb1[:, 0:512])
        _pipe_tok(mm_fn, ev_fn)

    def emit_attn_l1s_ubound():
        for gi in range(6):
            def vup(e, gi=gi):
                if gi < 4:
                    sq = SQKQ if gi % 2 == 0 else SQKQ2
                    for c in range(NC):
                        e.tensor_mul(sq[:, c, :],
                                     QF[:, c, 512 * gi:512 * gi + 512],
                                     QF[:, c, 512 * gi:512 * gi + 512])
                gr = gi - 2
                if gr >= 0:
                    e.reduce_max(SX[0:16, gr:gr + 1],
                                 PB[4 if gr % 2 == 0 else 5][0:16, 0:512],
                                 axis=AX.X)
            plan.v(vup)
            gm = gi - 1
            if 0 <= gm < 4:
                def msq(e, gm=gm):
                    sq = SQKQ if gm % 2 == 0 else SQKQ2
                    pb = PB[4 if gm % 2 == 0 else 5]
                    for c in range(NC):
                        e.matmul(pb[0:16, 0:512], ONEHS[:, c, :], sq[:, c, :],
                                 start=(c == 0), stop=(c == NC - 1))
                plan.t(msq)
            plan.bar()
        for tt in range(NT + 2):
            def vkp(e, tt=tt):
                if tt < NT:
                    sq = SQKQ if tt % 2 == 0 else SQKQ2
                    kd = (KTTD if tt < 8 else KTTA).ap()[:, tt % 8, :] \
                        .rearrange("p (a b) -> p a b", b=256)
                    for c4, cl in ((0, 0), (1, 192)):
                        e.tensor_mul(sq[:, c4, :].rearrange(
                                         "p (a b) -> p a b", b=64),
                                     kd[:, :, cl:cl + 64], kd[:, :, cl:cl + 64])
                tr = tt - 2
                if tr >= 0:
                    e.reduce_max(K2M[0:2, tr:tr + 1],
                                 PB[5 if tr % 2 == 0 else 6][0:2, 0:512],
                                 axis=AX.X)
            plan.v(vkp)
            tm = tt - 1
            if 0 <= tm < NT:
                def mks(e, tm=tm):
                    sq = SQKQ if tm % 2 == 0 else SQKQ2
                    pb = PB[5 if tm % 2 == 0 else 6]
                    e.matmul(pb[0:2, 0:512], ONES2[:, :], sq[:, 0, :],
                             start=True, stop=False)
                    e.matmul(pb[0:2, 0:512], ONES2[:, :], sq[:, 1, :],
                             start=False, stop=True)
                plan.t(mks)
            plan.bar()

        def vkred(e):
            e.reduce_max(SC[0:16, 20:21], SX[0:16, 0:4], axis=AX.X)
            e.reduce_max(SC[0:2, 21:22], K2M[0:2, 0:16], axis=AX.X)
        plan.v(vkred)
        plan.bar()
        plan.t(lambda e: e.transpose(PB[6][0:1, 0:16], SC[0:16, 20:21],
                                     IDF[0:16, 0:16]))
        plan.bar()

        def vq2(e):
            e.reduce_max(SR[0:1, 20:21], PB[6][0:1, 0:16], axis=AX.X)
        plan.v(vq2)
        plan.bar()
        plan.t(lambda e: e.transpose(PB[6][0:1, 0:2], SC[0:2, 21:22],
                                     IDF[0:2, 0:2]))
        plan.bar()

        def vk2(e):
            e.reduce_max(SR[0:1, 21:22], PB[6][0:1, 0:2], axis=AX.X)
            e.tensor_mul(SR[0:1, 22:23], SR[0:1, 20:21], SR[0:1, 21:22])
        plan.v(vk2)
        plan.bar()
        plan.s(lambda e: e.activation(SR[0:1, 23:24], SR[0:1, 22:23], AF.Sqrt))
        plan.bar()

        def vnU(e):
            e.tensor_scalar(out=SR[0:1, 24:25], in0=SR[0:1, 23:24],
                            scalar1=-1.0, scalar2=None, op0=ALU.mult)
        plan.v(vnU)
        plan.bar()
        bcast_scalar(24, 23)

    def emit_attention_cz(kslice, out_buf):
        blocks = [(h, jq) for h in range(H) for jq in range(4)]
        nb = len(blocks)
        av_b = attn_cnt.setdefault("av", 0)
        ev_b = attn_cnt.setdefault("ev", 0)
        attn_cnt["av"] += nb
        attn_cnt["ev"] += nb
        for bi in range(nb + 2):
            if bi < nb:
                h, jq = blocks[bi]
                n_kt = 4 * jq + 4
                pav = PB[4 + bi % 3]
                sc_b = attn_cnt["sc"]
                ex_b = attn_cnt["ex"]
                attn_cnt["sc"] += n_kt
                attn_cnt["ex"] += n_kt

                def tf(e, h=h, jq=jq, bi=bi, n_kt=n_kt, pav=pav,
                       sc_b=sc_b, ex_b=ex_b):
                    hb = 64 * (h % 2)

                    def av(j, start, stop):
                        for cx in range(4):
                            mm = e.matmul(pav[:, 65 * cx:65 * cx + 65],
                                          PTS[:, j % 4, 128 * cx:128 * cx + 128],
                                          VA[:, j, h, :],
                                          start=(start and cx == 0),
                                          stop=(stop and cx == 3))
                        return mm

                    for i in range(n_kt):
                        ps = PB[i % 4]
                        if ex_b + i - 3 > 0:
                            e.wait_ge(sem_ex, ex_b + i - 3)
                        d = i - 4 * jq
                        if d >= 0:
                            e.matmul(ps[0:128, 0:512], kslice(h, i),
                                     QF[hb:hb + 64, h // 2,
                                        512 * jq:512 * jq + 512],
                                     start=True, stop=False)
                            e.matmul(ps[0:128, 0:512], IDB[:, :], MNEG[:, d, :],
                                     start=False, stop=True).then_inc(sem_sc, 1)
                        else:
                            e.matmul(ps[0:128, 0:512], kslice(h, i),
                                     QF[hb:hb + 64, h // 2,
                                        512 * jq:512 * jq + 512],
                                     start=True, stop=True).then_inc(sem_sc, 1)
                        if i >= 3:
                            if i == 3 and bi >= 3:
                                e.wait_ge(sem_ev, ev_b + bi - 2)
                            e.wait_ge(sem_ex, ex_b + i - 2)
                            av(i - 3, start=(i == 3), stop=False)
                    for j in (n_kt - 3, n_kt - 2, n_kt - 1):
                        e.wait_ge(sem_ex, ex_b + j + 1)
                        mm = av(j, start=False, stop=(j == n_kt - 1))
                    mm.then_inc(sem_av, 1)
                plan.t(tf)

                def sf(e, n_kt=n_kt, sc_b=sc_b):
                    for i in range(n_kt):
                        e.wait_ge(sem_sc, sc_b + i + 1)
                        e.activation(PTS[:, i % 4, :], PB[i % 4][:, 0:512], AF.Exp,
                                     bias=SC[:, 23:24], scale=1.0
                                     ).then_inc(sem_ex, 1)
                plan.s(sf)
            bj = bi - 1
            if 0 <= bj < nb:
                hj, jqj = blocks[bj]
                pavj = PB[4 + bj % 3]
                pav3 = pavj.ap()[:, 0:260].rearrange("p (c j) -> p c j", c=4)
                rcpj = RCP[:, 0:4] if bj % 2 == 0 else RCP[:, 4:8]
                hslot = out_buf.ap()[:, 4 * jqj:4 * jqj + 4,
                                     64 * hj:64 * hj + 64]

                def ev(e, bj=bj, pav3=pav3, rcpj=rcpj, hslot=hslot):
                    e.wait_ge(sem_av, av_b + bj + 1)
                    e.reciprocal(rcpj, pav3[:, :, 64:65])
                    e.tensor_mul(hslot, pav3[:, :, 0:64],
                                 rcpj[:, :, None].broadcast_to([128, 4, 64]))
                    e.drain().then_inc(sem_ev, 1)
                plan.v(ev)
        plan.bar()

    def emit_attention_l1s():
        emit_attention_cz(
            lambda h, i: (KTTD if i < 8 else KTTA)
            [64 * (h % 2):64 * (h % 2) + 64, i % 8, 128 * h:128 * h + 128],
            HOUT)
        for tt in range(NT):
            DMAACC(d_yt[:, tt, :], HOUT[:, tt, :])
        plan.bar()

    def emit_proj_tok_V():
        def mm_fn(e, tt):
            pb0, pb1 = PB[(tt % 2) * 2], PB[(tt % 2) * 2 + 1]
            for kc in range(NC):
                e.matmul(pb0[:, 0:512], XNF[:, kc, 128 * tt:128 * tt + 128],
                         WB[:, kc, 0:512], start=(kc == 0), stop=(kc == NC - 1))
            for kc in range(NC):
                e.matmul(pb1[:, 0:512], XNF[:, kc, 128 * tt:128 * tt + 128],
                         WB2[:, kc, 0:512], start=(kc == 0), stop=(kc == NC - 1))

        def ev_fn(e, tt):
            pb0, pb1 = PB[(tt % 2) * 2], PB[(tt % 2) * 2 + 1]
            e.copy(VA[:, tt, 0:8, 0:64], pb0[:, 0:512])
            e.copy(VA[:, tt, 8:16, 0:64], pb1[:, 0:512])
        _pipe_tok(mm_fn, ev_fn)

    def load_wb_halves(w_ap, col0):
        DMA1("gpsimd", WB.ap(),
             w_ap[:, col0:col0 + 512].rearrange("(c p) n -> p c n", p=128))
        DMA1("gpsimd", WB2.ap(),
             w_ap[:, col0 + 512:col0 + 1024].rearrange("(c p) n -> p c n", p=128))

    DMA1("gpsimd", d_y, d_x)
    plan.bar()
    if SUBST >= 0.2:
        emit_layernorm(d_x.rearrange("(t p) d -> p t d", p=128))

    def _noop(e):
        pass

    def xnmax(e):
        for t in range(NT):
            e.reduce_max(SX[:, t:t + 1], XNT[:, t, :], axis=AX.X,
                         apply_absolute_value=True)
        e.reduce_max(SC[:, 0:1], SX[:, :], axis=AX.X)
    if SUBST >= 0.3:
        plan.v(xnmax)
        plan.bar()
        scalar_from_col(SC[:, 0:1], 0)
        plan.v(lambda e: (e.tensor_copy(SR[0:1, 1:2], SR[0:1, 0:1]), e.drain()))
        plan.bar()
        emit_allgather(0)

    def mkscale1(e):
        e.reduce_max(SR[0:1, 0:1], AG2[:, :, 0:1], axis=AX.XY)
        e.tensor_scalar(out=SR[0:1, 1:2], in0=SR[0:1, 0:1], scalar1=1.0 / QMAX,
                        scalar2=1e-12, op0=ALU.mult, op1=ALU.max)
        e.reciprocal(SR[0:1, 2:3], SR[0:1, 1:2])
    if SUBST >= 0.3:
        plan.v(mkscale1)
        plan.bar()
        bcast_scalar(1, 1)
        bcast_scalar(2, 2)

    def qxn(e):
        for c in range(NC):
            e.tensor_scalar(out=EV0[:, :], in0=XNF[:, c, :], scalar1=SC[:, 2:3],
                            scalar2=MAGIC, op0=ALU.mult, op1=ALU.add)
            e.tensor_scalar(out=XQF[:, c, :], in0=EV0[:, :], scalar1=MAGIC,
                            scalar2=SC[:, 1:2], op0=ALU.subtract, op1=ALU.mult)
    if SUBST >= 0.4:
        plan.v(qxn)
        plan.bar()
    if SUBST >= 0.5:
        emit_upgate(din["l0_wu"], d_up, UXM)
    if SUBST >= 0.6:
        emit_upgate(din["l0_wg"], d_gate, GXM)
        emit_wd_dequant()

    def upgmax(e):
        e.reduce_max(SC[:, 0:1], UXM[:, :], axis=AX.X)
        e.reduce_max(SC[:, 3:4], GXM[:, :], axis=AX.X)
    if SUBST < 0.7:
        plan.bar()

        def _finish0(e):
            pass
        plan.v(_finish0)
    _guard = SUBST >= 0.7
    if _guard:
        plan.v(upgmax)
        plan.bar()
        scalar_from_col(SC[:, 0:1], 0)
        scalar_from_col(SC[:, 3:4], 1)
        emit_allgather(1)

    def mkscale23(e):
        e.reduce_max(SR[0:1, 0:1], AG2[:, :, 0:1], axis=AX.XY)
        e.reduce_max(SR[0:1, 3:4], AG2[:, :, 1:2], axis=AX.XY)
        e.tensor_scalar(out=SR[0:1, 1:2], in0=SR[0:1, 0:1], scalar1=1.0 / QMAX,
                        scalar2=1e-12, op0=ALU.mult, op1=ALU.max)
        e.tensor_scalar(out=SR[0:1, 4:5], in0=SR[0:1, 3:4], scalar1=1.0 / QMAX,
                        scalar2=1e-12, op0=ALU.mult, op1=ALU.max)
        e.reciprocal(SR[0:1, 2:3], SR[0:1, 1:2])
        e.reciprocal(SR[0:1, 5:6], SR[0:1, 4:5])
        e.tensor_copy(SR[0:1, 17:18], SR[0:1, 1:2])
    plan.v(mkscale23)
    plan.bar()
    bcast_scalar(1, 3)
    bcast_scalar(2, 4)
    bcast_scalar(4, 5)
    bcast_scalar(5, 6)

    for m in range(NI + 4):
        if m < NI:
            DMA1("sync", (EB0 if m % 2 == 0 else EB1)[:, :], d_gate[m, :, :])
        ma = m - 1
        if 0 <= ma < NI:
            ebuf = EB0 if ma % 2 == 0 else EB1
            gq = GQ[ma % 3]

            def cfa(e, ebuf=ebuf, gq=gq):
                e.tensor_scalar(out=EV5[:, :], in0=ebuf[:, :], scalar1=SC[:, 6:7],
                                scalar2=MAGIC, op0=ALU.mult, op1=ALU.add)
                e.tensor_scalar(out=gq[:, :], in0=EV5[:, :], scalar1=MAGIC,
                                scalar2=None, op0=ALU.subtract)
            plan.v(cfa)
        mb = m - 2
        if 0 <= mb < NI:
            gq = GQ[mb % 3]
            sg = EV3 if mb % 2 == 0 else EV4
            plan.s(lambda e, gq=gq, sg=sg: e.activation(
                sg[:, :], gq[:, :], AF.Sigmoid, scale=SC[:, 5:6]))
        mc_ = m - 3
        if 0 <= mc_ < NI:
            gq = GQ[mc_ % 3]
            sg = EV3 if mc_ % 2 == 0 else EV4
            tout = EV0 if mc_ % 2 == 0 else EV1

            def cfc(e, gq=gq, sg=sg, tout=tout, mc_=mc_):
                e.tensor_scalar(out=EV2[:, :], in0=sg[:, :], scalar1=65536.0,
                                scalar2=MAGIC, op0=ALU.mult, op1=ALU.add)
                e.tensor_scalar(out=EV2[:, :], in0=EV2[:, :], scalar1=MAGIC,
                                scalar2=65535.0, op0=ALU.subtract, op1=ALU.min)
                e.tensor_mul(tout[:, :], gq[:, :], EV2[:, :])
                e.reduce_max(TXM[:, mc_:mc_ + 1], tout[:, :], axis=AX.X,
                             apply_absolute_value=True)
            plan.v(cfc)
        msp = m - 4
        if 0 <= msp < NI:
            tsrc = EV0 if msp % 2 == 0 else EV1
            DMA1("scalar", d_t[msp, :, :], tsrc[:, :])
        plan.bar()

    def tmaxv(e):
        e.reduce_max(SC[:, 0:1], TXM[:, :], axis=AX.X)
    plan.v(tmaxv)
    plan.bar()
    scalar_from_col(SC[:, 0:1], 0)
    plan.v(lambda e: (e.tensor_copy(SR[0:1, 1:2], SR[0:1, 0:1]), e.drain()))
    plan.bar()
    emit_allgather(2)

    def mkscale4(e):
        e.reduce_max(SR[0:1, 0:1], AG2[:, :, 0:1], axis=AX.XY)
        e.tensor_scalar(out=SR[0:1, 6:7], in0=SR[0:1, 4:5], scalar1=1.0 / 65536.0,
                        scalar2=None, op0=ALU.mult)
        e.tensor_mul(SR[0:1, 7:8], SR[0:1, 0:1], SR[0:1, 6:7])
        e.tensor_scalar(out=SR[0:1, 11:12], in0=SR[0:1, 7:8], scalar1=1.0 / QMAX,
                        scalar2=1e-12, op0=ALU.mult, op1=ALU.max)
        e.reciprocal(SR[0:1, 18:19], SR[0:1, 11:12])
        e.tensor_mul(SR[0:1, 12:13], SR[0:1, 6:7], SR[0:1, 18:19])
    plan.v(mkscale4)
    plan.bar()
    bcast_scalar(11, 7)
    bcast_scalar(12, 8)

    EVT3 = (EV3, EV4, EV2)
    EV56 = (EV5, EV6)
    for m in range(NI + 4):
        if m < NI:
            DMA1("sync", (EV0 if m % 2 == 0 else EV1)[:, :], d_t[m, :, :])
            DMA1("scalar", (EB0 if m % 2 == 0 else EB1)[:, :], d_up[m, :, :])
        ma = m - 1
        if 0 <= ma < NI:
            tsrc = EV0 if ma % 2 == 0 else EV1
            usrc = EB0 if ma % 2 == 0 else EB1

            def cfb1(e, tsrc=tsrc, tb=EVT3[ma % 3]):
                e.tensor_scalar(out=tb[:, :], in0=tsrc[:, :], scalar1=SC[:, 8:9],
                                scalar2=MAGIC, op0=ALU.mult, op1=ALU.add)
            plan.v(cfb1)

            def cfb2(e, usrc=usrc, ub=EV56[ma % 2]):
                e.tensor_scalar(out=ub[:, :], in0=usrc[:, :], scalar1=SC[:, 4:5],
                                scalar2=MAGIC, op0=ALU.mult, op1=ALU.add)
            plan.g(cfb2)
        mb = m - 2
        if 0 <= mb < NI:

            def cfb3(e, tb=EVT3[mb % 3], ub=EV56[mb % 2], mb=mb):
                e.tensor_scalar(out=EV7[:, :], in0=ub[:, :], scalar1=MAGIC,
                                scalar2=None, op0=ALU.subtract)
                e.scalar_tensor_tensor(out=tb[:, :], in0=tb[:, :], scalar=MAGIC,
                                       in1=EV7[:, :], op0=ALU.subtract,
                                       op1=ALU.mult)
                e.reduce_max(T2XM[:, mb:mb + 1], tb[:, :], axis=AX.X,
                             apply_absolute_value=True)
            plan.v(cfb3)
        msp = m - 3
        if 0 <= msp < NI:
            DMA1("scalar", d_t2[msp, :, :], EVT3[msp % 3][:, :])
        plan.bar()

    def t2maxv(e):
        e.reduce_max(SC[:, 0:1], T2XM[:, :], axis=AX.X)
    plan.v(t2maxv)
    plan.bar()
    scalar_from_col(SC[:, 0:1], 0)
    plan.v(lambda e: (e.tensor_copy(SR[0:1, 1:2], SR[0:1, 0:1]), e.drain()))
    plan.bar()
    emit_allgather(3)

    def mkscale5(e):
        e.reduce_max(SR[0:1, 0:1], AG2[:, :, 0:1], axis=AX.XY)
        e.tensor_mul(SR[0:1, 13:14], SR[0:1, 17:18], SR[0:1, 11:12])
        e.tensor_mul(SR[0:1, 14:15], SR[0:1, 0:1], SR[0:1, 13:14])
        e.tensor_scalar(out=SR[0:1, 15:16], in0=SR[0:1, 14:15], scalar1=1.0 / QMAX,
                        scalar2=1e-12, op0=ALU.mult, op1=ALU.max)
        e.reciprocal(SR[0:1, 18:19], SR[0:1, 15:16])
        e.tensor_mul(SR[0:1, 16:17], SR[0:1, 13:14], SR[0:1, 18:19])
    plan.v(mkscale5)
    plan.bar()
    bcast_scalar(15, 9)
    bcast_scalar(16, 10)

    def obf(i):
        if i < 8:
            return OB_A[:, i, :]
        if i < 16:
            return OB_D[:, i - 8, :]
        return OB_E[:, i - 16, :]

    for m in range(NI + 1):
        if m < NI:
            DMA1("sync", (EV0 if m % 2 == 0 else EV1)[:, :], d_t2[m, :, :])
        ma = m - 1
        if 0 <= ma < NI:
            tsrc = EV0 if ma % 2 == 0 else EV1

            def cfo(e, tsrc=tsrc, ma=ma):
                e.tensor_scalar(out=EV2[:, :], in0=tsrc[:, :], scalar1=SC[:, 10:11],
                                scalar2=MAGIC, op0=ALU.mult, op1=ALU.add)
                e.tensor_scalar(out=obf(ma), in0=EV2[:, :], scalar1=MAGIC,
                                scalar2=SC[:, 9:10], op0=ALU.subtract, op1=ALU.mult)
            plan.v(cfo)
        plan.bar()

    for g in range(4):
        for mi in range(NI // 2 + 1):
            if mi < NI // 2:
                wl = (WDR0, WDR1) if mi % 2 == 0 else (WDR2, WDR3)
                DMA1("sync", wl[0][:, :], d_wdT[2 * mi, :, :])
                DMA1("sync", wl[1][:, :], d_wdT[2 * mi + 1, :, :])
            mmi = mi - 1
            if mmi >= 0:
                wl = (WDR0, WDR1) if mmi % 2 == 0 else (WDR2, WDR3)

                def mfd(e, wl=wl, g=g, mmi=mmi):
                    for k in range(2):
                        mm_ = 2 * mmi + k
                        for tt in range(4):
                            gt = 4 * g + tt
                            for hh in range(2):
                                e.matmul(PB[2 * tt + hh][:, 0:512],
                                         obf(mm_)[:, 128 * gt:128 * gt + 128],
                                         wl[k][:, 512 * hh:512 * hh + 512],
                                         start=(mm_ == 0), stop=(mm_ == NI - 1))
                plan.t(mfd)
            plan.bar()
        for tt in range(5):
            if tt < 4:
                evg = EVG if tt % 2 == 0 else EVG2

                def evd(e, tt=tt, evg=evg):
                    e.copy(evg[:, 0:512], PB[2 * tt][:, 0:512])
                    e.copy(evg[:, 512:1024], PB[2 * tt + 1][:, 0:512])
                plan.s(evd)
            ta = tt - 1
            if ta >= 0:
                evga = EVG if ta % 2 == 0 else EVG2
                DMAACC(d_yt[:, 4 * g + ta, :], evga[:, :])
            plan.bar()

    if STAGE >= 2:
        DMA1("gpsimd", CCT[:, :], din["cc_tab"])
        DMA1("gpsimd", SST[:, :], din["ss_tab"])
        DMA1("gpsimd", MNEG.ap().rearrange("p a b -> p (a b)"), din["mneg"])
        plan.bar()
        emit_proj_fm(QF, din["l0_wq"], 0, rope=True, qscale=0.125)
        emit_proj_fm(KF, din["l0_wk"], 0, rope=True)
        load_wb_halves(din["l0_wv"], 0)

        def vone(e):
            e.memset(VA[:, :, :, 64:65], 1.0)
        plan.v(vone)
        plan.bar()
        emit_proj_tok_V()
        print("[kernel] bars before L0 attention:", plan.nbars())
        emit_attn_l0_ubound()
        emit_attention_cz(
            lambda h, i: KF[64 * (h % 2):64 * (h % 2) + 64, h // 2,
                            128 * i:128 * i + 128],
            HT)
        print("[kernel] bars after L0 attention:", plan.nbars())

        DMA1("gpsimd", WOBR[0:1, :], din["l0_wob"])
        load_wb_halves(din["l0_wo"], 0)
        emit_transpose_tok_to_fm(HT, QF)
        for tt in range(NT + 2):
            if tt < NT:
                pb0, pb1 = PB[(tt % 2) * 2], PB[(tt % 2) * 2 + 1]

                def mfo(e, pb0=pb0, pb1=pb1, tt=tt):
                    for kc in range(NC):
                        e.matmul(pb0[:, 0:512], QF[:, kc, 128 * tt:128 * tt + 128],
                                 WB[:, kc, 0:512], start=(kc == 0), stop=False)
                    e.matmul(pb0[:, 0:512], ONESF[0:1, :], WOBR[0:1, 0:512],
                             start=False, stop=True)
                    for kc in range(NC):
                        e.matmul(pb1[:, 0:512], QF[:, kc, 128 * tt:128 * tt + 128],
                                 WB2[:, kc, 0:512], start=(kc == 0), stop=False)
                    e.matmul(pb1[:, 0:512], ONESF[0:1, :], WOBR[0:1, 512:1024],
                             start=False, stop=True)
                plan.t(mfo)
            te = tt - 1
            if 0 <= te < NT:
                pb0, pb1 = PB[(te % 2) * 2], PB[(te % 2) * 2 + 1]
                evg = EVG if te % 2 == 0 else EVG2

                def evo(e, pb0=pb0, pb1=pb1, evg=evg):
                    e.copy(evg[:, 0:512], pb0[:, 0:512])
                    e.copy(evg[:, 512:1024], pb1[:, 0:512])
                plan.s(evo)
            ta = tt - 2
            if ta >= 0:
                evg = EVG if ta % 2 == 0 else EVG2
                DMAACC(d_yt[:, ta, :], evg[:, :])
            plan.bar()

    if STAGE >= 3:
        emit_layernorm(d_yt)

        load_wb_halves(din["l1_bw_t"], 0)
        DMA1("gpsimd", WBL[0:64, :], din["l1_B"])
        DMA1("gpsimd", WPK.ap(),
             din["l1_pack"].rearrange("(c p) n -> p c n", p=128))
        DMA1("sync", RB16[0:1, 0:16], din["l1_rb16"])
        DMA1("sync", BBR[0:1, :], din["l1_bb"])
        plan.bar()

        for tt in range(NT + 1):
            if tt < NT:
                pb = PB[4 + tt % 2]

                def mfl(e, pb=pb, tt=tt):
                    for kc in range(NC):
                        e.matmul(pb[:, 0:16], XNF[:, kc, 128 * tt:128 * tt + 128],
                                 WPK[:, kc, :], start=(kc == 0), stop=False)
                    e.matmul(pb[:, 0:16], ONESF[0:1, :], RB16[0:1, 0:16],
                             start=False, stop=True)
                plan.t(mfl)
            te = tt - 1
            if te >= 0:
                pbe = PB[4 + te % 2]
                plan.s(lambda e, pbe=pbe, te=te: e.copy(LG[:, te, :], pbe[:, 0:16]))
            plan.bar()

        def vmax_all(e):
            for tt in range(NT):
                e.reduce_max(SM1[:, tt:tt + 1], LG[:, tt, 0:8], axis=AX.X,
                             negate=True)
                e.tensor_scalar(out=M2T[:, tt, :], in0=LG[:, tt, 8:16],
                                scalar1=2.0, scalar2=None, op0=ALU.mult)
        plan.v(vmax_all)
        plan.bar()

        def sexp_all(e):
            for tt in range(NT):
                e.activation(LG[:, tt, 0:8], LG[:, tt, 0:8], AF.Exp,
                             bias=SM1[:, tt:tt + 1], scale=1.0,
                             accum_out=SM2[:, tt:tt + 1])
        plan.s(sexp_all)
        plan.bar()

        def vz_all(e):
            e.reciprocal(SM3[:, :], SM2[:, :])
            for tt in range(NT):
                e.tensor_scalar(out=GT[:, tt, :], in0=LG[:, tt, 0:8],
                                scalar1=SM3[:, tt:tt + 1], scalar2=None,
                                op0=ALU.mult)
            for tt in range(NT):
                e.tensor_mul(
                    EVG.ap()[:, 64 * tt:64 * tt + 64]
                    .rearrange("p (a b) -> p a b", a=8),
                    GT[:, tt, :, None].broadcast_to([128, 8, 8]),
                    M2T[:, tt, None, :].broadcast_to([128, 8, 8]))
        plan.v(vz_all)
        plan.bar()

        def tz_all(e):
            for tt in range(NT):
                e.transpose(PB[4 + tt // 4][0:64, 128 * (tt % 4):128 * (tt % 4) + 128],
                            EVG[:, 64 * tt:64 * tt + 64], IDF[:, :])
        plan.t(tz_all)
        plan.bar()

        def sz_all(e):
            for tt in range(NT):
                e.copy(ZF[0:64, tt, :],
                       PB[4 + tt // 4][0:64, 128 * (tt % 4):128 * (tt % 4) + 128])
        plan.s(sz_all)
        plan.bar()

        for tt in range(NT + 2):
            if tt < NT:
                pb0, pb1 = PB[(tt % 2) * 2], PB[(tt % 2) * 2 + 1]

                def mfb(e, pb0=pb0, pb1=pb1, tt=tt):
                    for kc in range(NC):
                        e.matmul(pb0[:, 0:512], XNF[:, kc, 128 * tt:128 * tt + 128],
                                 WB[:, kc, 0:512], start=(kc == 0), stop=False)
                    e.matmul(pb0[:, 0:512], ZF[0:64, tt, :], WBL[0:64, 0:512],
                             start=False, stop=False)
                    e.matmul(pb0[:, 0:512], ONESF[0:1, :], BBR[0:1, 0:512],
                             start=False, stop=True)
                    for kc in range(NC):
                        e.matmul(pb1[:, 0:512], XNF[:, kc, 128 * tt:128 * tt + 128],
                                 WB2[:, kc, 0:512], start=(kc == 0), stop=False)
                    e.matmul(pb1[:, 0:512], ZF[0:64, tt, :], WBL[0:64, 512:1024],
                             start=False, stop=False)
                    e.matmul(pb1[:, 0:512], ONESF[0:1, :], BBR[0:1, 512:1024],
                             start=False, stop=True)
                plan.t(mfb)
            te = tt - 1
            if 0 <= te < NT:
                pb0, pb1 = PB[(te % 2) * 2], PB[(te % 2) * 2 + 1]
                evg = EVG if te % 2 == 0 else EVG2

                def evb(e, pb0=pb0, pb1=pb1, evg=evg):
                    e.copy(evg[:, 0:512], pb0[:, 0:512])
                    e.copy(evg[:, 512:1024], pb1[:, 0:512])
                plan.s(evb)
            ta = tt - 2
            if ta >= 0:
                evg = EVG if ta % 2 == 0 else EVG2
                DMAACC(d_yt[:, ta, :], evg[:, :])
            plan.bar()

    if STAGE >= 4:
        DMA1("gpsimd", ONES2[:, :], din["ones2"])
        plan.bar()
        emit_proj_fm(QF, din["l1_qkv"], 0, rope=False, qscale=0.125)
        load_wb_halves(din["l1_qkv"], 1024)
        plan.bar()
        emit_proj_tok_KT()
        print("[kernel] bars after KT:", plan.nbars())
        emit_proj_tok_KTS()
        print("[kernel] bars after KTS:", plan.nbars())
        emit_attn_l1s_ubound()
        print("[kernel] bars after ubound:", plan.nbars())
        DMA1("gpsimd", MNEG.ap().rearrange("p a b -> p (a b)"), din["mneg"])
        load_wb_halves(din["l1_qkv"], 2048)

        def vone2(e):
            e.memset(VA[:, :, :, 64:65], 1.0)
        plan.v(vone2)
        plan.bar()
        emit_proj_tok_V()
        print("[kernel] bars after V:", plan.nbars())
        emit_attention_l1s()

    plan.bar()

    bar_limit = int(os.environ.get("MK_TRUNC", "999999"))

    def _ap_range(ap):
        try:
            t = ap.tensor
            off = ap.offset
            dims = list(ap.ap)
            if len(dims) >= 2:
                pitch, pcount = dims[0]
                if pitch <= 0:
                    pitch = 1
                p0 = off // pitch
                f0 = off % pitch
                span = 0
                for step, count in dims[1:]:
                    if count > 1:
                        span += abs(step) * (count - 1)
                return (t.name, p0, p0 + pcount, f0, f0 + span + 1)
            span = 0
            for step, count in dims:
                if count > 1:
                    span += abs(step) * (count - 1)
            return (t.name, 0, 1 << 30, off, off + span + 1)
        except Exception:
            return None

    _DRAINSTAT = [0, 0, 0]

    def _is_ap(x):
        return hasattr(x, "tensor") and hasattr(x, "ap")

    class AutoDrain:
        _PASS = ("wait_ge", "nop", "sem_inc", "dma_start",
                 "dma_start_transpose", "collective_compute", "load_library")

        def __init__(self, e):
            object.__setattr__(self, "_e", e)
            object.__setattr__(self, "_hr", [])
            object.__setattr__(self, "_hw", [])

        def drain(self):
            self._hr.clear()
            self._hw.clear()
            return self._e.drain()

        def _emit(self, name, a, args, kw):
            e = self._e
            aps = []
            for i, x in enumerate(args):
                if _is_ap(x):
                    aps.append(("out" if i == 0 else "in", x))
            for k, x in kw.items():
                if _is_ap(x):
                    kind = "out" if (k.startswith("out") or k == "accum_out"
                                     or k == "dest") else "in"
                    aps.append((kind, x))
            writes = [r for r in (_ap_range(x) for k, x in aps if k == "out") if r]
            reads = [r for r in (_ap_range(x) for k, x in aps if k == "in") if r]
            if name == "memset":
                writes, reads = writes + reads, []

            def overlap(r1, r2):
                return (r1[0] == r2[0] and r1[1] < r2[2] and r2[1] < r1[2]
                        and r1[3] < r2[4] and r2[3] < r1[4])

            hr, hw = self._hr, self._hw
            conflict = (any(overlap(w, p) for w in writes for p in hr)
                        or any(overlap(w, p) for w in writes for p in hw)
                        or any(overlap(r, p) for r in reads for p in hw))
            if conflict:
                e.drain()
                _DRAINSTAT[0] += 1
                hr.clear()
                hw.clear()
            _DRAINSTAT[1] += 1
            _DRAINSTAT[2] += len(writes) + len(reads)
            res = a(*args, **kw)
            hr.extend(reads)
            hw.extend(writes)
            return res

        def __getattr__(self, n):
            a = getattr(self._e, n)
            if callable(a) and n not in self._PASS and not n.startswith("_"):
                def w(*args, _n=n, _a=a, **kw):
                    return self._emit(_n, _a, args, kw)
                return w
            return a

    def make_body(eng_name):
        def body(e):
            if eng_name in ("vector", "scalar", "gpsimd"):
                e = AutoDrain(e)
            dma_count = 0
            cc_count = 0
            bar_i = 0
            my_dma_sem = dma_sems.get(eng_name)
            for it in plan.items:
                if bar_i >= bar_limit:
                    break
                if it[0] == "op":
                    _, oeng, fn = it
                    if oeng != eng_name:
                        continue
                    if isinstance(fn, tuple):
                        kind, f = fn
                        if kind == "DMA":
                            dma_count += f(e, my_dma_sem)
                        else:
                            f(e).then_inc(cc_sem, 1)
                            cc_count += 1
                    else:
                        fn(e)
                else:
                    bar_i += 1
                    if my_dma_sem is not None and dma_count > 0:
                        e.wait_ge(my_dma_sem, dma_count * 16)
                    if eng_name == "gpsimd" and cc_count > 0:
                        e.wait_ge(cc_sem, cc_count)
                    e.drain().then_inc(phase_sems[eng_name], 1)
                    for other in ENGINES:
                        if other != eng_name:
                            e.wait_ge(phase_sems[other], bar_i)
        return body

    nbars = sum(1 for it in plan.items if it[0] == "bar")
    print(f"[kernel] plan: {len(plan.items)} items, {nbars} barriers")
    with nc.Block() as block:
        block.sync(make_body("sync"))
        block.tensor(make_body("tensor"))
        block.vector(make_body("vector"))
        block.scalar(make_body("scalar"))
        block.gpsimd(make_body("gpsimd"))
    st.close()
    print("[kernel] proxy: drains", _DRAINSTAT[0], "ops", _DRAINSTAT[1],
          "aps", _DRAINSTAT[2])
    return nc


_cache = {}


def kernel(**inputs):
    cst = host_constants()
    wd = {
        "l0_wq": inputs["L0_attn_wq"],
        "l0_wk": inputs["L0_attn_wk"],
        "l0_wv": inputs["L0_attn_wv"],
        "l0_wo": inputs["L0_attn_wo_w"],
        "l0_wob": inputs["L0_attn_wo_b"],
        "l0_wg": inputs["L0_ffn_w_gate"],
        "l0_wu": inputs["L0_ffn_w_up"],
        "l0_wd": inputs["L0_ffn_w_down"],
        "l1_qkv": inputs["L1_attn_qkv_weights"],
        "l1_bw_t": np.asarray(inputs["L1_ffn_base_w"]).T,
        "l1_bb": inputs["L1_ffn_base_b"],
        "l1_pack": np.concatenate([np.asarray(inputs["L1_ffn_router_w"]).T,
                                   np.asarray(inputs["L1_ffn_lora_A"])], axis=1),
        "l1_rb16": np.concatenate([np.asarray(inputs["L1_ffn_router_b"]),
                                   np.zeros(8, np.float32)]),
        "l1_B": np.asarray(inputs["L1_ffn_lora_B"]).reshape(64, D),
    }
    wd = {k: np.ascontiguousarray(np.asarray(v, dtype=np.float32))
          for k, v in wd.items()}
    x_full = np.ascontiguousarray(np.asarray(inputs["L0_attn_x"], dtype=np.float32))

    if "nc" not in _cache:
        _cache["nc"] = build_nc()
    nc = _cache["nc"]

    in_maps = []
    for b in range(B):
        m = {"x": np.ascontiguousarray(x_full[b])}
        m.update(wd)
        m.update(cst)
        in_maps.append(m)
    res = run_bass_kernel_spmd(nc, in_maps, core_ids=list(range(8)))
    out = np.stack([res.results[b]["y"] for b in range(B)], axis=0)
    return out.astype(np.float32)
